# version 12
# baseline (speedup 1.0000x reference)
import contextlib
import numpy as np
import ml_dtypes
import concourse.bass as bass
import concourse.mybir as mybir
from concourse.alu_op_type import AluOpType as OP
from concourse.bass_utils import run_bass_kernel_spmd

F32 = mybir.dt.float32
BF16 = mybir.dt.bfloat16
F16 = mybir.dt.float16
AF = mybir.ActivationFunctionType
AX = mybir.AxisListType

MAGIC = float(1.5 * 2 ** 23)
NCORES = 8
SEQ = 2048
D = 1024
DH = 4096
NT = SEQ // 128
ND = D // 128
NH = DH // 128
ENGS = ("sync", "tensor", "vector", "scalar", "gpsimd")


class _Op:
    __slots__ = ("eng", "fn", "after", "kind", "key", "sig", "val", "name", "small")

    def __init__(self, eng, fn, after, kind, key, name, small=False):
        flat = []
        for a in after:
            if a is None:
                continue
            if isinstance(a, (list, tuple)):
                flat.extend(x for x in a if x is not None)
            else:
                flat.append(a)
        self.eng, self.fn, self.after, self.kind, self.key, self.name = (
            eng, fn, tuple(flat), kind, key, name)
        self.sig = False
        self.val = None
        self.small = small


class Prog:

    def __init__(self):
        self.seq = {e: [] for e in ENGS}
        self.all = []
        self.last_dma = {}

    def op(self, eng, fn, after=(), name="", small=False):
        o = _Op(eng, fn, after, "c", None, name, small)
        self.seq[eng].append(o)
        self.all.append(o)
        return o

    def dma(self, key, fn, after=(), eng="sync", name=""):
        o = _Op(eng, fn, after, "d", key, name)
        self.seq[eng].append(o)
        self.all.append(o)
        self.last_dma[key] = o
        return o

    def seal(self, name="seal"):
        lasts = [self.seq[e][-1] for e in ENGS if self.seq[e]]
        lasts += list(self.last_dma.values())
        for e in ENGS:
            self.op(e, lambda eng: eng.nop(), after=lasts, name=f"{name}_{e}")

    def barrier(self, eng, after, name="bar"):
        return self.op(eng, lambda e: e.nop(), after, name)

    def emit(self, nc, stack):
        for o in self.all:
            if o.kind == "d":
                o.sig = True
            for d in o.after:
                d.sig = True
        sems = {}
        for e in ENGS:
            if any(o.sig and o.kind == "c" for o in self.seq[e]):
                sems["c:" + e] = stack.enter_context(nc.semaphore(f"se_{e}"))
        dkeys = sorted({o.key for o in self.all if o.kind == "d" and o.sig})
        for k in dkeys:
            sems["d:" + k] = stack.enter_context(nc.semaphore(f"sd_{k}"))
        ccount = {e: 0 for e in ENGS}
        dcount = {}
        for e in ENGS:
            for o in self.seq[e]:
                if not o.sig:
                    continue
                if o.kind == "c":
                    ccount[e] += 1
                    o.val = ("c:" + e, ccount[e], 1)
                else:
                    dcount[o.key] = dcount.get(o.key, 0) + 16
                    o.val = ("d:" + o.key, dcount[o.key], 16)
        for o in self.all:
            for d in o.after:
                assert d.val is not None, f"dep of {o.name} not assigned"

        block = stack.enter_context(nc.Block())
        prog = self

        def mk_body(ename):
            def body(eng):
                waited = {}
                for o in prog.seq[ename]:
                    for d in o.after:
                        if d.eng == ename and d.kind == "c" and not d.small:
                            continue
                        sname, v, _ = d.val
                        if waited.get(sname, 0) >= v:
                            continue
                        eng.wait_ge(sems[sname], v)
                        waited[sname] = v
                    inst = o.fn(eng) if o.fn is not None else None
                    if o.sig:
                        assert inst is not None, f"signaling op {o.name} has no inst"
                        sname, _, inc = o.val
                        inst.then_inc(sems[sname], inc)
            return body

        for e in ENGS:
            if self.seq[e]:
                getattr(block, e)(mk_body(e))
        stack.close()



def host_consts():
    sels = np.zeros((128, 16, 128), dtype=ml_dtypes.bfloat16)
    for j in range(16):
        sels[j, j, :] = 1.0
        sels[64 + j, j, :] = 1.0
    selsf4 = np.zeros((128, 4, 128), dtype=np.float32)
    for j in range(4):
        selsf4[j, j, :] = 1.0
    return {"identbf": np.eye(128, dtype=ml_dtypes.bfloat16),
            "identf": np.eye(128, dtype=np.float32),
            "ones1bf": np.ones((1, 128), dtype=ml_dtypes.bfloat16),
            "ones1f": np.ones((1, 128), dtype=np.float32),
            "sels": sels, "selsf4": selsf4}


CONST_SPECS = [("identbf", [128, 128], BF16), ("identf", [128, 128], F32),
               ("ones1bf", [1, 128], BF16), ("ones1f", [1, 128], F32),
               ("sels", [128, 16, 128], BF16), ("selsf4", [128, 4, 128], F32)]


def declare_consts(nc, names=None):
    return {name: nc.dram_tensor(name, shape, dt, kind="ExternalInput").ap()
            for name, shape, dt in CONST_SPECS if names is None or name in names}


def alloc_consts(nc, stack, names=None):
    return {"c_" + name: stack.enter_context(nc.sbuf_tensor("c_" + name, shape, dt))
            for name, shape, dt in CONST_SPECS if names is None or name in names}


def load_consts(p, cs, sb):
    evs = {}
    for i, (name, _, _) in enumerate(CONST_SPECS):
        if name in cs:
            evs[i] = p.dma("c_" + name, lambda e, n=name: e.dma_start(
                out=sb["c_" + n][:], in_=cs[n]), name=f"ld_{name}")
        else:
            evs[i] = None
    return evs



def emit_xphase(p, sb, psum, x_ap, quant, cident, eps=1e-5):
    xs = sb["xbuf"]
    ptp = psum["ptp"]
    zp = p.op("vector", lambda v: v.memset(sb["xt"][:, :, 0:1], 0.0), name="xtpad0")
    zp2 = p.op("vector", lambda v: v.memset(sb["xt"][:, :, 2049:2050], 0.0), after=[zp])
    slot_free = [None, None]
    prev_tp = None
    prev_evac = zp2
    prev_c1b = None
    evac_list = []
    for t in range(NT):
        sl = t % 2
        C = lambda buf, off, t: sb[buf][:, off + 4 * t:off + 4 * t + 1]
        ld = p.dma(f"xs{sl}", lambda e, t=t, sl=sl: e.dma_start(
            out=xs[:, sl, :], in_=x_ap[128 * t:128 * (t + 1), :]),
            after=[slot_free[sl]], name=f"x_ld{t}")
        oa = p.op("scalar", lambda s, sl=sl, t=t: s.activation(
            sb["scr_sq"][:, 0:1024], xs[:, sl, :], AF.Square,
            accum_out=C("st_ssq", 0, t)), after=[ld, prev_c1b], name=f"xssq{t}")

        def dv(v, t=t, sl=sl):
            r = v.tensor_reduce(C("st_sum", 0, t), xs[:, sl, :], AX.X, OP.add)
            if quant:
                v.tensor_reduce(C("st_max", 0, t), xs[:, sl, :], AX.X, OP.max)
                r = v.tensor_reduce(C("st_min", 0, t), xs[:, sl, :], AX.X, OP.min)
            return r
        od = p.op("vector", dv, after=[ld], name=f"xred{t}", small=True)
        c1 = p.op("gpsimd", lambda g, t=t: g.tensor_scalar(C("st_m", 0, t), C("st_sum", 0, t),
                  1.0 / D, None, OP.mult), after=[od], name=f"xm{t}", small=True)
        c2 = p.op("vector", lambda v, t=t: v.tensor_tensor(C("scr_s", 0, t), C("st_m", 0, t),
                  C("st_m", 0, t), OP.mult), after=[c1], name=f"xm2{t}", small=True)
        c3 = p.op("gpsimd", lambda g, t=t, eps=eps: g.tensor_scalar(C("scr_s", 64, t),
                  C("st_ssq", 0, t), 1.0 / D, eps, OP.mult, OP.add),
                  after=[oa, c1], name=f"xv1{t}", small=True)
        c4 = p.op("vector", lambda v, t=t: v.tensor_tensor(C("scr_s", 64, t), C("scr_s", 64, t),
                  C("scr_s", 0, t), OP.subtract), after=[c2, c3], name=f"xv2{t}", small=True)
        c5 = p.op("scalar", lambda sc, t=t: sc.activation(C("scr_s", 0, t), C("scr_s", 64, t),
                  AF.Ln), after=[c4], name=f"xln{t}", small=True)
        c6 = p.op("scalar", lambda sc, t=t: sc.activation(C("st_rstd", 0, t), C("scr_s", 0, t),
                  AF.Exp, scale=-0.5), after=[c5], name=f"xrstd{t}", small=True)
        if quant:
            c7 = p.op("vector", lambda v, t=t: v.tensor_tensor(C("scr_s", 128, t),
                      C("st_max", 0, t), C("st_m", 0, t), OP.subtract),
                      after=[c1], name=f"xd1{t}", small=True)
            c8 = p.op("gpsimd", lambda g, t=t: g.tensor_tensor(C("scr_s", 192, t),
                      C("st_m", 0, t), C("st_min", 0, t), OP.subtract),
                      after=[c1], name=f"xd2{t}", small=True)
            c9 = p.op("vector", lambda v, t=t: v.tensor_tensor(C("scr_s", 128, t),
                      C("scr_s", 128, t), C("scr_s", 192, t), OP.max),
                      after=[c7, c8], name=f"xd3{t}", small=True)
            ca = p.op("gpsimd", lambda g, t=t: g.tensor_scalar(C("scr_s", 128, t),
                      C("scr_s", 128, t), 1e-5, None, OP.max),
                      after=[c9], name=f"xd4{t}", small=True)
            cb = p.op("scalar", lambda sc, t=t: sc.activation(C("scr_s", 192, t),
                      C("scr_s", 128, t), AF.Ln), after=[ca, c6], name=f"xgl{t}", small=True)
            cc = p.op("scalar", lambda sc, t=t: sc.activation(C("scr_s", 192, t),
                      C("scr_s", 192, t), AF.Exp, scale=-1.0),
                      after=[cb], name=f"xge{t}", small=True)
            cd = p.op("vector", lambda v, t=t: v.tensor_scalar(C("st_g", 0, t),
                      C("scr_s", 192, t), 127.0, None, OP.mult),
                      after=[cc], name=f"xg{t}", small=True)
            ce = p.op("gpsimd", lambda g, t=t: g.tensor_tensor(C("scr_s", 192, t),
                      C("st_rstd", 0, t), C("scr_s", 128, t), OP.mult),
                      after=[c6, ca, cd], name=f"xi1{t}", small=True)
            cf = p.op("gpsimd", lambda g, t=t: g.tensor_scalar(C("st_iso", 0, t),
                      C("scr_s", 192, t), 1.0 / 127.0, None, OP.mult),
                      after=[ce], name=f"xiso{t}", small=True)
            cg = p.op("vector", lambda v, t=t: v.tensor_tensor(C("scr_s", 0, t),
                      C("st_m", 0, t), C("st_g", 0, t), OP.mult),
                      after=[cd], name=f"xb1{t}", small=True)
            ch = p.op("vector", lambda v, t=t: v.tensor_scalar(C("st_bg", 0, t),
                      C("scr_s", 0, t), -1.0, None, OP.mult),
                      after=[cg, cf], name=f"xbg{t}", small=True)
        else:
            cd = p.op("vector", lambda v, t=t: v.tensor_copy(C("st_g", 0, t), C("st_rstd", 0, t)),
                      after=[c6], name=f"xg{t}", small=True)
            cg = p.op("gpsimd", lambda g, t=t: g.tensor_tensor(C("scr_s", 0, t),
                      C("st_m", 0, t), C("st_rstd", 0, t), OP.mult),
                      after=[c6], name=f"xb1{t}", small=True)
            ch = p.op("vector", lambda v, t=t: v.tensor_scalar(C("st_bg", 0, t),
                      C("scr_s", 0, t), -1.0, None, OP.mult),
                      after=[cg, cd], name=f"xbg{t}", small=True)
        a1 = p.op("scalar", lambda sc, t=t, sl=sl: sc.activation(
            sb["scr_sq"][:, 0:1024], xs[:, sl, :], AF.Identity,
            bias=C("st_bg", 0, t), scale=C("st_g", 0, t)),
            after=[ch, ld], name=f"xq_a1_{t}")
        slot_free[sl] = a1
        if quant:
            r1 = p.op("vector", lambda v: v.tensor_scalar(
                sb["scr_sq"][:, 0:1024], sb["scr_sq"][:, 0:1024], MAGIC, MAGIC,
                OP.add, OP.subtract), after=[a1], name=f"xq_rnd{t}")
            c1b = p.op("scalar", lambda sc, t=t: sc.activation(
                sb["xq_bf"][:], sb["scr_sq"][:, 0:1024], AF.Copy, scale=C("st_iso", 0, t)),
                after=[r1, prev_tp], name=f"xq_c1_{t}")
        else:
            c1b = p.op("vector", lambda v: v.tensor_copy(sb["xq_bf"][:], sb["scr_sq"][:, 0:1024]),
                      after=[a1, prev_tp], name=f"xq_c1_{t}")
        prev_c1b = c1b
        tp = p.op("tensor", lambda te, t=t: [te.transpose(
            ptp[:, dc, :], sb["xq_bf"][:, 128 * dc:128 * (dc + 1)], sb["c_identbf"][:])
            for dc in range(ND)][-1], after=[c1b, prev_evac, cident], name=f"xq_tp{t}")
        ev = p.op("vector", lambda v, t=t: v.tensor_copy(
            sb["xt"][:, :, 1 + 128 * t: 1 + 128 * (t + 1)], ptp[:, :, :]),
            after=[tp], name=f"xq_ev{t}")
        prev_tp, prev_evac = tp, ev
        evac_list.append(ev)
    return prev_evac, evac_list


def emit_stats_factors(p, sb, x_ap):
    xs = sb["xbuf"]
    slot_free = [None, None]
    last = None
    for t in range(NT):
        sl = t % 2
        C = lambda buf, off, t: sb[buf][:, off + 4 * t:off + 4 * t + 1]
        ld = p.dma(f"xs{sl}", lambda e, t=t, sl=sl: e.dma_start(
            out=xs[:, sl, :], in_=x_ap[128 * t:128 * (t + 1), :]),
            after=(slot_free[sl] or []), name=f"sf_ld{t}")
        oa = p.op("scalar", lambda s, sl=sl, t=t: s.activation(
            sb["scr_sq"][:, 0:1024], xs[:, sl, :], AF.Square,
            accum_out=C("st_ssq", 0, t)), after=[ld], name=f"sf_ssq{t}")
        od = p.op("vector", lambda v, t=t, sl=sl: v.tensor_reduce(
            C("st_sum", 0, t), xs[:, sl, :], AX.X, OP.add),
            after=[ld], name=f"sf_red{t}", small=True)
        c1 = p.op("gpsimd", lambda g, t=t: g.tensor_scalar(C("st_m", 0, t), C("st_sum", 0, t),
                  1.0 / D, None, OP.mult), after=[od], name=f"sf_m{t}", small=True)
        c2 = p.op("vector", lambda v, t=t: v.tensor_tensor(C("scr_s", 0, t), C("st_m", 0, t),
                  C("st_m", 0, t), OP.mult), after=[c1], name=f"sf_m2{t}", small=True)
        c3 = p.op("gpsimd", lambda g, t=t: g.tensor_scalar(C("scr_s", 64, t),
                  C("st_ssq", 0, t), 1.0 / D, 1e-5, OP.mult, OP.add),
                  after=[oa, c1], name=f"sf_v1{t}", small=True)
        c4 = p.op("vector", lambda v, t=t: v.tensor_tensor(C("scr_s", 64, t), C("scr_s", 64, t),
                  C("scr_s", 0, t), OP.subtract), after=[c2, c3], name=f"sf_v2{t}", small=True)
        c5 = p.op("scalar", lambda sc, t=t: sc.activation(C("scr_s", 0, t), C("scr_s", 64, t),
                  AF.Ln), after=[c4], name=f"sf_ln{t}", small=True)
        c6 = p.op("scalar", lambda sc, t=t: sc.activation(C("st_g", 0, t), C("scr_s", 0, t),
                  AF.Exp, scale=-0.5), after=[c5], name=f"sf_rstd{t}", small=True)
        cg = p.op("gpsimd", lambda g, t=t: g.tensor_tensor(C("scr_s", 0, t),
                  C("st_m", 0, t), C("st_g", 0, t), OP.mult),
                  after=[c6], name=f"sf_b1{t}", small=True)
        ch = p.op("vector", lambda v, t=t: v.tensor_scalar(C("st_bg", 0, t),
                  C("scr_s", 0, t), -1.0, None, OP.mult),
                  after=[cg], name=f"sf_bg{t}", small=True)
        slot_free[sl] = [oa, od]
        last = ch
    return last



def emit_ffn0(nc, p, sbc, cevs, aps, tag, debug=False, shared=None):
    x_ap, y_ap = aps["x"], aps["y"]
    w1T_ap, w2T_ap, cw_ap, cb_ap = aps["w1T"], aps["w2T"], aps["cw"], aps["cb"]
    dbg = {}
    if debug:
        dbg["d_sc1"] = nc.dram_tensor("d_sc1", [1, 64], F32, kind="ExternalOutput").ap()
        dbg["d_g"] = nc.dram_tensor("d_g", [128, 64], F32, kind="ExternalOutput").ap()
        dbg["d_bg"] = nc.dram_tensor("d_bg", [128, 64], F32, kind="ExternalOutput").ap()
        dbg["d_iso"] = nc.dram_tensor("d_iso", [128, 64], F32, kind="ExternalOutput").ap()
        dbg["d_xt"] = nc.dram_tensor("d_xt", [128, ND, 2050], BF16, kind="ExternalOutput").ap()
        dbg["d_w1q"] = nc.dram_tensor("d_w1q", [D, DH], BF16, kind="ExternalOutput").ap()
        dbg["d_w2q"] = nc.dram_tensor("d_w2q", [DH, D], BF16, kind="ExternalOutput").ap()
        dbg["d_sil"] = nc.dram_tensor("d_sil", [128, NH, 512], F16, kind="ExternalOutput").ap()
        dbg["d_sq2"] = nc.dram_tensor("d_sq2", [128, 512], F32, kind="ExternalOutput").ap()
        dbg["d_iso2"] = nc.dram_tensor("d_iso2", [128, 512], F32, kind="ExternalOutput").ap()
        dbg["d_taps"] = nc.dram_tensor("d_taps", [128, 96], F32, kind="ExternalOutput").ap()
    if shared is None:
        w1q_dram = nc.dram_tensor(tag + "w1q_scr", [D, DH], BF16).ap()
        w2q_dram = nc.dram_tensor(tag + "w2q_scr", [DH, D], BF16).ap()
        sc1_dram = nc.dram_tensor(tag + "sc1_scr", [1, 64], F32).ap()
    else:
        w1q_dram = shared["w1q"]
        w2q_dram = shared["w2q"]
        sc1_dram = shared["sc1"]

    stack = contextlib.ExitStack()
    E = stack.enter_context
    sb = dict(sbc)
    sb["xbuf"] = E(nc.sbuf_tensor(tag + "xbuf", [128, 2, 1024], F32))
    sb["scr_a"] = E(nc.sbuf_tensor(tag + "scr_a", [128, 4096], F32))
    sb["wsdump"] = E(nc.sbuf_tensor(tag + "wsdump", [128, 2048], BF16))
    sb["scr_sq"] = E(nc.sbuf_tensor(tag + "scr_sq", [128, 1024], F32))
    sb["scr_s"] = E(nc.sbuf_tensor(tag + "scr_s", [128, 256], F32))
    sb["xq_bf"] = E(nc.sbuf_tensor(tag + "xq_bf", [128, 1024], BF16))
    for n in ("st_sum", "st_ssq", "st_max", "st_min", "st_m", "st_rstd",
              "st_g", "st_bg", "st_iso"):
        sb[n] = E(nc.sbuf_tensor(tag + n, [128, 64], F32))
    sb["xt"] = E(nc.sbuf_tensor(tag + "xt", [128, ND, 2050], BF16))
    sb["w2q"] = E(nc.sbuf_tensor(tag + "w2q", [128, NH, 1024], BF16))
    sb["ring1"] = E(nc.sbuf_tensor(tag + "ring1", [128, 4, ND, 128], BF16))
    sb["sil"] = E(nc.sbuf_tensor(tag + "sil", [128, NH, 512], F16))
    sb["runM"] = E(nc.sbuf_tensor(tag + "runM", [128, 512], F32))
    sb["runL"] = E(nc.sbuf_tensor(tag + "runL", [128, 512], F32))
    sb["convu"] = E(nc.sbuf_tensor(tag + "convu", [128, 2, 512], F32))
    sb["qs_t"] = E(nc.sbuf_tensor(tag + "qs_t", [128, 512], F32))
    sb["taps"] = E(nc.sbuf_tensor(tag + "taps", [128, 96], F32))
    sb["cb_sb"] = E(nc.sbuf_tensor(tag + "cb_sb", [128, 32], F32))
    sb["wabs"] = E(nc.sbuf_tensor(tag + "wabs", [128, 256], F32))
    sb["wqbf"] = E(nc.sbuf_tensor(tag + "wqbf", [128, 4096], BF16))
    sb["sc1"] = E(nc.sbuf_tensor(tag + "sc1", [1, 64], F32))
    sb["sccol"] = E(nc.sbuf_tensor(tag + "sccol", [128, 20], F32))
    sb["sqT"] = E(nc.sbuf_tensor(tag + "sqT", [4, 256], F32))
    sb["sq2rep"] = E(nc.sbuf_tensor(tag + "sq2rep", [128, 512], F32))
    sb["iso2rep"] = E(nc.sbuf_tensor(tag + "iso2rep", [128, 512], F32))
    sb["mt_tok"] = E(nc.sbuf_tensor(tag + "mt_tok", [128, 32], F32))

    psum = {
        "hp": E(nc.psum_tensor(tag + "hp", [128, 2, 1024], F32)),
        "out2": E(nc.psum_tensor(tag + "out2", [128, 1024], F32)),
        "ptp": E(nc.psum_tensor(tag + "ptp", [128, ND, 128], BF16)),
        "pbr": E(nc.psum_tensor(tag + "pbr", [128, 512], F32)),
    }

    pbr_user = None
    ptp_user = None

    prev = None
    slotu = [None, None]
    for i in range(16 if shared is None else 0):
        sl = i % 2
        ld = p.dma(f"wt{sl}", lambda e, i=i, sl=sl: e.dma_start(
            out=sb["scr_a"][:, 2048 * sl:2048 * (sl + 1)],
            in_=w1T_ap[128 * (i // 2):128 * (i // 2 + 1), 2048 * (i % 2):2048 * (i % 2 + 1)]),
            after=[slotu[sl]], name=f"w1s_ld{i}")
        prev = p.op("scalar", lambda a, i=i, sl=sl: a.activation(
            sb["wsdump"][:], sb["scr_a"][:, 2048 * sl:2048 * (sl + 1)], AF.Abs,
            accum_out=sb["wabs"][:, 4 * i:4 * i + 1]), after=[ld], name=f"w1s_abs{i}")
        slotu[sl] = prev
    for i in range(32 if shared is None else 0):
        sl = i % 2
        ld = p.dma(f"wt{sl}", lambda e, i=i, sl=sl: e.dma_start(
            out=sb["scr_a"][:, 2048 * sl:2048 * sl + 1024], in_=w2T_ap[128 * i:128 * (i + 1), :]),
            after=[slotu[sl]], name=f"w2s_ld{i}")
        prev = p.op("scalar", lambda a, i=i, sl=sl: a.activation(
            sb["wsdump"][:, 0:1024], sb["scr_a"][:, 2048 * sl:2048 * sl + 1024], AF.Abs,
            accum_out=sb["wabs"][:, 64 + 4 * i:65 + 4 * i]), after=[ld], name=f"w2s_abs{i}")
        slotu[sl] = prev

    if shared is not None:
        ld_sc1 = p.dma("sc1ld", lambda e: e.dma_start(out=sb["sc1"][:], in_=sc1_dram),
                       name=tag + "ld_sc1")
        w6 = w8 = ld_sc1
        o_mean = ld_sc1
        w1q_done = None
        wq_done = None
    def wtot(v):
        w1v = sb["wabs"][:, 0:64].rearrange("p (t u) -> p t u", u=4)[:, :, 0:1]
        w2v = sb["wabs"][:, 64:192].rearrange("p (t u) -> p t u", u=4)[:, :, 0:1]
        v.tensor_reduce(sb["wabs"][:, 208:209], w2v, AX.XY, OP.add)
        return v.tensor_reduce(sb["wabs"][:, 192:193], w1v, AX.XY, OP.add)
    if shared is None:
        o_tot = p.op("vector", wtot, after=[prev], name="wtot", small=True)

        def wtp(te):
            te.transpose(psum["pbr"][0:1, 0:128], sb["wabs"][:, 192:193], sb["c_identf"][:])
            return te.transpose(psum["pbr"][0:1, 128:256], sb["wabs"][:, 208:209], sb["c_identf"][:])
        o_tp = p.op("tensor", wtp, after=[o_tot, cevs[1], pbr_user], name="wtot_tp")

        w1r = p.op("vector", lambda v: v.tensor_reduce(
            sb["sc1"][0:1, 0:1], psum["pbr"][0:1, 0:128], AX.X, OP.add),
            after=[o_tp], name="wm_r1", small=True)
        w2r = p.op("vector", lambda v: v.tensor_reduce(
            sb["sc1"][0:1, 16:17], psum["pbr"][0:1, 128:256], AX.X, OP.add),
            after=[o_tp], name="wm_r2", small=True)
        w3 = p.op("gpsimd", lambda g: g.tensor_scalar(
            sb["sc1"][0:1, 4:5], sb["sc1"][0:1, 0:1], 1.0 / (D * DH), 1e-5, OP.mult, OP.max),
            after=[w1r], name="wm_mA", small=True)
        w4 = p.op("gpsimd", lambda g: g.tensor_scalar(
            sb["sc1"][0:1, 20:21], sb["sc1"][0:1, 16:17], 1.0 / (D * DH), 1e-5, OP.mult, OP.max),
            after=[w2r], name="wm_mB", small=True)
        w5 = p.op("scalar", lambda sc: sc.activation(sb["sc1"][0:1, 8:9], sb["sc1"][0:1, 4:5], AF.Ln),
                  after=[w3], name="wm_lnA", small=True)
        w6 = p.op("scalar", lambda sc: sc.activation(sb["sc1"][0:1, 12:13], sb["sc1"][0:1, 8:9],
                                                     AF.Exp, scale=-1.0),
                  after=[w5], name="wm_sA", small=True)
        w7 = p.op("scalar", lambda sc: sc.activation(sb["sc1"][0:1, 24:25], sb["sc1"][0:1, 20:21], AF.Ln),
                  after=[w4, w6], name="wm_lnB", small=True)
        w8 = p.op("scalar", lambda sc: sc.activation(sb["sc1"][0:1, 28:29], sb["sc1"][0:1, 24:25],
                                                     AF.Exp, scale=-1.0),
                  after=[w7], name="wm_sB", small=True)
        o_mean = p.barrier("vector", [w6, w8], name="wm_done")
        pbr_user = o_tp
        st_sc1 = p.dma("sc1st", lambda e: e.dma_start(out=sc1_dram, in_=sb["sc1"][:]),
                       after=[w6, w8], name=tag + "st_sc1")

    bco = []
    for j, off in enumerate([12, 28, 4, 20]):
        mm = p.op("tensor", lambda te, off=off: te.matmul(
            psum["pbr"][:, 0:1], sb["c_ones1f"][:, :], sb["sc1"][0:1, off:off + 1],
            start=True, stop=True), after=[w6, w8, pbr_user, cevs[3]], name=f"bc{j}_mm")
        ev = p.op("vector", lambda v, j=j: v.tensor_copy(
            sb["sccol"][:, 4 * j:4 * j + 1], psum["pbr"][:, 0:1]), after=[mm],
            name=f"bc{j}_ev", small=True)
        pbr_user = ev
        bco.append(ev)
    o_half = p.op("gpsimd", lambda g: g.tensor_scalar(
        sb["sccol"][:, 16:17], sb["sccol"][:, 12:13], 0.5, None, OP.mult),
        after=[bco[3]], name="m2half", small=True)

    ld_cw = p.dma("cwld", lambda e: e.dma_start(out=sb["taps"][:], in_=cw_ap), name="ld_cw")
    ld_cb = p.dma("cbld", lambda e: e.dma_start(out=sb["cb_sb"][:], in_=cb_ap), name="ld_cb")
    o_taps = p.op("vector", lambda v: v.tensor_scalar(
        sb["taps"][:], sb["taps"][:], sb["sccol"][:, 8:9], None, OP.mult),
        after=[ld_cw, bco[2]], name="taps")

    prev_ld = prev
    if shared is None:
        w1q_done = None
    slotq = [o_mean, o_mean]
    slotw = [None, None]
    for wi, (n_chunks, src, dst, width, scol) in enumerate(
            [(16, w1T_ap, w1q_dram, 2048, 0), (32, w2T_ap, w2q_dram, 1024, 1)]
            if shared is None else []):
        for i in range(n_chunks):
            sl = i % 2
            if wi == 0:
                srcap = src[128 * (i // 2):128 * (i // 2 + 1), 2048 * (i % 2):2048 * (i % 2 + 1)]
            else:
                srcap = src[128 * i:128 * (i + 1), :]
            ld = p.dma(f"wt{sl}", lambda e, srcap=srcap, width=width, sl=sl: e.dma_start(
                out=sb["scr_a"][:, 2048 * sl:2048 * sl + width], in_=srcap),
                after=[slotq[sl], o_mean], name=f"wq_ld{wi}_{i}")
            a = p.op("scalar", lambda s, width=width, scol=scol, sl=sl: s.activation(
                sb["scr_a"][:, 2048 * sl:2048 * sl + width],
                sb["scr_a"][:, 2048 * sl:2048 * sl + width], AF.Copy,
                scale=sb["sccol"][:, 4 * scol:4 * scol + 1]),
                after=[ld, bco[scol]], name=f"wq_sc{wi}_{i}")
            r = p.op("vector", lambda v, width=width, sl=sl: v.tensor_scalar(
                sb["scr_a"][:, 2048 * sl:2048 * sl + width],
                sb["scr_a"][:, 2048 * sl:2048 * sl + width], MAGIC, MAGIC,
                OP.add, OP.subtract), after=[a], name=f"wq_rnd{wi}_{i}")
            c = p.op("vector", lambda v, width=width, sl=sl: v.tensor_scalar(
                sb["wqbf"][:, 2048 * sl:2048 * sl + width],
                sb["scr_a"][:, 2048 * sl:2048 * sl + width], 1.0, -1.0,
                OP.min, OP.max), after=[r, slotw[sl]], name=f"wq_clip{wi}_{i}")
            slotq[sl] = c
            if wi == 0:
                dstap = dst[128 * (i // 2):128 * (i // 2 + 1), 2048 * (i % 2):2048 * (i % 2 + 1)]
            else:
                dstap = dst[128 * i:128 * (i + 1), :]
            st = p.dma(f"wqs{sl}", lambda e, dstap=dstap, width=width, sl=sl: e.dma_start(
                out=dstap, in_=sb["wqbf"][:, 2048 * sl:2048 * sl + width]),
                after=[c], name=f"wq_st{wi}_{i}")
            slotw[sl] = st
            if wi == 0 and i == n_chunks - 1:
                w1q_done = p.barrier("sync", [slotw[0], slotw[1]], name=tag + "w1qd")
    if shared is None:
        wq_done = p.barrier("sync", [slotw[0], slotw[1]], name=tag + "wqd")

    ld_w2q = None
    for part in range(4):
        ld_w2q = p.dma("w2qld", lambda e, part=part: e.dma_start(
            out=sb["w2q"][:, 8 * part:8 * (part + 1), :],
            in_=w2q_dram.rearrange("(hc p) d -> p hc d", p=128)[:, 8 * part:8 * (part + 1), :]),
            after=[wq_done], name=f"ld_w2q{part}")

    xt_done, xt_evs = emit_xphase(p, sb, psum, x_ap, quant=True, cident=cevs[0])
    ptp_user = xt_done

    y_stores = []
    ring_user = [None] * 4
    hp_user = [None, None]
    conv_user = [None, None]
    sil_user = [None] * NH
    out2_user = None
    o1_prev = None
    prev_ep = [xt_done, xt_done]
    for q in range(4):
        c0 = 512 * q
        def mzf(v):
            v.memset(sb["runM"][:], 0.0)
            return v.memset(sb["runL"][:], 0.0)
        mz = p.op("gpsimd", mzf, after=[o1_prev], name=f"runM_z{q}")
        conv_ops = []
        for hc in range(NH):
            r = hc % 4
            ld = p.dma(f"r1s{r}", lambda e, hc=hc, r=r: e.dma_start(
                out=sb["ring1"][:, r, :, :],
                in_=w1q_dram.rearrange("(dc p) (hc j) -> p dc hc j", p=128, j=128)[:, :, hc, :]),
                after=[ring_user[r], w1q_done], name=f"r1_ld{q}_{hc}")
            b = hc % 2

            def mm1(te, hc=hc, r=r, b=b, c0=c0):
                for dc in range(ND):
                    te.matmul(psum["hp"][:, b, 0:512], sb["ring1"][:, r, dc, :],
                              sb["xt"][:, dc, c0:c0 + 512],
                              start=(dc == 0), stop=(dc == ND - 1))
                for dc in range(ND):
                    mm = te.matmul(psum["hp"][:, b, 512:514], sb["ring1"][:, r, dc, :],
                                   sb["xt"][:, dc, c0 + 512:c0 + 514],
                                   start=(dc == 0), stop=(dc == ND - 1))
                return mm
            mm = p.op("tensor", mm1, after=[ld, xt_evs[min(NT - 1, 4 * q + 4)], hp_user[b]],
                      name=f"mm1_{q}_{hc}")
            ring_user[r] = mm

            cs_ = hc % 2
            cva = p.op("scalar", lambda sc, hc=hc, b=b, cs_=cs_: sc.activation(
                sb["convu"][:, cs_, :], psum["hp"][:, b, 0:512], AF.Copy,
                scale=sb["taps"][:, 3 * hc:3 * hc + 1]),
                after=[mm, o_taps, conv_user[cs_]], name=f"cva{q}_{hc}")

            def conv(v, hc=hc, b=b, cs_=cs_):
                v.scalar_tensor_tensor(sb["convu"][:, cs_, :], psum["hp"][:, b, 1:513],
                                       sb["taps"][:, 3 * hc + 1:3 * hc + 2],
                                       sb["convu"][:, cs_, :], OP.mult, OP.add)
                return v.scalar_tensor_tensor(sb["convu"][:, cs_, :], psum["hp"][:, b, 2:514],
                                              sb["taps"][:, 3 * hc + 2:3 * hc + 3],
                                              sb["convu"][:, cs_, :], OP.mult, OP.add)
            cv = p.op("vector", conv, after=[cva], name=f"conv{q}_{hc}")
            hp_user[b] = cv

            si = p.op("scalar", lambda s, hc=hc: s.activation(
                sb["sil"][:, hc, :], sb["convu"][:, hc % 2, :], AF.Silu,
                bias=sb["cb_sb"][:, hc:hc + 1], scale=1.0),
                after=[cv, ld_cb, sil_user[hc]], name=f"silu{q}_{hc}")
            conv_user[hc % 2] = si

            def rmf(v, hc=hc):
                v.tensor_tensor(sb["runM"][:], sb["sil"][:, hc, :], sb["runM"][:], OP.max)
                return v.tensor_tensor(sb["runL"][:], sb["sil"][:, hc, :], sb["runL"][:], OP.min)
            rm = p.op("vector", rmf, after=[si, mz], name=f"rmax{q}_{hc}")
            conv_ops.append(rm)

        cmb = p.op("vector", lambda v: [v.tensor_scalar(sb["runL"][:], sb["runL"][:], -1.0, None, OP.mult),
                    v.tensor_tensor(sb["runM"][:], sb["runM"][:], sb["runL"][:], OP.max)][-1],
                    after=[conv_ops[-1]], name=f"rcomb{q}")

        def tpm_f(te, q=q):
            for j in range(4):
                mmi = te.transpose(psum["pbr"][:, 128 * j:128 * (j + 1)],
                                   sb["runM"][:, 128 * j:128 * (j + 1)], sb["c_identf"][:])
            return mmi
        o1 = p.op("tensor", tpm_f, after=[cmb, pbr_user], name=f"mtp{q}")
        o1_prev = o1

        mview = sb["mt_tok"][:, 0:16].rearrange("p (t u) -> p t u", u=4)[:, :, 0:1]
        lview = sb["mt_tok"][:, 16:32].rearrange("p (t u) -> p t u", u=4)[:, :, 0:1]
        def mredr(v, q=q):
            for j in range(4):
                r = v.tensor_reduce(sb["mt_tok"][:, 4 * j:4 * j + 1],
                                    psum["pbr"][:, 128 * j:128 * (j + 1)],
                                    AX.X, OP.max, apply_absolute_value=True)
            return r
        o2a = p.op("vector", mredr, after=[o1], name=f"mred{q}", small=True)
        o2b = p.op("gpsimd", lambda g: g.tensor_scalar(mview, mview, 1e-5, None, OP.max),
                   after=[o2a], name=f"mclip{q}", small=True)
        o2c = p.op("scalar", lambda sc: sc.activation(lview, mview, AF.Ln),
                   after=[o2b], name=f"mln{q}", small=True)
        o2d = p.op("scalar", lambda sc: sc.activation(lview, lview, AF.Exp, scale=-1.0),
                   after=[o2c], name=f"mexp{q}", small=True)
        o2e = p.op("vector", lambda v: v.tensor_scalar(lview, lview, 127.0, None, OP.mult),
                   after=[o2d], name=f"msq{q}", small=True)
        o2 = p.op("gpsimd", lambda g: g.tensor_scalar(mview, mview, 1.0 / 127.0, None, OP.mult),
                  after=[o2b, o2c], name=f"miso{q}", small=True)
        pbr_user = o2a

        def mtp2(te):
            te.transpose(psum["pbr"][0:4, 0:128],
                         sb["mt_tok"][:, 0:16].rearrange("p (t u) -> p t u", u=4)[:, :, 0:1].rearrange("p t u -> p (t u)"),
                         sb["c_identf"][:])
            return te.transpose(psum["pbr"][0:4, 128:256],
                                sb["mt_tok"][:, 16:32].rearrange("p (t u) -> p t u", u=4)[:, :, 0:1].rearrange("p t u -> p (t u)"),
                                sb["c_identf"][:])
        o3 = p.op("tensor", mtp2, after=[o2, o2e, pbr_user], name=f"mtp2{q}")
        o4 = p.op("vector", lambda v: v.tensor_copy(sb["sqT"][:, :], psum["pbr"][0:4, 0:256]),
                  after=[o3], name=f"sqT{q}")
        pbr_user = o4
        o5 = p.op("tensor", lambda te: [te.matmul(
            psum["pbr"][:, 128 * j:128 * (j + 1)], sb["c_selsf4"][0:4, j, :],
            sb["sqT"][0:4, 128:256], start=True, stop=True) for j in range(4)][-1],
            after=[o4, pbr_user, cevs[5]], name=f"sqrep_mm{q}")
        o6 = p.op("vector", lambda v: v.tensor_copy(sb["sq2rep"][:], psum["pbr"][:]),
                  after=[o5], name=f"sqrep_ev{q}")
        pbr_user = o6
        o7 = p.op("tensor", lambda te: [te.matmul(
            psum["pbr"][:, 128 * j:128 * (j + 1)], sb["c_selsf4"][0:4, j, :],
            sb["sqT"][0:4, 0:128], start=True, stop=True) for j in range(4)][-1],
            after=[o6, pbr_user], name=f"isorep_mm{q}")
        o8 = p.op("vector", lambda v: v.tensor_copy(sb["iso2rep"][:], psum["pbr"][:]),
                  after=[o7], name=f"isorep_ev{q}")
        pbr_user = o8

        qops = []
        qprev = o8
        for hc in range(NH):
            qa_ = p.op("gpsimd", lambda g, hc=hc: g.tensor_tensor(
                sb["qs_t"][:], sb["sil"][:, hc, :], sb["sq2rep"][:], OP.mult),
                after=[qprev, conv_ops[hc]], name=f"quanta{q}_{hc}")
            qb_ = p.op("vector", lambda v: v.tensor_scalar(
                sb["qs_t"][:], sb["qs_t"][:], MAGIC, MAGIC, OP.add, OP.subtract),
                after=[qa_], name=f"quantb{q}_{hc}")
            qo = p.op("gpsimd", lambda g, hc=hc: g.tensor_tensor(
                sb["sil"][:, hc, :], sb["qs_t"][:], sb["iso2rep"][:], OP.mult),
                after=[qb_], name=f"quant{q}_{hc}")
            qprev = qo
            qops.append(qo)

        m2_last = None
        for tt in range(4):
            gt = 4 * q + tt

            def mm2(te, tt=tt):
                for piece in range(2):
                    for hc in range(NH):
                        mm = te.matmul(psum["out2"][:, 512 * piece:512 * (piece + 1)],
                                       sb["sil"][:, hc, 128 * tt:128 * (tt + 1)],
                                       sb["w2q"][:, hc, 512 * piece:512 * (piece + 1)],
                                       start=(hc == 0), stop=(hc == NH - 1))
                return mm
            m2 = p.op("tensor", mm2, after=[qops[-1], ld_w2q, out2_user], name=f"mm2_{q}_{tt}")
            m2_last = m2
            sl = tt % 2
            lx = p.dma(f"xe{sl}", lambda e, gt=gt, sl=sl: e.dma_start(
                out=sb["xbuf"][:, sl, :], in_=x_ap[128 * gt:128 * (gt + 1), :]),
                after=[prev_ep[0], prev_ep[1]], name=f"xe_ld{gt}")
            ep = p.op("vector", lambda v, sl=sl: v.scalar_tensor_tensor(
                sb["xbuf"][:, 1 - sl, :], psum["out2"][:, :], sb["sccol"][:, 16:17],
                sb["xbuf"][:, sl, :], OP.mult, OP.add),
                after=[m2, lx, o_half, prev_ep[0], prev_ep[1]], name=f"ep{gt}")
            out2_user = ep
            st = p.dma(f"yst{sl}", lambda e, gt=gt, sl=sl: e.dma_start(
                out=y_ap[128 * gt:128 * (gt + 1), :], in_=sb["xbuf"][:, 1 - sl, :]),
                after=[ep], name=f"y_st{gt}")
            prev_ep[sl] = st
            y_stores.append(st)
        for hc in range(NH):
            sil_user[hc] = m2_last

    if debug:
        dstores = []
        dstores.append(p.dma("dbg", lambda e: e.dma_start(out=dbg["d_sc1"], in_=sb["sc1"][:]), after=[o_mean], name="d_sc1"))
        dstores.append(p.dma("dbg", lambda e: e.dma_start(out=dbg["d_g"], in_=sb["st_g"][:]), after=[fac], name="d_g"))
        dstores.append(p.dma("dbg", lambda e: e.dma_start(out=dbg["d_bg"], in_=sb["st_bg"][:]), after=[fac], name="d_bg"))
        dstores.append(p.dma("dbg", lambda e: e.dma_start(out=dbg["d_iso"], in_=sb["st_iso"][:]), after=[fac], name="d_iso"))
        dstores.append(p.dma("dbg", lambda e: e.dma_start(out=dbg["d_xt"], in_=sb["xt"][:]), after=[xt_done], name="d_xt"))
        dstores.append(p.dma("dbg", lambda e: e.dma_start(out=dbg["d_w1q"], in_=w1q_dram), after=[wq_done], name="d_w1q"))
        dstores.append(p.dma("dbg", lambda e: e.dma_start(out=dbg["d_w2q"], in_=w2q_dram), after=[wq_done], name="d_w2q"))
        dstores.append(p.dma("dbg", lambda e: e.dma_start(out=dbg["d_sil"], in_=sb["sil"][:]), after=[y_stores[-1]], name="d_sil"))
        dstores.append(p.dma("dbg", lambda e: e.dma_start(out=dbg["d_sq2"], in_=sb["sq2rep"][:]), after=[y_stores[-1]], name="d_sq2"))
        dstores.append(p.dma("dbg", lambda e: e.dma_start(out=dbg["d_iso2"], in_=sb["iso2rep"][:]), after=[y_stores[-1]], name="d_iso2"))
        dstores.append(p.dma("dbg", lambda e: e.dma_start(out=dbg["d_taps"], in_=sb["taps"][:]), after=[o_taps], name="d_taps"))
        y_stores = y_stores + dstores
    p.barrier("sync", y_stores, name=tag + "final")
    stack.close()
    p.seal(name=tag + "seal")

def build_ffn0(debug=False):
    nc = bass.Bass("TRN2", target_bir_lowering=False)
    aps = {}
    aps["x"] = nc.dram_tensor("x", [SEQ, D], F32, kind="ExternalInput").ap()
    aps["y"] = nc.dram_tensor("y", [SEQ, D], F32, kind="ExternalOutput").ap()
    aps["w1T"] = nc.dram_tensor("w1T", [D, DH], F32, kind="ExternalInput").ap()
    aps["w2T"] = nc.dram_tensor("w2T", [DH, D], F32, kind="ExternalInput").ap()
    aps["cw"] = nc.dram_tensor("cw", [128, 96], F32, kind="ExternalInput").ap()
    aps["cb"] = nc.dram_tensor("cb", [128, 32], F32, kind="ExternalInput").ap()
    cs = declare_consts(nc)
    outer = contextlib.ExitStack()
    sbc = alloc_consts(nc, outer)
    p = Prog()
    cevs = load_consts(p, cs, sbc)
    emit_ffn0(nc, p, sbc, cevs, aps, "f0_", debug=debug)
    p.emit(nc, outer)
    return nc




def emit_attn0(nc, p, sbc, cevs, aps, tag, debug=False):
    x_ap, y_ap = aps["x"], aps["y"]
    wq_ap, wk_ap, wv_ap, lam_ap = aps["wq12"], aps["wk12"], aps["wv"], aps["lam4"]
    wqd = nc.dram_tensor(tag + "wq_scr", [D, 2 * D], BF16).ap()
    wkd = nc.dram_tensor(tag + "wk_scr", [D, 2 * D], BF16).ap()

    SCALE = float(D ** (-0.25))
    stack = contextlib.ExitStack()
    E = stack.enter_context
    sb = dict(sbc)
    sb["xbuf"] = E(nc.sbuf_tensor(tag + "xbuf", [128, 2, 1024], F32))
    sb["ybuf"] = E(nc.sbuf_tensor(tag + "ybuf", [128, 2, 1024], F32))
    sb["scr_sq"] = E(nc.sbuf_tensor(tag + "scr_sq", [128, 1024], F32))
    sb["scr_s"] = E(nc.sbuf_tensor(tag + "scr_s", [128, 256], F32))
    sb["xq_bf"] = E(nc.sbuf_tensor(tag + "xq_bf", [128, 1024], BF16))
    for n in ("st_sum", "st_ssq", "st_max", "st_min", "st_m", "st_rstd",
              "st_g", "st_bg", "st_iso"):
        sb[n] = E(nc.sbuf_tensor(tag + n, [128, 64], F32))
    sb["k12T"] = E(nc.sbuf_tensor(tag + "k12T", [128, 16, 2048], BF16))
    sb["v_sb"] = E(nc.sbuf_tensor(tag + "v_sb", [128, 16, 1024], BF16))
    sb["wv_bf"] = E(nc.sbuf_tensor(tag + "wv_bf", [128, ND, 1024], BF16))
    sb["xtp"] = E(nc.sbuf_tensor(tag + "xtp", [128, ND, 256], BF16))
    sb["wring"] = E(nc.sbuf_tensor(tag + "wring", [128, 4, ND, 128], BF16))
    sb["wcast"] = E(nc.sbuf_tensor(tag + "wcast", [128, 1, 2048], F32))
    sb["wcbf"] = E(nc.sbuf_tensor(tag + "wcbf", [128, 2048], BF16))
    sb["q12T"] = E(nc.sbuf_tensor(tag + "q12T", [128, 16, 256], BF16))
    sb["e1b"] = E(nc.sbuf_tensor(tag + "e1b", [128, 2048], BF16))
    sb["e2h"] = E(nc.sbuf_tensor(tag + "e2h", [128, 2048], BF16))
    sb["dT"] = E(nc.sbuf_tensor(tag + "dT", [128, 16, 256], BF16))
    sb["oT"] = E(nc.sbuf_tensor(tag + "oT", [128, ND, 256], BF16))
    sb["otok"] = E(nc.sbuf_tensor(tag + "otok", [128, 2, 1024], BF16))
    sb["sc1"] = E(nc.sbuf_tensor(tag + "sc1", [1, 64], F32))
    sb["lam4"] = E(nc.sbuf_tensor(tag + "lam4_sb", [128, 4, 8], F32))
    sb["lscr"] = E(nc.sbuf_tensor(tag + "lscr", [128, 64], F32))
    sb["lamcol"] = E(nc.sbuf_tensor(tag + "lamcol", [128, 4], F32))
    sb["stt"] = E(nc.sbuf_tensor(tag + "stt", [128, 128], F32))
    sb["r1sav"] = E(nc.sbuf_tensor(tag + "r1sav", [128, 8], F32))
    sb["socol"] = E(nc.sbuf_tensor(tag + "socol", [128, 8], F32))

    psum = {
        "psA": E(nc.psum_tensor(tag + "psA", [128, 1024], F32)),
        "psB": E(nc.psum_tensor(tag + "psB", [128, 1024], F32)),
        "ptp": E(nc.psum_tensor(tag + "ptp", [128, ND, 128], BF16)),
        "pq": E(nc.psum_tensor(tag + "pq", [128, 2, 512], F32)),
        "pbr": E(nc.psum_tensor(tag + "pbr", [128, 512], F32)),
    }

    U = {"pbr": None, "ptp": None, "psA": None, "psB": None,
         "pq0": None, "pq1": None, "xtp": None, "q12T": None, "dT": None,
         "oT": None, "otok0": None, "otok1": None, "e1b": None, "e2h": None,
         "ring": [None] * 4, "xn_tp": None, "xslot": [None, None]}

    prev_ld = None
    cast_done = {}
    prev_cv = None
    for name, src, dst, cols in [("wq", wq_ap, wqd, 2048), ("wk", wk_ap, wkd, 2048)]:
        for i in range(ND):
            sl = 0
            ld = p.dma(f"wc{sl}", lambda e, i=i, sl=sl, src=src, cols=cols: e.dma_start(
                out=sb["wcast"][:, sl, 0:cols], in_=src[128 * i:128 * (i + 1), :]),
                after=[prev_ld, prev_cv], name=f"wcl_{name}{i}")
            cv = p.op("vector", lambda v, sl=sl, cols=cols: v.tensor_copy(
                sb["wcbf"][:, 0:cols], sb["wcast"][:, sl, 0:cols]),
                after=[ld], name=f"wcc_{name}{i}")
            prev_cv = cv
            std = p.dma("wcs", lambda e, i=i, dst=dst, cols=cols: e.dma_start(
                out=dst[128 * i:128 * (i + 1), :], in_=sb["wcbf"][:, 0:cols]),
                after=[cv], name=f"wcst_{name}{i}")
            prev_ld = std
            cast_done[name] = std
    for i in range(ND):
        sl = 0
        ld = p.dma(f"wc{sl}", lambda e, i=i, sl=sl: e.dma_start(
            out=sb["wcast"][:, sl, 0:1024], in_=wv_ap[128 * i:128 * (i + 1), :]),
            after=[prev_ld, prev_cv], name=f"wcl_wv{i}")
        cv = p.op("vector", lambda v, i=i, sl=sl: v.tensor_copy(
            sb["wv_bf"][:, i, :], sb["wcast"][:, sl, 0:1024]),
            after=[ld], name=f"wcc_wv{i}")
        prev_ld = ld
        prev_cv = cv
        cast_done["wv"] = cv

    ldl = p.dma("wld", lambda e: e.dma_start(out=sb["lam4"][:], in_=lam_ap), name="ld_lam")
    l1 = p.op("vector", lambda v: v.tensor_tensor(
        sb["lscr"][:, 0:8], sb["lam4"][:, 0, :], sb["lam4"][:, 1, :], OP.mult),
        after=[ldl], name="lam_m1", small=True)
    l2 = p.op("gpsimd", lambda g: g.tensor_tensor(
        sb["lscr"][:, 8:16], sb["lam4"][:, 2, :], sb["lam4"][:, 3, :], OP.mult),
        after=[ldl], name="lam_m2", small=True)
    l3 = p.op("vector", lambda v: v.tensor_reduce(
        sb["lscr"][:, 16:17], sb["lscr"][:, 0:8], AX.X, OP.add),
        after=[l1], name="lam_r1", small=True)
    l4 = p.op("vector", lambda v: v.tensor_reduce(
        sb["lscr"][:, 32:33], sb["lscr"][:, 8:16], AX.X, OP.add),
        after=[l2, l3], name="lam_r2", small=True)
    l5 = p.op("tensor", lambda te: [te.transpose(
        psum["pbr"][0:1, 0:128], sb["lscr"][:, 16:17], sb["c_identf"][:]),
        te.transpose(psum["pbr"][0:1, 128:256], sb["lscr"][:, 32:33], sb["c_identf"][:])][-1],
        after=[l4, cevs[1], U["pbr"]], name="lam_tp")
    l6 = p.op("vector", lambda v: v.tensor_reduce(
        sb["sc1"][0:1, 0:1], psum["pbr"][0:1, 0:128], AX.X, OP.add),
        after=[l5], name="lam_d1", small=True)
    l7 = p.op("vector", lambda v: v.tensor_reduce(
        sb["sc1"][0:1, 4:5], psum["pbr"][0:1, 128:256], AX.X, OP.add),
        after=[l5, l6], name="lam_d2", small=True)
    U["pbr"] = l7
    l8 = p.op("scalar", lambda sc: sc.activation(sb["sc1"][0:1, 8:9], sb["sc1"][0:1, 0:1], AF.Exp),
              after=[l6], name="lam_e1", small=True)
    l9 = p.op("scalar", lambda sc: sc.activation(sb["sc1"][0:1, 12:13], sb["sc1"][0:1, 4:5], AF.Exp),
              after=[l7, l8], name="lam_e2", small=True)
    la_ = p.op("vector", lambda v: v.tensor_tensor(
        sb["sc1"][0:1, 16:17], sb["sc1"][0:1, 8:9], sb["sc1"][0:1, 12:13], OP.subtract),
        after=[l8, l9], name="lam_sub", small=True)
    lb_ = p.op("gpsimd", lambda g: g.tensor_scalar(
        sb["sc1"][0:1, 20:21], sb["sc1"][0:1, 16:17], 0.8, None, OP.add),
        after=[la_], name="lam_add", small=True)
    lc_ = p.op("tensor", lambda te: te.matmul(
        psum["pbr"][:, 0:1], sb["c_ones1f"][:, :], sb["sc1"][0:1, 20:21],
        start=True, stop=True), after=[lb_, U["pbr"], cevs[3]], name="lam_bc")
    lamc = p.op("vector", lambda v: v.tensor_copy(sb["lamcol"][:, 0:1], psum["pbr"][:, 0:1]),
                after=[lc_], name="lam_ev", small=True)
    U["pbr"] = lamc

    fac = emit_stats_factors(p, sb, x_ap)
    U["xslot"] = [fac, fac]

    def build_xn(tiles):
        ev = None
        for j, t in enumerate(tiles):
            sl = t % 2
            ld = p.dma(f"xn{sl}", lambda e, t=t, sl=sl: e.dma_start(
                out=sb["xbuf"][:, sl, :], in_=x_ap[128 * t:128 * (t + 1), :]),
                after=[U["xslot"][sl], fac], name=f"xn_ld{t}")
            a1 = p.op("scalar", lambda s, t=t, sl=sl: s.activation(
                sb["xq_bf"][:], sb["xbuf"][:, sl, :], AF.Identity,
                bias=sb["st_bg"][:, 4 * t:4 * t + 1], scale=sb["st_g"][:, 4 * t:4 * t + 1]),
                after=[ld, U["xn_tp"]], name=f"xn_a{t}")
            U["xslot"][sl] = a1
            tp = p.op("tensor", lambda te: [te.transpose(
                psum["ptp"][:, dc, :], sb["xq_bf"][:, 128 * dc:128 * (dc + 1)],
                sb["c_identbf"][:]) for dc in range(ND)][-1],
                after=[a1, U["ptp"], cevs[0], U["xtp"]], name=f"xn_tp{t}")
            ev = p.op("vector", lambda v, j=j: v.tensor_copy(
                sb["xtp"][:, :, 128 * j:128 * (j + 1)], psum["ptp"][:, :, :]),
                after=[tp], name=f"xn_ev{t}")
            U["xn_tp"] = tp
            U["ptp"] = ev
        return ev

    last_kv_ev = None
    for piece in range(8):
        t0 = 2 * piece
        xe = build_xn([t0, t0 + 1])
        for ec in range(16):
            r = ec % 4
            ld = p.dma(f"kr{r}", lambda e, ec=ec, r=r: e.dma_start(
                out=sb["wring"][:, r, :, :],
                in_=wkd.rearrange("(dc p) (ec j) -> p dc ec j", p=128, j=128)[:, :, ec, :]),
                after=[U["ring"][r], cast_done["wk"]], name=f"krl_{piece}_{ec}")
            bk = "psA" if ec % 2 == 0 else "psB"

            def kmm(te, r=r, bk=bk):
                for dc in range(ND):
                    mm = te.matmul(psum[bk][:, 0:256], sb["wring"][:, r, dc, :],
                                   sb["xtp"][:, dc, :], start=(dc == 0), stop=(dc == ND - 1))
                return mm
            mm = p.op("tensor", kmm, after=[ld, xe, U[bk]], name=f"kmm{piece}_{ec}")
            U["ring"][r] = mm
            U["xtp"] = mm
            ev = p.op("vector", lambda v, ec=ec, piece=piece, bk=bk: v.tensor_copy(
                sb["k12T"][:, ec, 256 * piece:256 * (piece + 1)], psum[bk][:, 0:256]),
                after=[mm], name=f"kev{piece}_{ec}")
            U[bk] = ev
            last_kv_ev = ev
        for j, t in enumerate((t0, t0 + 1)):
            for fp in range(2):
                pqs = "pq0" if fp == 0 else "pq1"

                def vmm(te, j=j, fp=fp):
                    for dc in range(ND):
                        mm = te.matmul(psum["pq"][:, fp, :],
                                       sb["xtp"][:, dc, 128 * j:128 * (j + 1)],
                                       sb["wv_bf"][:, dc, 512 * fp:512 * (fp + 1)],
                                       start=(dc == 0), stop=(dc == ND - 1))
                    return mm
                mm = p.op("tensor", vmm, after=[xe, cast_done["wv"], U[pqs]], name=f"vmm{t}_{fp}")
                U["xtp"] = mm
                ev = p.op("vector", lambda v, t=t, fp=fp: v.tensor_copy(
                    sb["v_sb"][:, t, 512 * fp:512 * (fp + 1)], psum["pq"][:, fp, :]),
                    after=[mm], name=f"vev{t}_{fp}")
                U[pqs] = ev
                last_kv_ev = ev

    y_stores = []
    prev_ep = [None, None]
    for B in range(8):
        xe = build_xn([2 * B, 2 * B + 1])
        q_last = None
        for ec in range(16):
            r = ec % 4
            ld = p.dma(f"kr{r}", lambda e, ec=ec, r=r: e.dma_start(
                out=sb["wring"][:, r, :, :],
                in_=wqd.rearrange("(dc p) (ec j) -> p dc ec j", p=128, j=128)[:, :, ec, :]),
                after=[U["ring"][r], cast_done["wq"]], name=f"qrl_{B}_{ec}")
            pqs = "pq0" if ec % 2 == 0 else "pq1"

            def qmm(te, r=r, fp=0 if True else 0, pqs=pqs):
                fp = 0 if pqs == "pq0" else 1
                for dc in range(ND):
                    mm = te.matmul(psum["pq"][:, fp, 0:256], sb["wring"][:, r, dc, :],
                                   sb["xtp"][:, dc, :], start=(dc == 0), stop=(dc == ND - 1))
                return mm
            mm = p.op("tensor", qmm, after=[ld, xe, U[pqs], last_kv_ev], name=f"qmm{B}_{ec}")
            U["ring"][r] = mm
            U["xtp"] = mm
            ev = p.op("vector", lambda v, ec=ec, pqs=pqs: v.tensor_copy(
                sb["q12T"][:, ec, :], psum["pq"][0:128, 0 if pqs == "pq0" else 1, 0:256]),
                after=[mm, U["q12T"]], name=f"qev{B}_{ec}")
            U[pqs] = ev
            q_last = ev
        dT_evs = []
        for tt in range(2):
            qsl = 128 * tt
            gt = 2 * B + tt
            s_ops = {}
            for which, ecbase in ((1, 0), (2, 8)):
                for half, ps in ((0, "psA"), (1, "psB")):
                    def smm2(te, qsl=qsl, ecbase=ecbase, half=half, ps=ps):
                        for pc in range(2):
                            for i, ec in enumerate(range(ecbase, ecbase + 8)):
                                mm = te.matmul(
                                    psum[ps][:, 512 * pc:512 * (pc + 1)],
                                    sb["q12T"][:, ec, qsl:qsl + 128],
                                    sb["k12T"][:, ec, 1024 * half + 512 * pc:1024 * half + 512 * (pc + 1)],
                                    start=(i == 0), stop=(i == 7))
                        return mm
                    dep = [q_last, U[ps]] if which == 1 else [s_ops[(1, half, "exp")], U[ps]]
                    mm = p.op("tensor", smm2, after=dep, name=f"smm{B}_{tt}_{which}_{half}")
                    st_off = (0 if which == 1 else 24) + 4 * half
                    rd = p.op("vector", lambda v, ps=ps, st_off=st_off: v.tensor_reduce(
                        sb["stt"][:, st_off:st_off + 1], psum[ps][:, :], AX.X, OP.max,
                        negate=True), after=[mm], name=f"smax{B}_{tt}_{which}_{half}", small=True)
                    s_ops[(which, half, "mm")] = mm
                    s_ops[(which, half, "rd")] = rd
                base = 0 if which == 1 else 24
                cb = p.op("vector", lambda g, base=base: g.tensor_tensor(
                    sb["stt"][:, base + 8:base + 9], sb["stt"][:, base:base + 1],
                    sb["stt"][:, base + 4:base + 5], OP.min),
                    after=[s_ops[(which, 0, "rd")], s_ops[(which, 1, "rd")]],
                    name=f"scmb{B}_{tt}_{which}", small=True)
                sc_ = p.op("vector", lambda v, base=base: v.tensor_scalar(
                    sb["stt"][:, base + 12:base + 13], sb["stt"][:, base + 8:base + 9],
                    SCALE, None, OP.mult), after=[cb], name=f"sbias{B}_{tt}_{which}", small=True)
                ebuf = "e1b" if which == 1 else "e2h"
                for half, ps in ((0, "psA"), (1, "psB")):
                    ex = p.op("scalar", lambda s, ps=ps, half=half, ebuf=ebuf, base=base: s.activation(
                        sb[ebuf][:, 1024 * half:1024 * (half + 1)], psum[ps][:, :], AF.Exp,
                        bias=sb["stt"][:, base + 12:base + 13], scale=SCALE,
                        accum_out=sb["stt"][:, base + 16 + 4 * half:base + 17 + 4 * half]),
                        after=[sc_, U[ebuf]], name=f"sexp{B}_{tt}_{which}_{half}")
                    s_ops[(which, half, "exp")] = ex
                    U[ps] = ex
                s_ops[(which, 1, "expboth")] = s_ops[(which, 1, "exp")]
                U[ebuf] = None
            r1c = p.op("vector", lambda v: v.tensor_tensor(
                sb["stt"][:, 48:49], sb["stt"][:, 16:17], sb["stt"][:, 20:21], OP.add),
                after=[s_ops[(1, 0, "exp")], s_ops[(1, 1, "exp")]], name=f"r1{B}_{tt}", small=True)
            r2c = p.op("gpsimd", lambda g: g.tensor_tensor(
                sb["stt"][:, 52:53], sb["stt"][:, 40:41], sb["stt"][:, 44:45], OP.add),
                after=[s_ops[(2, 0, "exp")], s_ops[(2, 1, "exp")]], name=f"r2{B}_{tt}", small=True)
            rv = p.op("scalar", lambda sc: sc.activation(
                sb["stt"][:, 56:57], sb["stt"][:, 52:53], AF.Ln),
                after=[r2c], name=f"rln{B}_{tt}", small=True)
            rv2 = p.op("scalar", lambda sc: sc.activation(
                sb["stt"][:, 60:61], sb["stt"][:, 56:57], AF.Exp, scale=-1.0),
                after=[rv], name=f"rexp{B}_{tt}", small=True)
            sv1 = p.op("vector", lambda v: v.tensor_tensor(
                sb["stt"][:, 64:65], sb["stt"][:, 48:49], sb["stt"][:, 60:61], OP.mult),
                after=[r1c, rv2], name=f"sv1{B}_{tt}", small=True)
            sv2 = p.op("gpsimd", lambda g: g.tensor_tensor(
                sb["stt"][:, 68:69], sb["stt"][:, 64:65], sb["lamcol"][:, 0:1], OP.mult),
                after=[sv1, lamc], name=f"sv2{B}_{tt}", small=True)
            negs = p.op("vector", lambda v: v.tensor_scalar(
                sb["stt"][:, 72:73], sb["stt"][:, 68:69], -1.0, None, OP.mult),
                after=[sv2], name=f"negs{B}_{tt}", small=True)
            r1s = p.op("gpsimd", lambda g, tt=tt: g.tensor_copy(
                sb["r1sav"][:, 4 * tt:4 * tt + 1], sb["stt"][:, 48:49]),
                after=[r1c], name=f"r1s{B}_{tt}", small=True)
            dcmb = p.op("vector", lambda v: v.scalar_tensor_tensor(
                sb["e1b"][:, :], sb["e2h"][:, :], sb["stt"][:, 72:73], sb["e1b"][:, :],
                OP.mult, OP.add), after=[negs, s_ops[(1, 1, "exp")], s_ops[(2, 1, "exp")]],
                name=f"dcmb{B}_{tt}")
            U["e1b"] = dcmb
            U["e2h"] = dcmb
            tp_prev = dcmb
            for half in range(2):
                tp = p.op("tensor", lambda te, half=half: [te.transpose(
                    psum["ptp"][:, kk, :],
                    sb["e1b"][:, 1024 * half + 128 * kk:1024 * half + 128 * (kk + 1)],
                    sb["c_identbf"][:]) for kk in range(8)][-1],
                    after=[tp_prev, U["ptp"]], name=f"dtp{B}_{tt}_{half}")
                ev = p.op("vector", lambda v, half=half, qsl=qsl: v.tensor_copy(
                    sb["dT"][:, 8 * half:8 * (half + 1), qsl:qsl + 128], psum["ptp"][:, :, :]),
                    after=[tp, U["dT"]], name=f"dtev{B}_{tt}_{half}")
                U["ptp"] = ev
                tp_prev = ev
                dT_evs.append(ev)
            U["e1b"] = dT_evs[-1]
            U["e2h"] = dcmb
            U["q12T"] = s_ops[(2, 1, "mm")]
        U["dT"] = None
        pv_evs = []
        for dc in range(ND):
            pqs = "pq0" if dc % 2 == 0 else "pq1"
            fp = 0 if dc % 2 == 0 else 1

            def pvmm(te, dc=dc, fp=fp):
                for kt in range(16):
                    mm = te.matmul(psum["pq"][:, fp, 0:256],
                                   sb["v_sb"][:, kt, 128 * dc:128 * (dc + 1)],
                                   sb["dT"][:, kt, :], start=(kt == 0), stop=(kt == 15))
                return mm
            mm = p.op("tensor", pvmm, after=dT_evs + [U[pqs], last_kv_ev], name=f"pv{B}_{dc}")
            ev = p.op("vector", lambda v, dc=dc, fp=fp: v.tensor_copy(
                sb["oT"][:, dc, :], psum["pq"][:, fp, 0:256]),
                after=[mm, U["oT"]], name=f"pvev{B}_{dc}")
            U[pqs] = ev
            pv_evs.append(ev)
        U["dT"] = pv_evs[-1]
        for tt in range(2):
            gt = 2 * B + tt
            tp = p.op("tensor", lambda te, tt=tt: [te.transpose(
                psum["ptp"][:, dc, :], sb["oT"][:, dc, 128 * tt:128 * (tt + 1)],
                sb["c_identbf"][:]) for dc in range(ND)][-1],
                after=pv_evs + [U["ptp"]], name=f"otp{B}_{tt}")
            oev = p.op("vector", lambda v, tt=tt: v.tensor_copy(
                sb["otok"][:, tt, :], psum["ptp"][:, :, :].rearrange("p a b -> p (a b)")),
                after=[tp, U["otok%d" % tt]], name=f"oev{B}_{tt}")
            U["ptp"] = oev
            U["oT"] = tp
            sq = p.op("scalar", lambda s, tt=tt: s.activation(
                sb["scr_sq"][:], sb["otok"][:, tt, :], AF.Square,
                accum_out=sb["stt"][:, 76 + 16 * tt:77 + 16 * tt]),
                after=[oev], name=f"osq{B}_{tt}")
            c1 = p.op("vector", lambda v, tt=tt: v.tensor_scalar(
                sb["stt"][:, 80 + 16 * tt:81 + 16 * tt], sb["stt"][:, 76 + 16 * tt:77 + 16 * tt],
                1.0 / D, None, OP.mult), after=[sq], name=f"oc1{B}_{tt}", small=True)
            c2 = p.op("gpsimd", lambda g, tt=tt: g.tensor_tensor(
                sb["stt"][:, 84 + 16 * tt:85 + 16 * tt], sb["r1sav"][:, 4 * tt:4 * tt + 1],
                sb["r1sav"][:, 4 * tt:4 * tt + 1], OP.mult),
                after=[r1s if tt == 1 else r1s, c1], name=f"oc2{B}_{tt}", small=True)
            c3 = p.op("vector", lambda v, tt=tt: v.scalar_tensor_tensor(
                sb["stt"][:, 88 + 16 * tt:89 + 16 * tt], sb["stt"][:, 84 + 16 * tt:85 + 16 * tt],
                1e-5, sb["stt"][:, 80 + 16 * tt:81 + 16 * tt], OP.mult, OP.add),
                after=[c2], name=f"oc3{B}_{tt}", small=True)
            c4 = p.op("scalar", lambda sc, tt=tt: sc.activation(
                sb["stt"][:, 92 + 16 * tt:93 + 16 * tt], sb["stt"][:, 88 + 16 * tt:89 + 16 * tt],
                AF.Ln), after=[c3], name=f"oc4{B}_{tt}", small=True)
            c5 = p.op("scalar", lambda sc, tt=tt: sc.activation(
                sb["stt"][:, 96 + 16 * tt:97 + 16 * tt], sb["stt"][:, 92 + 16 * tt:93 + 16 * tt],
                AF.Exp, scale=-0.5), after=[c4], name=f"oc5{B}_{tt}", small=True)
            c6 = p.op("vector", lambda v, tt=tt: v.tensor_scalar(
                sb["socol"][:, 4 * tt:4 * tt + 1], sb["stt"][:, 96 + 16 * tt:97 + 16 * tt],
                0.2, None, OP.mult), after=[c5], name=f"oc6{B}_{tt}", small=True)
            sl = gt % 2
            lx = p.dma(f"xe{sl}", lambda e, gt=gt, sl=sl: e.dma_start(
                out=sb["xbuf"][:, sl, :], in_=x_ap[128 * gt:128 * (gt + 1), :]),
                after=[prev_ep[sl], U["xslot"][sl]], name=f"xe_ld{gt}")
            ep = p.op("vector", lambda v, sl=sl, tt=tt: v.scalar_tensor_tensor(
                sb["ybuf"][:, sl, :], sb["otok"][:, tt, :], sb["socol"][:, 4 * tt:4 * tt + 1],
                sb["xbuf"][:, sl, :], OP.mult, OP.add),
                after=[c6, lx, oev, prev_ep[sl]], name=f"aep{gt}")
            U["otok%d" % tt] = ep
            U["xslot"][sl] = ep
            st = p.dma(f"yst{sl}", lambda e, gt=gt, sl=sl: e.dma_start(
                out=y_ap[128 * gt:128 * (gt + 1), :], in_=sb["ybuf"][:, sl, :]),
                after=[ep], name=f"y_st{gt}")
            prev_ep[sl] = st
            y_stores.append(st)

    p.barrier("sync", y_stores, name=tag + "final")
    stack.close()
    p.seal(name=tag + "seal")

def build_attn0(debug=False):
    nc = bass.Bass("TRN2", target_bir_lowering=False)
    aps = {}
    aps["x"] = nc.dram_tensor("x", [SEQ, D], F32, kind="ExternalInput").ap()
    aps["y"] = nc.dram_tensor("y", [SEQ, D], F32, kind="ExternalOutput").ap()
    aps["wq12"] = nc.dram_tensor("wq12", [D, 2 * D], F32, kind="ExternalInput").ap()
    aps["wk12"] = nc.dram_tensor("wk12", [D, 2 * D], F32, kind="ExternalInput").ap()
    aps["wv"] = nc.dram_tensor("wv", [D, D], F32, kind="ExternalInput").ap()
    aps["lam4"] = nc.dram_tensor("lam4", [128, 4, 8], F32, kind="ExternalInput").ap()
    cs = declare_consts(nc)
    outer = contextlib.ExitStack()
    sbc = alloc_consts(nc, outer)
    p = Prog()
    cevs = load_consts(p, cs, sbc)
    emit_attn0(nc, p, sbc, cevs, aps, "a0_", debug=debug)
    p.emit(nc, outer)
    return nc



def run_attn0(xs, inputs, consts):
    nc = build_attn0()
    lam4 = np.stack([np.asarray(inputs[k], np.float32).reshape(128, 8) for k in
                     ("L0_attn_lambda_q1", "L0_attn_lambda_k1",
                      "L0_attn_lambda_q2", "L0_attn_lambda_k2")], axis=1)
    w = {"wq12": np.ascontiguousarray(np.asarray(inputs["L0_attn_w_q12"], np.float32)),
         "wk12": np.ascontiguousarray(np.asarray(inputs["L0_attn_w_k12"], np.float32)),
         "wv": np.ascontiguousarray(np.asarray(inputs["L0_attn_w_v"], np.float32)),
         "lam4": np.ascontiguousarray(lam4)}
    in_maps = []
    for c in range(NCORES):
        m = {"x": xs[c]}
        m.update(w)
        m.update(consts)
        in_maps.append(m)
    res = run_bass_kernel_spmd(nc, in_maps, core_ids=list(range(NCORES)))
    return [res.results[c]["y"] for c in range(NCORES)]



def emit_ffn1(nc, p, sbc, cevs, aps, tag, debug=False, shared=None):
    x_ap, y_ap = aps["x"], aps["y"]
    k_ap, bias_ap, a_ap, b_ap = aps["kern"], aps["bias"], aps["la"], aps["lb"]
    dbg = {}
    if debug:
        dbg["d_kq"] = nc.dram_tensor("d_kq", [128, ND, 1024], BF16, kind="ExternalOutput").ap()
        dbg["d_sc1"] = nc.dram_tensor("d_sc1", [1, 64], F32, kind="ExternalOutput").ap()
        dbg["d_xt"] = nc.dram_tensor("d_xt", [128, ND, 2050], BF16, kind="ExternalOutput").ap()
        dbg["d_g"] = nc.dram_tensor("d_g", [128, 64], F32, kind="ExternalOutput").ap()
        dbg["d_bg"] = nc.dram_tensor("d_bg", [128, 64], F32, kind="ExternalOutput").ap()

    stack = contextlib.ExitStack()
    E = stack.enter_context
    sb = dict(sbc)
    sb["xbuf"] = E(nc.sbuf_tensor(tag + "xbuf", [128, 2, 1024], F32))
    sb["ybuf"] = E(nc.sbuf_tensor(tag + "ybuf", [128, 2, 1024], F32))
    sb["scr_sq"] = E(nc.sbuf_tensor(tag + "scr_sq", [128, 1024], F32))
    sb["kscr"] = E(nc.sbuf_tensor(tag + "kscr", [128, 1024], F32))
    sb["scr_s"] = E(nc.sbuf_tensor(tag + "scr_s", [128, 256], F32))
    sb["xq_bf"] = E(nc.sbuf_tensor(tag + "xq_bf", [128, 1024], BF16))
    for n in ("st_sum", "st_ssq", "st_max", "st_min", "st_m", "st_rstd",
              "st_g", "st_bg", "st_iso"):
        sb[n] = E(nc.sbuf_tensor(tag + n, [128, 64], F32))
    sb["xt"] = E(nc.sbuf_tensor(tag + "xt", [128, ND, 2050], BF16))
    sb["kf"] = E(nc.sbuf_tensor(tag + "kf", [128, ND, 1024], F32))
    sb["kq"] = E(nc.sbuf_tensor(tag + "kq", [128, ND, 1024], BF16))
    sb["a_bf"] = E(nc.sbuf_tensor(tag + "a_bf", [128, ND, 16], BF16))
    sb["a_f"] = E(nc.sbuf_tensor(tag + "a_f", [128, ND, 16], F32))
    sb["b_bf"] = E(nc.sbuf_tensor(tag + "b_bf", [16, 1024], BF16))
    sb["b_f"] = E(nc.sbuf_tensor(tag + "b_f", [16, 1024], F32))
    sb["bias_bf"] = E(nc.sbuf_tensor(tag + "bias_bf", [1, 1024], BF16))
    sb["bias_f"] = E(nc.sbuf_tensor(tag + "bias_f", [1, 1024], F32))
    sb["xa_bf"] = E(nc.sbuf_tensor(tag + "xa_bf", [128, 16], BF16))
    sb["xaT"] = E(nc.sbuf_tensor(tag + "xaT", [16, 128], BF16))
    sb["wabs"] = E(nc.sbuf_tensor(tag + "wabs", [128, 64], F32))
    sb["sc1"] = E(nc.sbuf_tensor(tag + "sc1", [1, 64], F32))
    sb["sccol"] = E(nc.sbuf_tensor(tag + "sccol", [128, 8], F32))

    psum = {
        "out2": E(nc.psum_tensor(tag + "out2", [128, 2, 1024], F32)),
        "ptp": E(nc.psum_tensor(tag + "ptp", [128, ND, 128], BF16)),
        "pbr": E(nc.psum_tensor(tag + "pbr", [128, 512], F32)),
    }

    pbr_user = None

    lda = p.dma("lda", lambda e: e.dma_start(
        out=sb["a_f"][:], in_=a_ap.rearrange("(dc p) r -> p dc r", p=128)), name="ld_a")
    ca = p.op("vector", lambda v: v.tensor_copy(sb["a_bf"][:], sb["a_f"][:]), after=[lda], name="cast_a")
    ldb = p.dma("ldb", lambda e: e.dma_start(out=sb["b_f"][:], in_=b_ap), name="ld_b")
    cb_ = p.op("vector", lambda v: v.tensor_copy(sb["b_bf"][:], sb["b_f"][:]), after=[ldb], name="cast_b")
    ldbias = p.dma("ldbias", lambda e: e.dma_start(out=sb["bias_f"][:], in_=bias_ap), name="ld_bias")
    cbias = p.op("vector", lambda v: v.tensor_copy(sb["bias_bf"][:], sb["bias_f"][:]),
                 after=[ldbias], name="cast_bias")

    if shared is None:
        kq_dram = nc.dram_tensor(tag + "kq_scr", [128, ND * 1024], BF16).ap()
        ldk = p.dma("kld", lambda e: e.dma_start(
            out=sb["kf"][:], in_=k_ap.rearrange("(dc p) f -> p dc f", p=128)), name="ld_k")
        kst = p.op("vector", lambda v: [v.tensor_reduce(
            sb["wabs"][:, 4 * dc:4 * dc + 1], sb["kf"][:, dc, :], AX.X, OP.max,
            apply_absolute_value=True) for dc in range(ND)][-1], after=[ldk], name="kst", small=True)
        wv = sb["wabs"][:, 0:32].rearrange("p (t u) -> p t u", u=4)[:, :, 0:1]
        kt1 = p.op("vector", lambda v: v.tensor_reduce(sb["wabs"][:, 40:41], wv, AX.XY, OP.max),
                   after=[kst], name="ktot", small=True)
        kt2 = p.op("tensor", lambda te: te.transpose(
            psum["pbr"][0:1, 0:128], sb["wabs"][:, 40:41], sb["c_identf"][:]),
            after=[kt1, cevs[1], pbr_user], name="ktot_tp")
        kt3 = p.op("vector", lambda v: v.tensor_reduce(
            sb["sc1"][0:1, 0:1], psum["pbr"][0:1, 0:128], AX.X, OP.max),
            after=[kt2], name="kmax", small=True)
        pbr_user = kt3
        kt4 = p.op("scalar", lambda sc: sc.activation(sb["sc1"][0:1, 4:5], sb["sc1"][0:1, 0:1], AF.Ln),
                   after=[kt3], name="kln", small=True)
        kt5 = p.op("scalar", lambda sc: sc.activation(sb["sc1"][0:1, 8:9], sb["sc1"][0:1, 4:5],
                                                      AF.Exp, scale=-1.0), after=[kt4], name="kexp", small=True)
        kt6 = p.op("vector", lambda v: v.tensor_scalar(sb["sc1"][0:1, 8:9], sb["sc1"][0:1, 8:9],
                                                       7.0, None, OP.mult), after=[kt5], name="ks", small=True)
        kt7 = p.op("gpsimd", lambda g: g.tensor_scalar(sb["sc1"][0:1, 12:13], sb["sc1"][0:1, 0:1],
                                                       1.0 / 7.0, None, OP.mult), after=[kt3], name="km", small=True)
        bcs = []
        for j, off in enumerate([8, 12]):
            mm = p.op("tensor", lambda te, off=off: te.matmul(
                psum["pbr"][:, 0:1], sb["c_ones1f"][:, :], sb["sc1"][0:1, off:off + 1],
                start=True, stop=True), after=[kt6, kt7, pbr_user, cevs[3]], name=f"kb{j}_mm")
            ev = p.op("vector", lambda v, j=j: v.tensor_copy(
                sb["sccol"][:, 4 * j:4 * j + 1], psum["pbr"][:, 0:1]), after=[mm],
                name=f"kb{j}_ev", small=True)
            pbr_user = ev
            bcs.append(ev)
        kq_done = None
        for dc in range(ND):
            a1 = p.op("scalar", lambda sc, dc=dc: sc.activation(
                sb["kscr"][:], sb["kf"][:, dc, :], AF.Copy, scale=sb["sccol"][:, 0:1]),
                after=[bcs[0], ldk, kq_done], name=f"kq_a{dc}")
            r1 = p.op("vector", lambda v: v.tensor_scalar(
                sb["kscr"][:], sb["kscr"][:], MAGIC, MAGIC, OP.add, OP.subtract),
                after=[a1], name=f"kq_r{dc}")
            kq_done = p.op("scalar", lambda sc, dc=dc: sc.activation(
                sb["kq"][:, dc, :], sb["kscr"][:], AF.Copy, scale=sb["sccol"][:, 4:5]),
                after=[r1, bcs[1]], name=f"kq_c{dc}")

        st_kq = p.dma("kqst", lambda e: e.dma_start(
            out=kq_dram, in_=sb["kq"][:].rearrange("p a b -> p (a b)")),
            after=[kq_done], name=tag + "st_kq")
    else:
        kq_dram = shared["kq"]
        kq_done = p.dma("kqld2", lambda e: e.dma_start(
            out=sb["kq"][:].rearrange("p a b -> p (a b)"), in_=kq_dram),
            name=tag + "ld_kq")
    xt_done, xt_evs = emit_xphase(p, sb, psum, x_ap, quant=False, cident=cevs[0])

    y_stores = []
    out2_user = [None, None]
    ptp_user = xt_done
    prev_ep = [xt_done, xt_done]
    xa_user = None
    xaT_user = None
    for t in range(NT):
        bk = t % 2
        mm_a = p.op("tensor", lambda te, t=t: [te.matmul(
            psum["pbr"][:, 0:16], sb["xt"][:, dc, 1 + 128 * t:1 + 128 * (t + 1)],
            sb["a_bf"][:, dc, :], start=(dc == 0), stop=(dc == ND - 1))
            for dc in range(ND)][-1], after=[xt_done, ca, pbr_user], name=f"mma{t}")
        ev_a = p.op("vector", lambda v: v.tensor_copy(sb["xa_bf"][:], psum["pbr"][:, 0:16]),
                    after=[mm_a, xa_user], name=f"eva{t}")
        pbr_user = ev_a
        tp_a = p.op("tensor", lambda te: te.transpose(
            psum["ptp"][0:16, 0, 0:128], sb["xa_bf"][:], sb["c_identbf"][:]),
            after=[ev_a, ptp_user], name=f"tpa{t}")
        xa_user = tp_a
        ev_t = p.op("vector", lambda v: v.tensor_copy(sb["xaT"][:], psum["ptp"][0:16, 0, 0:128]),
                    after=[tp_a, xaT_user], name=f"evt{t}")
        ptp_user = ev_t

        def mm2(te, t=t, bk=bk):
            for piece in range(2):
                po = psum["out2"][:, bk, 512 * piece:512 * (piece + 1)]
                for dc in range(ND):
                    te.matmul(po, sb["xt"][:, dc, 1 + 128 * t:1 + 128 * (t + 1)],
                              sb["kq"][:, dc, 512 * piece:512 * (piece + 1)],
                              start=(dc == 0), stop=False)
                te.matmul(po, sb["c_ones1bf"][:, :], sb["bias_bf"][0:1, 512 * piece:512 * (piece + 1)],
                          start=False, stop=False)
                mm = te.matmul(po, sb["xaT"][:, :], sb["b_bf"][0:16, 512 * piece:512 * (piece + 1)],
                               start=False, stop=True)
            return mm
        m2 = p.op("tensor", mm2, after=[ev_t, kq_done, cbias, cb_, out2_user[bk]], name=f"mm2_{t}")
        xaT_user = m2
        sl = t % 2
        lx = p.dma(f"xe{sl}", lambda e, t=t, sl=sl: e.dma_start(
            out=sb["xbuf"][:, sl, :], in_=x_ap[128 * t:128 * (t + 1), :]),
            after=[prev_ep[sl]], name=f"xe_ld{t}")
        ep = p.op("vector", lambda v, sl=sl, bk=bk: v.scalar_tensor_tensor(
            sb["ybuf"][:, sl, :], psum["out2"][:, bk, :], 0.5,
            sb["xbuf"][:, sl, :], OP.mult, OP.add),
            after=[m2, lx], name=f"ep{t}")
        out2_user[bk] = ep
        st = p.dma(f"yst{sl}", lambda e, t=t, sl=sl: e.dma_start(
            out=y_ap[128 * t:128 * (t + 1), :], in_=sb["ybuf"][:, sl, :]),
            after=[ep], name=f"y_st{t}")
        prev_ep[sl] = st
        y_stores.append(st)

    if debug:
        y_stores = y_stores + [
            p.dma("dbg", lambda e: e.dma_start(out=dbg["d_kq"], in_=sb["kq"][:]), after=[kq_done], name="d_kq"),
            p.dma("dbg", lambda e: e.dma_start(out=dbg["d_sc1"], in_=sb["sc1"][:]), after=[kt6, kt7], name="d_sc1"),
            p.dma("dbg", lambda e: e.dma_start(out=dbg["d_xt"], in_=sb["xt"][:]), after=[xt_done], name="d_xt"),
            p.dma("dbg", lambda e: e.dma_start(out=dbg["d_g"], in_=sb["st_g"][:]), after=[fac], name="d_g"),
            p.dma("dbg", lambda e: e.dma_start(out=dbg["d_bg"], in_=sb["st_bg"][:]), after=[fac], name="d_bg"),
        ]
    p.barrier("sync", y_stores, name=tag + "final")
    stack.close()
    p.seal(name=tag + "seal")
    return {"kq": kq_dram}

def build_ffn1(debug=False):
    nc = bass.Bass("TRN2", target_bir_lowering=False)
    aps = {}
    aps["x"] = nc.dram_tensor("x", [SEQ, D], F32, kind="ExternalInput").ap()
    aps["y"] = nc.dram_tensor("y", [SEQ, D], F32, kind="ExternalOutput").ap()
    aps["kern"] = nc.dram_tensor("kern", [D, D], F32, kind="ExternalInput").ap()
    aps["bias"] = nc.dram_tensor("bias", [1, D], F32, kind="ExternalInput").ap()
    aps["la"] = nc.dram_tensor("la", [D, 16], F32, kind="ExternalInput").ap()
    aps["lb"] = nc.dram_tensor("lb", [16, D], F32, kind="ExternalInput").ap()
    cs = declare_consts(nc)
    outer = contextlib.ExitStack()
    sbc = alloc_consts(nc, outer)
    p = Prog()
    cevs = load_consts(p, cs, sbc)
    emit_ffn1(nc, p, sbc, cevs, aps, "f1_", debug=debug)
    p.emit(nc, outer)
    return nc



def run_ffn1(xs, inputs, consts):
    nc = build_ffn1()
    w = {"kern": np.ascontiguousarray(np.asarray(inputs["L1_ffn_kernel"], np.float32)),
         "bias": np.ascontiguousarray(np.asarray(inputs["L1_ffn_bias"], np.float32).reshape(1, D)),
         "la": np.ascontiguousarray(np.asarray(inputs["L1_ffn_a"], np.float32)),
         "lb": np.ascontiguousarray(np.asarray(inputs["L1_ffn_b"], np.float32))}
    in_maps = []
    for c in range(NCORES):
        m = {"x": xs[c]}
        m.update(w)
        m.update(consts)
        in_maps.append(m)
    res = run_bass_kernel_spmd(nc, in_maps, core_ids=list(range(NCORES)))
    return [res.results[c]["y"] for c in range(NCORES)]



def emit_attn1(nc, p, sbc, cevs, aps, tag, debug=False):
    x_ap, y_ap = aps["x"], aps["y"]
    wq_ap, wk_ap, wv_ap, wo_ap = aps["Wq"], aps["Wk"], aps["Wv"], aps["Wo"]
    bq_ap, bk_ap, bv_ap, bo_ap = aps["bq"], aps["bk"], aps["bv"], aps["bo"]
    qnw_ap, knw_ap = aps["qnw_t"], aps["knw_t"]

    H = 16
    ISQ = 0.125
    stack = contextlib.ExitStack()
    E = stack.enter_context
    sb = dict(sbc)
    sb["xbuf"] = E(nc.sbuf_tensor(tag + "xbuf", [128, 2, 1024], F32))
    sb["scr_sq"] = E(nc.sbuf_tensor(tag + "scr_sq", [128, 1040], F32))
    sb["scr_s"] = E(nc.sbuf_tensor(tag + "scr_s", [128, 256], F32))
    sb["xq_bf"] = E(nc.sbuf_tensor(tag + "xq_bf", [128, 1024], BF16))
    for n in ("st_sum", "st_ssq", "st_max", "st_min", "st_m", "st_rstd",
              "st_g", "st_bg", "st_iso"):
        sb[n] = E(nc.sbuf_tensor(tag + n, [128, 64], F32))
    sb["xtp"] = E(nc.sbuf_tensor(tag + "xtp", [128, ND, 128], BF16))
    sb["wcur"] = E(nc.sbuf_tensor(tag + "wcur", [128, ND, 1040], BF16))
    sb["wo_bf"] = E(nc.sbuf_tensor(tag + "wo_bf", [128, ND, 1024], BF16))
    sb["wcast"] = E(nc.sbuf_tensor(tag + "wcast", [128, 1040], F32))
    sb["brow"] = E(nc.sbuf_tensor(tag + "brow", [1, 1, 1040], BF16))
    sb["brow_f"] = E(nc.sbuf_tensor(tag + "brow_f", [1, 1040], F32))
    sb["qnrep"] = E(nc.sbuf_tensor(tag + "qnrep", [128, 1024], BF16))
    sb["knrep"] = E(nc.sbuf_tensor(tag + "knrep", [128, 1024], BF16))
    sb["nwrow"] = E(nc.sbuf_tensor(tag + "nwrow", [1, 1024], F32))
    sb["qnT"] = E(nc.sbuf_tensor(tag + "qnT", [128, ND, 2048], BF16))
    sb["knT"] = E(nc.sbuf_tensor(tag + "knT", [128, ND, 2048], BF16))
    sb["vaug"] = E(nc.sbuf_tensor(tag + "vaug", [128, 16, 16, 65], BF16))
    sb["gsig"] = E(nc.sbuf_tensor(tag + "gsig", [128, 16, 16], BF16))
    sb["eT0"] = E(nc.sbuf_tensor(tag + "eT0", [128, 16, 256], BF16))
    sb["eT1"] = E(nc.sbuf_tensor(tag + "eT1", [128, 16, 256], BF16))
    sb["negmT"] = E(nc.sbuf_tensor(tag + "negmT", [128, 2048], BF16))
    sb["mqt"] = E(nc.sbuf_tensor(tag + "mqt", [128, 136], F32))
    sb["otokS"] = E(nc.sbuf_tensor(tag + "otokS", [128, 2, 16, 65], BF16))
    sb["ob65"] = E(nc.sbuf_tensor(tag + "ob65", [65, 256], BF16))
    sb["ofT"] = E(nc.sbuf_tensor(tag + "ofT", [128, ND, 256], BF16))
    sb["qn_s"] = E(nc.sbuf_tensor(tag + "qn_s", [128, 1024], BF16))
    sb["stt"] = E(nc.sbuf_tensor(tag + "stt", [128, 128], F32))
    sb["fcol"] = E(nc.sbuf_tensor(tag + "fcol", [128, 2, 16], F32))

    psum = {
        "psA": E(nc.psum_tensor(tag + "psA", [128, 1024], F32)),
        "psB": E(nc.psum_tensor(tag + "psB", [128, 1024], F32)),
        "psC": E(nc.psum_tensor(tag + "psC", [128, 1024], F32)),
        "ptp": E(nc.psum_tensor(tag + "ptp", [128, ND, 128], BF16)),
        "pbr": E(nc.psum_tensor(tag + "pbr", [128, 512], F32)),
    }

    U = {"pbr": None, "ptp": None, "psA": None, "psB": None, "xtp": None,
         "xn_tp": None, "xslot": [None, None], "eT": None, "qnT": None,
         "knT": None, "vaug": None, "negmT": None, "otokS": None, "ofT": None,
         "mqt": None, "wcur": None, "qn_s": None, "ob65": None,
         "otokS_last": None, "psB0": None, "psB1": None,
         "psA0": None, "psA1": None, "psC0": None, "psC1": None,
         "eT0": None, "eT1": None}

    bevs = {}
    prev_b = [None]
    _baps = {0: (bq_ap, 1040), 1: (bk_ap, 1024), 2: (bv_ap, 1024), 3: (bo_ap, 1024)}

    def load_bias(j, extra=()):
        bap, cols = _baps[j]
        ld = p.dma("brl", lambda e, bap=bap, cols=cols: e.dma_start(
            out=sb["brow_f"][:, 0:cols], in_=bap),
            after=[prev_b[0]] + list(extra), name=f"br_ld{j}")
        cvt = p.op("vector", lambda v, cols=cols: v.tensor_copy(
            sb["brow"][:, 0, 0:cols], sb["brow_f"][:, 0:cols]), after=[ld],
            name=f"br_c{j}", small=True)
        bevs[j] = cvt
        prev_b[0] = cvt
        return cvt
    nwevs = {}
    for j, nap in enumerate([qnw_ap, knw_ap]):
        dest = sb["qnrep"] if j == 0 else sb["knrep"]
        ld = p.dma("brl", lambda e, nap=nap: e.dma_start(
            out=sb["nwrow"][:], in_=nap), after=[prev_b, nwevs.get(j - 1)], name=f"nw_ld{j}")
        last = None
        for pc in range(2):
            mm = p.op("tensor", lambda te, pc=pc: te.matmul(
                psum["pbr"][:, 0:512], sb["c_ones1f"][:, :],
                sb["nwrow"][0:1, 512 * pc:512 * (pc + 1)], start=True, stop=True),
                after=[ld, U["pbr"], cevs[3]], name=f"nw_mm{j}_{pc}")
            ev = p.op("vector", lambda v, pc=pc, dest=dest: v.tensor_copy(
                dest[:, 512 * pc:512 * (pc + 1)], psum["pbr"][:, 0:512]),
                after=[mm], name=f"nw_ev{j}_{pc}")
            U["pbr"] = ev
            last = ev
        nwevs[j] = last
    wo_done = None
    for i in range(ND):
        ld = p.dma("wcl", lambda e, i=i: e.dma_start(
            out=sb["wcast"][:, 0:1024], in_=wo_ap[128 * i:128 * (i + 1), :]),
            after=[wo_done], name=f"wo_ld{i}")
        wo_done = p.op("vector", lambda v, i=i: v.tensor_copy(
            sb["wo_bf"][:, i, :], sb["wcast"][:, 0:1024]), after=[ld], name=f"wo_c{i}")

    fac = emit_stats_factors(p, sb, x_ap)
    U["xslot"] = [fac, fac]

    def build_xn_tile(t):
        sl = t % 2
        ld = p.dma(f"xn{sl}", lambda e, t=t, sl=sl: e.dma_start(
            out=sb["xbuf"][:, sl, :], in_=x_ap[128 * t:128 * (t + 1), :]),
            after=[U["xslot"][sl], fac], name=f"xn_ld{t}")
        a1 = p.op("scalar", lambda s, t=t, sl=sl: s.activation(
            sb["xq_bf"][:], sb["xbuf"][:, sl, :], AF.Identity,
            bias=sb["st_bg"][:, 4 * t:4 * t + 1], scale=sb["st_g"][:, 4 * t:4 * t + 1]),
            after=[ld, U["xn_tp"]], name=f"xn_a{t}")
        U["xslot"][sl] = a1
        tp = p.op("tensor", lambda te: [te.transpose(
            psum["ptp"][:, dc, :], sb["xq_bf"][:, 128 * dc:128 * (dc + 1)],
            sb["c_identbf"][:]) for dc in range(ND)][-1],
            after=[a1, U["ptp"], cevs[0], U["xtp"]], name=f"xn_tpq{t}")
        ev = p.op("vector", lambda v: v.tensor_copy(sb["xtp"][:, :, :], psum["ptp"][:, :, :]),
                  after=[tp], name=f"xn_evq{t}")
        U["xn_tp"] = tp
        U["ptp"] = ev
        return ev

    vz = p.op("vector", lambda v: v.memset(sb["vaug"][:, :, :, 64:65], 1.0), name="v_ones")

    def load_w(name, wap, cols, prev):
        last = None
        for i in range(ND):
            ld = p.dma("wcl", lambda e, i=i, wap=wap, cols=cols: e.dma_start(
                out=sb["wcast"][:, 0:cols], in_=wap[128 * i:128 * (i + 1), :]),
                after=[prev], name=f"wl_{name}{i}")
            cvt = p.op("vector", lambda v, i=i, cols=cols: v.tensor_copy(
                sb["wcur"][:, i, 0:cols], sb["wcast"][:, 0:cols]),
                after=[ld, U["wcur"]], name=f"wc_{name}{i}")
            prev = cvt
            last = cvt
        return last

    qkv_last = None
    wprev = wo_done
    rn_prev = None
    for phase, (wap, cols, bj, nrep, dstT) in enumerate(
            [(wq_ap, 1040, 0, "qnrep", "qnT"), (wk_ap, 1024, 1, "knrep", "knT"),
             (wv_ap, 1024, 2, None, None)]):
        wld = load_w("qkv%d" % phase, wap, cols, wprev)
        wprev = wld
        load_bias(bj, extra=[qkv_last])
        last_mm = None
        for t in range(NT):
            xe = build_xn_tile(t)
            npieces = [(0, 512), (512, 512)] + ([(1024, 16)] if cols == 1040 else [])

            def qmm(te, t=t, npieces=npieces, bj=bj):
                for off, n in npieces:
                    tgt = psum["psA"][:, off:off + n] if off < 1024 else psum["pbr"][:, 0:16]
                    for dc in range(ND):
                        te.matmul(tgt, sb["xtp"][:, dc, :],
                                  sb["wcur"][:, dc, off:off + n], start=(dc == 0), stop=False)
                    mm = te.matmul(tgt, sb["c_ones1bf"][:, :], sb["brow"][0:1, 0, off:off + n],
                                   start=False, stop=True)
                return mm
            mm = p.op("tensor", qmm,
                      after=[xe, wld, bevs[bj], cevs[2], U["psA"]] +
                            ([U["pbr"]] if cols == 1040 else []),
                      name=f"qkv{phase}_mm{t}")
            U["xtp"] = mm
            last_mm = mm
            if phase == 2:
                ev = p.op("vector", lambda v, t=t: v.tensor_copy(
                    sb["vaug"][:, t, :, 0:64],
                    psum["psA"][:, 0:1024].rearrange("p (h i) -> p h i", i=64)),
                    after=[mm, vz, U["vaug"]], name=f"vev{t}")
                U["psA"] = ev
                U["vaug"] = None
                qkv_last = ev
                continue
            if phase == 0:
                gs = p.op("scalar", lambda s, t=t: s.activation(
                    sb["gsig"][:, t, :], psum["pbr"][:, 0:16], AF.Sigmoid),
                    after=[mm], name=f"gsig{t}")
                U["pbr"] = gs
            sq = p.op("scalar", lambda s: s.activation(
                sb["scr_sq"][:, 0:1024], psum["psA"][:, 0:1024], AF.Square),
                after=[mm, rn_prev], name=f"rn_sq{phase}_{t}")
            sg = p.op("vector", lambda v: v.tensor_reduce(
                sb["stt"][:, 0:16], sb["scr_sq"][:, 0:1024].rearrange("p (h i) -> p h i", i=64),
                AX.X, OP.add), after=[sq, rn_prev], name=f"rn_red{phase}_{t}", small=True)
            s1 = p.op("gpsimd", lambda g: g.tensor_scalar(
                sb["stt"][:, 16:32], sb["stt"][:, 0:16], 1.0 / 64.0, 1e-6, OP.mult, OP.add),
                after=[sg], name=f"rn_m{phase}_{t}", small=True)
            s2 = p.op("scalar", lambda sc: sc.activation(
                sb["stt"][:, 32:48], sb["stt"][:, 16:32], AF.Ln),
                after=[s1], name=f"rn_l{phase}_{t}", small=True)
            s3 = p.op("scalar", lambda sc: sc.activation(
                sb["stt"][:, 48:64], sb["stt"][:, 32:48], AF.Exp, scale=-0.5),
                after=[s2], name=f"rn_e{phase}_{t}", small=True)

            def rnmul(v):
                for h in range(H):
                    r = v.tensor_scalar(sb["qn_s"][:, 64 * h:64 * (h + 1)],
                                        psum["psA"][:, 64 * h:64 * (h + 1)],
                                        sb["stt"][:, 48 + h:49 + h], None, OP.mult)
                return r
            s4 = p.op("vector", rnmul, after=[s3, U["qn_s"]], name=f"rn_x{phase}_{t}")
            U["psA"] = s4
            s5 = p.op("vector", lambda v, nrep=nrep: v.tensor_tensor(
                sb["xq_bf"][:, 0:1024], sb["qn_s"][:], sb[nrep][:], OP.mult),
                after=[s4, nwevs[0 if nrep == "qnrep" else 1], U["xn_tp"]],
                name=f"rn_y{phase}_{t}")
            U["qn_s"] = s5
            rn_prev = s4
            tp = p.op("tensor", lambda te: [te.transpose(
                psum["ptp"][:, dc, :], sb["xq_bf"][:, 128 * dc:128 * (dc + 1)],
                sb["c_identbf"][:]) for dc in range(ND)][-1],
                after=[s5, U["ptp"]], name=f"rn_tp{phase}_{t}")
            U["xn_tp"] = tp
            ev = p.op("vector", lambda v, t=t, dstT=dstT: v.tensor_copy(
                sb[dstT][:, :, 128 * t:128 * (t + 1)], psum["ptp"][:, :, :]),
                after=[tp, U[dstT]], name=f"rn_ev{phase}_{t}")
            U["ptp"] = ev
            qkv_last = ev
        U["wcur"] = last_mm
    load_bias(3, extra=[qkv_last])
    qkv_bar = qkv_last

    U["psA0"] = U["psA"]
    U["psA1"] = U["psA"]
    U["psB0"] = U["psB"]
    U["psB1"] = U["psB"]
    stat_state = {"last": None}

    def emit_stats_qt(qt):
        stat_last = stat_state["last"]
        for h in range(H):
            rb = 64 * (h % 2)
            ch = h // 2
            for half, ps in ((0, "psA"), (1, "psB")):
                def smm2(te, rb=rb, ch=ch, qt=qt, half=half, ps=ps):
                    for pc in range(2):
                        mm = te.matmul(psum[ps][:, 512 * pc:512 * (pc + 1)],
                                       sb["qnT"][rb:rb + 64, ch, 128 * qt:128 * (qt + 1)],
                                       sb["knT"][rb:rb + 64, ch,
                                                 1024 * half + 512 * pc:1024 * half + 512 * (pc + 1)],
                                       start=True, stop=True)
                    return mm
                mm = p.op("tensor", smm2, after=[qkv_bar, U[ps + "0"], U[ps + "1"]],
                          name=f"st_mm{qt}_{h}_{half}")
                rd = p.op("vector", lambda v, ps=ps, half=half, h=h: v.tensor_reduce(
                    sb["mqt"][:, 8 * h + 4 * half:8 * h + 4 * half + 1], psum[ps][:, :],
                    AX.X, OP.max, negate=True),
                    after=[mm, U["mqt"]], name=f"st_rd{qt}_{h}_{half}", small=True)
                U[ps + "0"] = rd
                U[ps + "1"] = rd
                stat_last = rd
        mva = sb["mqt"][:, 0:128].rearrange("p (t u) -> p t u", u=8)[:, :, 0:1]
        mvb = sb["mqt"][:, 4:132].rearrange("p (t u) -> p t u", u=8)[:, :, 0:1]
        cmb = p.op("vector", lambda v, mva=mva, mvb=mvb: v.tensor_tensor(
            mva, mva, mvb, OP.min),
            after=[U["psA0"], U["psB0"]], name=f"st_cmb{qt}", small=True)
        tqv = sb["mqt"][:, 0:128].rearrange("p (t u) -> p t u", u=8)[:, :, 0:1].rearrange(
            "p t u -> p (t u)")
        tp = p.op("tensor", lambda te, tqv=tqv: te.transpose(
            psum["pbr"][0:16, 0:128], tqv, sb["c_identf"][:]),
            after=[cmb, U["pbr"], cevs[1]], name=f"st_tp{qt}")
        ev1 = p.op("vector", lambda v, qt=qt: v.tensor_copy(
            sb["negmT"][0:16, 128 * qt:128 * (qt + 1)], psum["pbr"][0:16, 0:128]),
            after=[tp, U["negmT"]], name=f"st_ev{qt}")
        ev2 = p.op("vector", lambda v, qt=qt: v.tensor_copy(
            sb["negmT"][64:80, 128 * qt:128 * (qt + 1)], psum["pbr"][0:16, 0:128]),
            after=[ev1], name=f"st_ev2{qt}")
        U["pbr"] = ev2
        U["mqt"] = tp
        stat_state["last"] = ev2
        return ev2

    U["psC0"] = None
    U["psC1"] = None
    y_stores = []
    prev_ep = [None, None]
    for sp in range(8):
        q0 = 256 * sp
        emit_stats_qt(2 * sp)
        negm_bar = emit_stats_qt(2 * sp + 1)
        for hp in range(H // 2):
            ch = hp
            exp_evs = {0: [], 1: []}
            for kt in range(NT):
                rot = kt % 2
                pst = "psB" if rot == 0 else "psC"
                for half, rb in ((0, 0), (1, 64)):
                    h = 2 * hp + half

                    def stmm(te, rb=rb, ch=ch, kt=kt, q0=q0, pst=pst, h=h, half=half):
                        te.matmul(psum[pst][:, 512 * half:512 * half + 256],
                                  sb["knT"][rb:rb + 64, ch, 128 * kt:128 * (kt + 1)],
                                  sb["qnT"][rb:rb + 64, ch, q0:q0 + 256],
                                  start=True, stop=False)
                        return te.matmul(psum[pst][:, 512 * half:512 * half + 256],
                                         sb["c_sels"][rb:rb + 16, h, :],
                                         sb["negmT"][rb:rb + 16, q0:q0 + 256],
                                         start=False, stop=True)
                    mm = p.op("tensor", stmm, after=[negm_bar, U[pst + str(half)], cevs[4]],
                              name=f"sT{sp}_{h}_{kt}")
                    ex = p.op("scalar", lambda s, kt=kt, pst=pst, half=half, hb=half: s.activation(
                        sb["eT%d" % hb][:, kt, :], psum[pst][:, 512 * half:512 * half + 256],
                        AF.Exp, scale=ISQ), after=[mm, U["eT%d" % half]],
                        name=f"sTe{sp}_{h}_{kt}")
                    U[pst + str(half)] = ex
                    exp_evs[half].append(ex)
            for half in (0, 1):
                h = 2 * hp + half
                pva = half

                def pvmm(te, h=h, pva=pva, half=half):
                    for kt in range(NT):
                        mm = te.matmul(psum["psA"][0:65, 512 * pva:512 * pva + 256],
                                       sb["vaug"][:, kt, h, :],
                                       sb["eT%d" % half][:, kt, :], start=(kt == 0), stop=(kt == 15))
                    return mm
                pv = p.op("tensor", pvmm, after=exp_evs[half] + [U["psA%d" % pva], qkv_bar],
                          name=f"pv{sp}_{h}")
                U["eT%d" % half] = pv
                ob = p.op("vector", lambda v, pva=pva: v.tensor_copy(
                    sb["ob65"][:, :], psum["psA"][0:65, 512 * pva:512 * pva + 256]),
                    after=[pv, U["ob65"]], name=f"obev{sp}_{h}")
                U["psA%d" % pva] = ob
                for tt2 in range(2):
                    tp = p.op("tensor", lambda te, tt2=tt2: te.transpose(
                        psum["ptp"][:, 0, 0:65],
                        sb["ob65"][:, 128 * tt2:128 * (tt2 + 1)], sb["c_identbf"][0:65, 0:65]),
                        after=[ob, U["ptp"]], name=f"otp{sp}_{h}_{tt2}")
                    ev = p.op("vector", lambda v, tt2=tt2, h=h: v.tensor_copy(
                        sb["otokS"][:, tt2, h, :], psum["ptp"][:, 0, 0:65]),
                        after=[tp, U["otokS"]], name=f"otev{sp}_{h}_{tt2}")
                    U["ptp"] = ev
                    U["ob65"] = tp
                    U["otokS_last"] = ev
        ot_bar = U["otokS_last"]
        of_evs = []
        for tt2 in range(2):
            gt = 2 * sp + tt2
            rsv = sb["otokS"][:, tt2, :, 64:65]
            rc = p.op("vector", lambda v, tt2=tt2, rsv=rsv: v.reciprocal(
                sb["fcol"][:, tt2, :], rsv), after=[ot_bar], name=f"frec{sp}_{tt2}", small=True)
            fm = p.op("gpsimd", lambda g, tt2=tt2, gt=gt: g.tensor_tensor(
                sb["fcol"][:, tt2, :], sb["fcol"][:, tt2, :], sb["gsig"][:, gt, :], OP.mult),
                after=[rc, qkv_bar], name=f"fmul{sp}_{tt2}", small=True)

            def fscale(v, tt2=tt2):
                for h in range(H):
                    r = v.tensor_scalar(sb["xq_bf"][:, 64 * h:64 * (h + 1)],
                                        sb["otokS"][:, tt2, h, 0:64],
                                        sb["fcol"][:, tt2, h:h + 1], None, OP.mult)
                return r
            fs = p.op("vector", fscale, after=[fm, ot_bar, U.get("xqo")], name=f"fsc{sp}_{tt2}")
            tpf = p.op("tensor", lambda te: [te.transpose(
                psum["ptp"][:, dc, :],
                sb["xq_bf"][:, 128 * dc:128 * (dc + 1)],
                sb["c_identbf"][:]) for dc in range(ND)][-1],
                after=[fs, U["ptp"]], name=f"ftp{sp}_{tt2}")
            U["xqo"] = tpf
            fe = p.op("vector", lambda v, tt2=tt2: v.tensor_copy(
                sb["ofT"][:, :, 128 * tt2:128 * (tt2 + 1)], psum["ptp"][:, :, :]),
                after=[tpf, U["ofT"]], name=f"fev{sp}_{tt2}")
            U["ptp"] = fe
            U["otokS"] = fs
            of_evs.append(fe)
        for tt2 in range(2):
            gt = 2 * sp + tt2

            def omm(te, tt2=tt2):
                for pc in range(2):
                    for dc in range(ND):
                        te.matmul(psum["psB"][:, 512 * pc:512 * (pc + 1)],
                                  sb["ofT"][:, dc, 128 * tt2:128 * (tt2 + 1)],
                                  sb["wo_bf"][:, dc, 512 * pc:512 * (pc + 1)],
                                  start=(dc == 0), stop=False)
                    mm = te.matmul(psum["psB"][:, 512 * pc:512 * (pc + 1)],
                                   sb["c_ones1bf"][:, :],
                                   sb["brow"][0:1, 0, 512 * pc:512 * (pc + 1)],
                                   start=False, stop=True)
                return mm
            m2 = p.op("tensor", omm, after=of_evs + [wo_done, bevs[3], U["psB0"], U["psB1"], U["psB"]], name=f"omm{gt}")
            U["ofT"] = m2
            sl = gt % 2
            lx = p.dma(f"xe{sl}", lambda e, gt=gt, sl=sl: e.dma_start(
                out=sb["xbuf"][:, sl, :], in_=x_ap[128 * gt:128 * (gt + 1), :]),
                after=[prev_ep[0], prev_ep[1], U["xslot"][sl]], name=f"xe_ld{gt}")
            ep = p.op("vector", lambda v, sl=sl: v.scalar_tensor_tensor(
                sb["xbuf"][:, 1 - sl, :], psum["psB"][:, 0:1024], 1.0,
                sb["xbuf"][:, sl, :], OP.mult, OP.add),
                after=[m2, lx, prev_ep[0], prev_ep[1], U["xslot"][1 - sl]], name=f"aep{gt}")
            U["psB"] = ep
            U["psB0"] = ep
            U["psB1"] = ep
            U["psC0"] = ep
            U["psC1"] = ep
            U["psA"] = ep
            U["xslot"][sl] = ep
            U["xslot"][1 - sl] = ep
            st = p.dma(f"yst{sl}", lambda e, gt=gt, sl=sl: e.dma_start(
                out=y_ap[128 * gt:128 * (gt + 1), :], in_=sb["xbuf"][:, 1 - sl, :]),
                after=[ep], name=f"y_st{gt}")
            prev_ep[sl] = st
            y_stores.append(st)

    p.barrier("sync", y_stores, name=tag + "final")
    stack.close()
    p.seal(name=tag + "seal")

def build_attn1(debug=False):
    nc = bass.Bass("TRN2", target_bir_lowering=False)
    aps = {}
    aps["x"] = nc.dram_tensor("x", [SEQ, D], F32, kind="ExternalInput").ap()
    aps["y"] = nc.dram_tensor("y", [SEQ, D], F32, kind="ExternalOutput").ap()
    aps["Wq"] = nc.dram_tensor("Wq", [D, 1040], F32, kind="ExternalInput").ap()
    aps["Wk"] = nc.dram_tensor("Wk", [D, D], F32, kind="ExternalInput").ap()
    aps["Wv"] = nc.dram_tensor("Wv", [D, D], F32, kind="ExternalInput").ap()
    aps["Wo"] = nc.dram_tensor("Wo", [D, D], F32, kind="ExternalInput").ap()
    aps["bq"] = nc.dram_tensor("bq", [1, 1040], F32, kind="ExternalInput").ap()
    aps["bk"] = nc.dram_tensor("bk", [1, D], F32, kind="ExternalInput").ap()
    aps["bv"] = nc.dram_tensor("bv", [1, D], F32, kind="ExternalInput").ap()
    aps["bo"] = nc.dram_tensor("bo", [1, D], F32, kind="ExternalInput").ap()
    aps["qnw_t"] = nc.dram_tensor("qnw_t", [1, D], F32, kind="ExternalInput").ap()
    aps["knw_t"] = nc.dram_tensor("knw_t", [1, D], F32, kind="ExternalInput").ap()
    cs = declare_consts(nc)
    outer = contextlib.ExitStack()
    sbc = alloc_consts(nc, outer)
    p = Prog()
    cevs = load_consts(p, cs, sbc)
    emit_attn1(nc, p, sbc, cevs, aps, "a1_", debug=debug)
    p.emit(nc, outer)
    return nc



def run_attn1(xs, inputs, consts):
    nc = build_attn1()
    w = {"Wq": np.ascontiguousarray(np.asarray(inputs["L1_attn_Wq"], np.float32)),
         "Wk": np.ascontiguousarray(np.asarray(inputs["L1_attn_Wk"], np.float32)),
         "Wv": np.ascontiguousarray(np.asarray(inputs["L1_attn_Wv"], np.float32)),
         "Wo": np.ascontiguousarray(np.asarray(inputs["L1_attn_Wo"], np.float32)),
         "bq": np.asarray(inputs["L1_attn_bq"], np.float32).reshape(1, 1040),
         "bk": np.asarray(inputs["L1_attn_bk"], np.float32).reshape(1, D),
         "bv": np.asarray(inputs["L1_attn_bv"], np.float32).reshape(1, D),
         "bo": np.asarray(inputs["L1_attn_bo"], np.float32).reshape(1, D),
         "qnw_t": np.ascontiguousarray(np.tile(np.asarray(inputs["L1_attn_qn_w"], np.float32), 16).reshape(1, D)),
         "knw_t": np.ascontiguousarray(np.tile(np.asarray(inputs["L1_attn_kn_w"], np.float32), 16).reshape(1, D))}
    in_maps = []
    for c in range(NCORES):
        m = {"x": xs[c]}
        m.update(w)
        m.update(consts)
        in_maps.append(m)
    res = run_bass_kernel_spmd(nc, in_maps, core_ids=list(range(NCORES)))
    return [res.results[c]["y"] for c in range(NCORES)]



def build_fused():
    nc = bass.Bass("TRN2", target_bir_lowering=False)
    F = F32
    ap = {}
    ap["x"] = nc.dram_tensor("x", [SEQ, D], F, kind="ExternalInput").ap()
    ap["y"] = nc.dram_tensor("y", [SEQ, D], F, kind="ExternalOutput").ap()
    xs0 = nc.dram_tensor("xscr0", [SEQ, D], F).ap()
    xs1 = nc.dram_tensor("xscr1", [SEQ, D], F).ap()
    ap["w1T"] = nc.dram_tensor("w1T", [D, DH], F, kind="ExternalInput").ap()
    ap["w2T"] = nc.dram_tensor("w2T", [DH, D], F, kind="ExternalInput").ap()
    ap["cw"] = nc.dram_tensor("cw", [128, 96], F, kind="ExternalInput").ap()
    ap["cb"] = nc.dram_tensor("cb", [128, 32], F, kind="ExternalInput").ap()
    ap["wq12"] = nc.dram_tensor("wq12", [D, 2 * D], F, kind="ExternalInput").ap()
    ap["wk12"] = nc.dram_tensor("wk12", [D, 2 * D], F, kind="ExternalInput").ap()
    ap["wv"] = nc.dram_tensor("wv", [D, D], F, kind="ExternalInput").ap()
    ap["lam4"] = nc.dram_tensor("lam4", [128, 4, 8], F, kind="ExternalInput").ap()
    ap["kern"] = nc.dram_tensor("kern", [D, D], F, kind="ExternalInput").ap()
    ap["bias"] = nc.dram_tensor("bias", [1, D], F, kind="ExternalInput").ap()
    ap["la"] = nc.dram_tensor("la", [D, 16], F, kind="ExternalInput").ap()
    ap["lb"] = nc.dram_tensor("lb", [16, D], F, kind="ExternalInput").ap()
    ap["Wq"] = nc.dram_tensor("Wq", [D, 1040], F, kind="ExternalInput").ap()
    ap["Wk"] = nc.dram_tensor("Wk", [D, D], F, kind="ExternalInput").ap()
    ap["Wv"] = nc.dram_tensor("Wv", [D, D], F, kind="ExternalInput").ap()
    ap["Wo"] = nc.dram_tensor("Wo", [D, D], F, kind="ExternalInput").ap()
    ap["bq"] = nc.dram_tensor("bq", [1, 1040], F, kind="ExternalInput").ap()
    ap["bk"] = nc.dram_tensor("bk", [1, D], F, kind="ExternalInput").ap()
    ap["bv"] = nc.dram_tensor("bv", [1, D], F, kind="ExternalInput").ap()
    ap["bo"] = nc.dram_tensor("bo", [1, D], F, kind="ExternalInput").ap()
    ap["qnw_t"] = nc.dram_tensor("qnw_t", [1, D], F, kind="ExternalInput").ap()
    ap["knw_t"] = nc.dram_tensor("knw_t", [1, D], F, kind="ExternalInput").ap()
    cs = declare_consts(nc)
    outer = contextlib.ExitStack()
    sbc = alloc_consts(nc, outer)
    p = Prog()
    cevs = load_consts(p, cs, sbc)
    chain = [ap["x"], xs0, xs1, xs0, xs1, xs0, ap["y"]]

    def io(i):
        d = dict(ap)
        d["x"], d["y"] = chain[i], chain[i + 1]
        return d
    sh0 = emit_ffn0(nc, p, sbc, cevs, io(0), "s1_")
    emit_attn0(nc, p, sbc, cevs, io(1), "s2_")
    emit_ffn0(nc, p, sbc, cevs, io(2), "s3_", shared=sh0)
    sh1 = emit_ffn1(nc, p, sbc, cevs, io(3), "s4_")
    emit_attn1(nc, p, sbc, cevs, io(4), "s5_")
    emit_ffn1(nc, p, sbc, cevs, io(5), "s6_", shared=sh1)
    p.emit(nc, outer)
    return nc


def _fused_inputs(inputs):
    w = _layout_ffn0_weights(inputs)
    lam4 = np.stack([np.asarray(inputs[k], np.float32).reshape(128, 8) for k in
                     ("L0_attn_lambda_q1", "L0_attn_lambda_k1",
                      "L0_attn_lambda_q2", "L0_attn_lambda_k2")], axis=1)
    w.update({
        "wq12": np.ascontiguousarray(np.asarray(inputs["L0_attn_w_q12"], np.float32)),
        "wk12": np.ascontiguousarray(np.asarray(inputs["L0_attn_w_k12"], np.float32)),
        "wv": np.ascontiguousarray(np.asarray(inputs["L0_attn_w_v"], np.float32)),
        "lam4": np.ascontiguousarray(lam4),
        "kern": np.ascontiguousarray(np.asarray(inputs["L1_ffn_kernel"], np.float32)),
        "bias": np.ascontiguousarray(np.asarray(inputs["L1_ffn_bias"], np.float32).reshape(1, D)),
        "la": np.ascontiguousarray(np.asarray(inputs["L1_ffn_a"], np.float32)),
        "lb": np.ascontiguousarray(np.asarray(inputs["L1_ffn_b"], np.float32)),
        "Wq": np.ascontiguousarray(np.asarray(inputs["L1_attn_Wq"], np.float32)),
        "Wk": np.ascontiguousarray(np.asarray(inputs["L1_attn_Wk"], np.float32)),
        "Wv": np.ascontiguousarray(np.asarray(inputs["L1_attn_Wv"], np.float32)),
        "Wo": np.ascontiguousarray(np.asarray(inputs["L1_attn_Wo"], np.float32)),
        "bq": np.asarray(inputs["L1_attn_bq"], np.float32).reshape(1, 1040),
        "bk": np.asarray(inputs["L1_attn_bk"], np.float32).reshape(1, D),
        "bv": np.asarray(inputs["L1_attn_bv"], np.float32).reshape(1, D),
        "bo": np.asarray(inputs["L1_attn_bo"], np.float32).reshape(1, D),
        "qnw_t": np.ascontiguousarray(np.tile(np.asarray(inputs["L1_attn_qn_w"], np.float32), 16).reshape(1, D)),
        "knw_t": np.ascontiguousarray(np.tile(np.asarray(inputs["L1_attn_kn_w"], np.float32), 16).reshape(1, D)),
    })
    w.update(host_consts())
    return w



def _layout_ffn0_weights(inputs):
    w1T = np.ascontiguousarray(inputs["L0_ffn_w1"].T).astype(np.float32)
    w2T = np.ascontiguousarray(inputs["L0_ffn_w2"].T).astype(np.float32)
    cw = np.ascontiguousarray(
        np.asarray(inputs["L0_ffn_conv_w"]).reshape(32, 128, 3).transpose(1, 0, 2).reshape(128, 96)
    ).astype(np.float32)
    cb = np.ascontiguousarray(np.asarray(inputs["L0_ffn_conv_b"]).reshape(32, 128).T).astype(np.float32)
    return {"w1T": w1T, "w2T": w2T, "cw": cw, "cb": cb}


def run_ffn0(xs, inputs, consts):
    nc = build_ffn0()
    w = _layout_ffn0_weights(inputs)
    in_maps = []
    for c in range(NCORES):
        m = {"x": xs[c]}
        m.update(w)
        m.update(consts)
        in_maps.append(m)
    res = run_bass_kernel_spmd(nc, in_maps, core_ids=list(range(NCORES)))
    return [res.results[c]["y"] for c in range(NCORES)]


def kernel(**inputs):
    x = np.ascontiguousarray(np.asarray(inputs["L0_attn_x"], dtype=np.float32))
    nc = build_fused()
    w = _fused_inputs(inputs)
    in_maps = []
    for c in range(NCORES):
        m = {"x": np.ascontiguousarray(x[c])}
        m.update(w)
        in_maps.append(m)
    res = run_bass_kernel_spmd(nc, in_maps, core_ids=list(range(NCORES)))
    return np.stack([res.results[c]["y"] for c in range(NCORES)]).reshape(NCORES, SEQ, D)


# version 13 (final)
# speedup vs baseline: 1.0408x; 1.0408x over previous
import contextlib
import numpy as np
import ml_dtypes
import concourse.bass as bass
import concourse.mybir as mybir
from concourse.alu_op_type import AluOpType as OP
from concourse.bass_utils import run_bass_kernel_spmd

F32 = mybir.dt.float32
BF16 = mybir.dt.bfloat16
F16 = mybir.dt.float16
AF = mybir.ActivationFunctionType
AX = mybir.AxisListType

MAGIC = float(1.5 * 2 ** 23)
NCORES = 8
SEQ = 2048
D = 1024
DH = 4096
NT = SEQ // 128
ND = D // 128
NH = DH // 128
ENGS = ("sync", "tensor", "vector", "scalar", "gpsimd")


class _Op:
    __slots__ = ("eng", "fn", "after", "kind", "key", "sig", "val", "name", "small")

    def __init__(self, eng, fn, after, kind, key, name, small=False):
        flat = []
        for a in after:
            if a is None:
                continue
            if isinstance(a, (list, tuple)):
                flat.extend(x for x in a if x is not None)
            else:
                flat.append(a)
        self.eng, self.fn, self.after, self.kind, self.key, self.name = (
            eng, fn, tuple(flat), kind, key, name)
        self.sig = False
        self.val = None
        self.small = small


class Prog:

    def __init__(self):
        self.seq = {e: [] for e in ENGS}
        self.all = []
        self.last_dma = {}

    def op(self, eng, fn, after=(), name="", small=False):
        o = _Op(eng, fn, after, "c", None, name, small)
        self.seq[eng].append(o)
        self.all.append(o)
        return o

    def dma(self, key, fn, after=(), eng="sync", name=""):
        o = _Op(eng, fn, after, "d", key, name)
        self.seq[eng].append(o)
        self.all.append(o)
        self.last_dma[key] = o
        return o

    def seal(self, name="seal"):
        lasts = [self.seq[e][-1] for e in ENGS if self.seq[e]]
        lasts += list(self.last_dma.values())
        for e in ENGS:
            self.op(e, lambda eng: eng.nop(), after=lasts, name=f"{name}_{e}")

    def barrier(self, eng, after, name="bar"):
        return self.op(eng, lambda e: e.nop(), after, name)

    def emit(self, nc, stack):
        for o in self.all:
            if o.kind == "d":
                o.sig = True
            for d in o.after:
                d.sig = True
        sems = {}
        for e in ENGS:
            if any(o.sig and o.kind == "c" for o in self.seq[e]):
                sems["c:" + e] = stack.enter_context(nc.semaphore(f"se_{e}"))
        dkeys = sorted({o.key for o in self.all if o.kind == "d" and o.sig})
        for k in dkeys:
            sems["d:" + k] = stack.enter_context(nc.semaphore(f"sd_{k}"))
        ccount = {e: 0 for e in ENGS}
        dcount = {}
        for e in ENGS:
            for o in self.seq[e]:
                if not o.sig:
                    continue
                if o.kind == "c":
                    ccount[e] += 1
                    o.val = ("c:" + e, ccount[e], 1)
                else:
                    dcount[o.key] = dcount.get(o.key, 0) + 16
                    o.val = ("d:" + o.key, dcount[o.key], 16)
        for o in self.all:
            for d in o.after:
                assert d.val is not None, f"dep of {o.name} not assigned"

        block = stack.enter_context(nc.Block())
        prog = self

        def mk_body(ename):
            def body(eng):
                waited = {}
                for o in prog.seq[ename]:
                    for d in o.after:
                        if d.eng == ename and d.kind == "c" and not d.small:
                            continue
                        sname, v, _ = d.val
                        if waited.get(sname, 0) >= v:
                            continue
                        eng.wait_ge(sems[sname], v)
                        waited[sname] = v
                    inst = o.fn(eng) if o.fn is not None else None
                    if o.sig:
                        assert inst is not None, f"signaling op {o.name} has no inst"
                        sname, _, inc = o.val
                        inst.then_inc(sems[sname], inc)
            return body

        for e in ENGS:
            if self.seq[e]:
                getattr(block, e)(mk_body(e))
        stack.close()



def host_consts():
    sels = np.zeros((128, 16, 128), dtype=ml_dtypes.bfloat16)
    for j in range(16):
        sels[j, j, :] = 1.0
        sels[64 + j, j, :] = 1.0
    selsf4 = np.zeros((128, 4, 128), dtype=np.float32)
    for j in range(4):
        selsf4[j, j, :] = 1.0
    return {"identbf": np.eye(128, dtype=ml_dtypes.bfloat16),
            "identf": np.eye(128, dtype=np.float32),
            "ones1bf": np.ones((1, 128), dtype=ml_dtypes.bfloat16),
            "ones1f": np.ones((1, 128), dtype=np.float32),
            "sels": sels, "selsf4": selsf4}


CONST_SPECS = [("identbf", [128, 128], BF16), ("identf", [128, 128], F32),
               ("ones1bf", [1, 128], BF16), ("ones1f", [1, 128], F32),
               ("sels", [128, 16, 128], BF16), ("selsf4", [128, 4, 128], F32)]


def declare_consts(nc, names=None):
    return {name: nc.dram_tensor(name, shape, dt, kind="ExternalInput").ap()
            for name, shape, dt in CONST_SPECS if names is None or name in names}


def alloc_consts(nc, stack, names=None):
    return {"c_" + name: stack.enter_context(nc.sbuf_tensor("c_" + name, shape, dt))
            for name, shape, dt in CONST_SPECS if names is None or name in names}


def load_consts(p, cs, sb):
    evs = {}
    for i, (name, _, _) in enumerate(CONST_SPECS):
        if name in cs:
            evs[i] = p.dma("c_" + name, lambda e, n=name: e.dma_start(
                out=sb["c_" + n][:], in_=cs[n]), name=f"ld_{name}")
        else:
            evs[i] = None
    return evs



def emit_xphase(p, sb, psum, x_ap, quant, cident, eps=1e-5):
    xs = sb["xbuf"]
    ptp = psum["ptp"]
    zp = p.op("vector", lambda v: v.memset(sb["xt"][:, :, 0:1], 0.0), name="xtpad0")
    zp2 = p.op("vector", lambda v: v.memset(sb["xt"][:, :, 2049:2050], 0.0), after=[zp])
    slot_free = [None, None]
    prev_tp = None
    prev_evac = zp2
    prev_c1b = None
    evac_list = []
    for t in range(NT):
        sl = t % 2
        C = lambda buf, off, t: sb[buf][:, off + 4 * t:off + 4 * t + 1]
        ld = p.dma(f"xs{sl}", lambda e, t=t, sl=sl: e.dma_start(
            out=xs[:, sl, :], in_=x_ap[128 * t:128 * (t + 1), :]),
            after=[slot_free[sl]], name=f"x_ld{t}")
        oa = p.op("scalar", lambda s, sl=sl, t=t: s.activation(
            sb["scr_sq"][:, 0:1024], xs[:, sl, :], AF.Square,
            accum_out=C("st_ssq", 0, t)), after=[ld, prev_c1b], name=f"xssq{t}")

        def dv(v, t=t, sl=sl):
            r = v.tensor_reduce(C("st_sum", 0, t), xs[:, sl, :], AX.X, OP.add)
            if quant:
                v.tensor_reduce(C("st_max", 0, t), xs[:, sl, :], AX.X, OP.max)
                r = v.tensor_reduce(C("st_min", 0, t), xs[:, sl, :], AX.X, OP.min)
            return r
        od = p.op("vector", dv, after=[ld], name=f"xred{t}", small=True)
        c1 = p.op("gpsimd", lambda g, t=t: g.tensor_scalar(C("st_m", 0, t), C("st_sum", 0, t),
                  1.0 / D, None, OP.mult), after=[od], name=f"xm{t}", small=True)
        c2 = p.op("vector", lambda v, t=t: v.tensor_tensor(C("scr_s", 0, t), C("st_m", 0, t),
                  C("st_m", 0, t), OP.mult), after=[c1], name=f"xm2{t}", small=True)
        c3 = p.op("gpsimd", lambda g, t=t, eps=eps: g.tensor_scalar(C("scr_s", 64, t),
                  C("st_ssq", 0, t), 1.0 / D, eps, OP.mult, OP.add),
                  after=[oa, c1], name=f"xv1{t}", small=True)
        c4 = p.op("vector", lambda v, t=t: v.tensor_tensor(C("scr_s", 64, t), C("scr_s", 64, t),
                  C("scr_s", 0, t), OP.subtract), after=[c2, c3], name=f"xv2{t}", small=True)
        c5 = p.op("scalar", lambda sc, t=t: sc.activation(C("scr_s", 0, t), C("scr_s", 64, t),
                  AF.Ln), after=[c4], name=f"xln{t}", small=True)
        c6 = p.op("scalar", lambda sc, t=t: sc.activation(C("st_rstd", 0, t), C("scr_s", 0, t),
                  AF.Exp, scale=-0.5), after=[c5], name=f"xrstd{t}", small=True)
        if quant:
            c7 = p.op("vector", lambda v, t=t: v.tensor_tensor(C("scr_s", 128, t),
                      C("st_max", 0, t), C("st_m", 0, t), OP.subtract),
                      after=[c1], name=f"xd1{t}", small=True)
            c8 = p.op("gpsimd", lambda g, t=t: g.tensor_tensor(C("scr_s", 192, t),
                      C("st_m", 0, t), C("st_min", 0, t), OP.subtract),
                      after=[c1], name=f"xd2{t}", small=True)
            c9 = p.op("vector", lambda v, t=t: v.tensor_tensor(C("scr_s", 128, t),
                      C("scr_s", 128, t), C("scr_s", 192, t), OP.max),
                      after=[c7, c8], name=f"xd3{t}", small=True)
            ca = p.op("gpsimd", lambda g, t=t: g.tensor_scalar(C("scr_s", 128, t),
                      C("scr_s", 128, t), 1e-5, None, OP.max),
                      after=[c9], name=f"xd4{t}", small=True)
            cb = p.op("scalar", lambda sc, t=t: sc.activation(C("scr_s", 192, t),
                      C("scr_s", 128, t), AF.Ln), after=[ca, c6], name=f"xgl{t}", small=True)
            cc = p.op("scalar", lambda sc, t=t: sc.activation(C("scr_s", 192, t),
                      C("scr_s", 192, t), AF.Exp, scale=-1.0),
                      after=[cb], name=f"xge{t}", small=True)
            cd = p.op("vector", lambda v, t=t: v.tensor_scalar(C("st_g", 0, t),
                      C("scr_s", 192, t), 127.0, None, OP.mult),
                      after=[cc], name=f"xg{t}", small=True)
            ce = p.op("gpsimd", lambda g, t=t: g.tensor_tensor(C("scr_s", 192, t),
                      C("st_rstd", 0, t), C("scr_s", 128, t), OP.mult),
                      after=[c6, ca, cd], name=f"xi1{t}", small=True)
            cf = p.op("gpsimd", lambda g, t=t: g.tensor_scalar(C("st_iso", 0, t),
                      C("scr_s", 192, t), 1.0 / 127.0, None, OP.mult),
                      after=[ce], name=f"xiso{t}", small=True)
            cg = p.op("vector", lambda v, t=t: v.tensor_tensor(C("scr_s", 0, t),
                      C("st_m", 0, t), C("st_g", 0, t), OP.mult),
                      after=[cd], name=f"xb1{t}", small=True)
            ch = p.op("vector", lambda v, t=t: v.tensor_scalar(C("st_bg", 0, t),
                      C("scr_s", 0, t), -1.0, None, OP.mult),
                      after=[cg, cf], name=f"xbg{t}", small=True)
        else:
            cd = p.op("vector", lambda v, t=t: v.tensor_copy(C("st_g", 0, t), C("st_rstd", 0, t)),
                      after=[c6], name=f"xg{t}", small=True)
            cg = p.op("gpsimd", lambda g, t=t: g.tensor_tensor(C("scr_s", 0, t),
                      C("st_m", 0, t), C("st_rstd", 0, t), OP.mult),
                      after=[c6], name=f"xb1{t}", small=True)
            ch = p.op("vector", lambda v, t=t: v.tensor_scalar(C("st_bg", 0, t),
                      C("scr_s", 0, t), -1.0, None, OP.mult),
                      after=[cg, cd], name=f"xbg{t}", small=True)
        a1 = p.op("scalar", lambda sc, t=t, sl=sl: sc.activation(
            sb["scr_sq"][:, 0:1024], xs[:, sl, :], AF.Identity,
            bias=C("st_bg", 0, t), scale=C("st_g", 0, t)),
            after=[ch, ld], name=f"xq_a1_{t}")
        slot_free[sl] = a1
        if quant:
            r1 = p.op("vector", lambda v: v.tensor_scalar(
                sb["scr_sq"][:, 0:1024], sb["scr_sq"][:, 0:1024], MAGIC, MAGIC,
                OP.add, OP.subtract), after=[a1], name=f"xq_rnd{t}")
            c1b = p.op("scalar", lambda sc, t=t: sc.activation(
                sb["xq_bf"][:], sb["scr_sq"][:, 0:1024], AF.Copy, scale=C("st_iso", 0, t)),
                after=[r1, prev_tp], name=f"xq_c1_{t}")
        else:
            c1b = p.op("vector", lambda v: v.tensor_copy(sb["xq_bf"][:], sb["scr_sq"][:, 0:1024]),
                      after=[a1, prev_tp], name=f"xq_c1_{t}")
        prev_c1b = c1b
        tp = p.op("tensor", lambda te, t=t: [te.transpose(
            ptp[:, dc, :], sb["xq_bf"][:, 128 * dc:128 * (dc + 1)], sb["c_identbf"][:])
            for dc in range(ND)][-1], after=[c1b, prev_evac, cident], name=f"xq_tp{t}")
        ev = p.op("vector", lambda v, t=t: v.tensor_copy(
            sb["xt"][:, :, 1 + 128 * t: 1 + 128 * (t + 1)], ptp[:, :, :]),
            after=[tp], name=f"xq_ev{t}")
        prev_tp, prev_evac = tp, ev
        evac_list.append(ev)
    return prev_evac, evac_list


def emit_stats_factors(p, sb, x_ap):
    xs = sb["xbuf"]
    slot_free = [None, None]
    last = None
    for t in range(NT):
        sl = t % 2
        C = lambda buf, off, t: sb[buf][:, off + 4 * t:off + 4 * t + 1]
        ld = p.dma(f"xs{sl}", lambda e, t=t, sl=sl: e.dma_start(
            out=xs[:, sl, :], in_=x_ap[128 * t:128 * (t + 1), :]),
            after=(slot_free[sl] or []), name=f"sf_ld{t}")
        oa = p.op("scalar", lambda s, sl=sl, t=t: s.activation(
            sb["scr_sq"][:, 0:1024], xs[:, sl, :], AF.Square,
            accum_out=C("st_ssq", 0, t)), after=[ld], name=f"sf_ssq{t}")
        od = p.op("vector", lambda v, t=t, sl=sl: v.tensor_reduce(
            C("st_sum", 0, t), xs[:, sl, :], AX.X, OP.add),
            after=[ld], name=f"sf_red{t}", small=True)
        c1 = p.op("gpsimd", lambda g, t=t: g.tensor_scalar(C("st_m", 0, t), C("st_sum", 0, t),
                  1.0 / D, None, OP.mult), after=[od], name=f"sf_m{t}", small=True)
        c2 = p.op("vector", lambda v, t=t: v.tensor_tensor(C("scr_s", 0, t), C("st_m", 0, t),
                  C("st_m", 0, t), OP.mult), after=[c1], name=f"sf_m2{t}", small=True)
        c3 = p.op("gpsimd", lambda g, t=t: g.tensor_scalar(C("scr_s", 64, t),
                  C("st_ssq", 0, t), 1.0 / D, 1e-5, OP.mult, OP.add),
                  after=[oa, c1], name=f"sf_v1{t}", small=True)
        c4 = p.op("vector", lambda v, t=t: v.tensor_tensor(C("scr_s", 64, t), C("scr_s", 64, t),
                  C("scr_s", 0, t), OP.subtract), after=[c2, c3], name=f"sf_v2{t}", small=True)
        c5 = p.op("scalar", lambda sc, t=t: sc.activation(C("scr_s", 0, t), C("scr_s", 64, t),
                  AF.Ln), after=[c4], name=f"sf_ln{t}", small=True)
        c6 = p.op("scalar", lambda sc, t=t: sc.activation(C("st_g", 0, t), C("scr_s", 0, t),
                  AF.Exp, scale=-0.5), after=[c5], name=f"sf_rstd{t}", small=True)
        cg = p.op("gpsimd", lambda g, t=t: g.tensor_tensor(C("scr_s", 0, t),
                  C("st_m", 0, t), C("st_g", 0, t), OP.mult),
                  after=[c6], name=f"sf_b1{t}", small=True)
        ch = p.op("vector", lambda v, t=t: v.tensor_scalar(C("st_bg", 0, t),
                  C("scr_s", 0, t), -1.0, None, OP.mult),
                  after=[cg], name=f"sf_bg{t}", small=True)
        slot_free[sl] = [oa, od]
        last = ch
    return last



def emit_ffn0(nc, p, sbc, cevs, aps, tag, debug=False, shared=None):
    x_ap, y_ap = aps["x"], aps["y"]
    w1T_ap, w2T_ap, cw_ap, cb_ap = aps["w1T"], aps["w2T"], aps["cw"], aps["cb"]
    dbg = {}
    if debug:
        dbg["d_sc1"] = nc.dram_tensor("d_sc1", [1, 64], F32, kind="ExternalOutput").ap()
        dbg["d_g"] = nc.dram_tensor("d_g", [128, 64], F32, kind="ExternalOutput").ap()
        dbg["d_bg"] = nc.dram_tensor("d_bg", [128, 64], F32, kind="ExternalOutput").ap()
        dbg["d_iso"] = nc.dram_tensor("d_iso", [128, 64], F32, kind="ExternalOutput").ap()
        dbg["d_xt"] = nc.dram_tensor("d_xt", [128, ND, 2050], BF16, kind="ExternalOutput").ap()
        dbg["d_w1q"] = nc.dram_tensor("d_w1q", [D, DH], BF16, kind="ExternalOutput").ap()
        dbg["d_w2q"] = nc.dram_tensor("d_w2q", [DH, D], BF16, kind="ExternalOutput").ap()
        dbg["d_sil"] = nc.dram_tensor("d_sil", [128, NH, 512], F16, kind="ExternalOutput").ap()
        dbg["d_sq2"] = nc.dram_tensor("d_sq2", [128, 512], F32, kind="ExternalOutput").ap()
        dbg["d_iso2"] = nc.dram_tensor("d_iso2", [128, 512], F32, kind="ExternalOutput").ap()
        dbg["d_taps"] = nc.dram_tensor("d_taps", [128, 96], F32, kind="ExternalOutput").ap()
    if shared is None:
        w1q_dram = nc.dram_tensor(tag + "w1q_scr", [D, DH], BF16).ap()
        w2q_dram = nc.dram_tensor(tag + "w2q_scr", [DH, D], BF16).ap()
        sc1_dram = nc.dram_tensor(tag + "sc1_scr", [1, 64], F32).ap()
    else:
        w1q_dram = shared["w1q"]
        w2q_dram = shared["w2q"]
        sc1_dram = shared["sc1"]

    stack = contextlib.ExitStack()
    E = stack.enter_context
    sb = dict(sbc)
    sb["xbuf"] = E(nc.sbuf_tensor(tag + "xbuf", [128, 2, 1024], F32))
    sb["scr_a"] = E(nc.sbuf_tensor(tag + "scr_a", [128, 4096], F32))
    sb["wsdump"] = E(nc.sbuf_tensor(tag + "wsdump", [128, 2048], BF16))
    sb["scr_sq"] = E(nc.sbuf_tensor(tag + "scr_sq", [128, 1024], F32))
    sb["scr_s"] = E(nc.sbuf_tensor(tag + "scr_s", [128, 256], F32))
    sb["xq_bf"] = E(nc.sbuf_tensor(tag + "xq_bf", [128, 1024], BF16))
    for n in ("st_sum", "st_ssq", "st_max", "st_min", "st_m", "st_rstd",
              "st_g", "st_bg", "st_iso"):
        sb[n] = E(nc.sbuf_tensor(tag + n, [128, 64], F32))
    sb["xt"] = E(nc.sbuf_tensor(tag + "xt", [128, ND, 2050], BF16))
    sb["w2q"] = E(nc.sbuf_tensor(tag + "w2q", [128, NH, 1024], BF16))
    sb["ring1"] = E(nc.sbuf_tensor(tag + "ring1", [128, 4, ND, 128], BF16))
    sb["sil"] = E(nc.sbuf_tensor(tag + "sil", [128, NH, 512], F16))
    sb["runM"] = E(nc.sbuf_tensor(tag + "runM", [128, 512], F32))
    sb["runL"] = E(nc.sbuf_tensor(tag + "runL", [128, 512], F32))
    sb["convu"] = E(nc.sbuf_tensor(tag + "convu", [128, 2, 512], F32))
    sb["qs_t"] = E(nc.sbuf_tensor(tag + "qs_t", [128, 512], F32))
    sb["taps"] = E(nc.sbuf_tensor(tag + "taps", [128, 96], F32))
    sb["cb_sb"] = E(nc.sbuf_tensor(tag + "cb_sb", [128, 32], F32))
    sb["wabs"] = E(nc.sbuf_tensor(tag + "wabs", [128, 256], F32))
    sb["wqbf"] = E(nc.sbuf_tensor(tag + "wqbf", [128, 4096], BF16))
    sb["sc1"] = E(nc.sbuf_tensor(tag + "sc1", [1, 64], F32))
    sb["sccol"] = E(nc.sbuf_tensor(tag + "sccol", [128, 20], F32))
    sb["sqT"] = E(nc.sbuf_tensor(tag + "sqT", [4, 256], F32))
    sb["sq2rep"] = E(nc.sbuf_tensor(tag + "sq2rep", [128, 512], F32))
    sb["iso2rep"] = E(nc.sbuf_tensor(tag + "iso2rep", [128, 512], F32))
    sb["mt_tok"] = E(nc.sbuf_tensor(tag + "mt_tok", [128, 32], F32))

    psum = {
        "hp": E(nc.psum_tensor(tag + "hp", [128, 2, 1024], F32)),
        "out2": E(nc.psum_tensor(tag + "out2", [128, 1024], F32)),
        "ptp": E(nc.psum_tensor(tag + "ptp", [128, ND, 128], BF16)),
        "pbr": E(nc.psum_tensor(tag + "pbr", [128, 512], F32)),
    }

    pbr_user = None
    ptp_user = None

    xt_done, xt_evs = emit_xphase(p, sb, psum, x_ap, quant=True, cident=cevs[0])
    ptp_user = xt_done

    prev = None
    slotu = [None, None]
    for i in range(16 if shared is None else 0):
        sl = i % 2
        ld = p.dma(f"wt{sl}", lambda e, i=i, sl=sl: e.dma_start(
            out=sb["scr_a"][:, 2048 * sl:2048 * (sl + 1)],
            in_=w1T_ap[128 * (i // 2):128 * (i // 2 + 1), 2048 * (i % 2):2048 * (i % 2 + 1)]),
            after=[slotu[sl]], name=f"w1s_ld{i}")
        prev = p.op("scalar", lambda a, i=i, sl=sl: a.activation(
            sb["wsdump"][:], sb["scr_a"][:, 2048 * sl:2048 * (sl + 1)], AF.Abs,
            accum_out=sb["wabs"][:, 4 * i:4 * i + 1]), after=[ld], name=f"w1s_abs{i}")
        slotu[sl] = prev
    for i in range(32 if shared is None else 0):
        sl = i % 2
        ld = p.dma(f"wt{sl}", lambda e, i=i, sl=sl: e.dma_start(
            out=sb["scr_a"][:, 2048 * sl:2048 * sl + 1024], in_=w2T_ap[128 * i:128 * (i + 1), :]),
            after=[slotu[sl]], name=f"w2s_ld{i}")
        prev = p.op("scalar", lambda a, i=i, sl=sl: a.activation(
            sb["wsdump"][:, 0:1024], sb["scr_a"][:, 2048 * sl:2048 * sl + 1024], AF.Abs,
            accum_out=sb["wabs"][:, 64 + 4 * i:65 + 4 * i]), after=[ld], name=f"w2s_abs{i}")
        slotu[sl] = prev

    if shared is not None:
        ld_sc1 = p.dma("sc1ld", lambda e: e.dma_start(out=sb["sc1"][:], in_=sc1_dram),
                       name=tag + "ld_sc1")
        w6 = w8 = ld_sc1
        o_mean = ld_sc1
        w1q_done = None
        wq_done = None
    def wtot(v):
        w1v = sb["wabs"][:, 0:64].rearrange("p (t u) -> p t u", u=4)[:, :, 0:1]
        w2v = sb["wabs"][:, 64:192].rearrange("p (t u) -> p t u", u=4)[:, :, 0:1]
        v.tensor_reduce(sb["wabs"][:, 208:209], w2v, AX.XY, OP.add)
        return v.tensor_reduce(sb["wabs"][:, 192:193], w1v, AX.XY, OP.add)
    if shared is None:
        o_tot = p.op("vector", wtot, after=[prev], name="wtot", small=True)

        def wtp(te):
            te.transpose(psum["pbr"][0:1, 0:128], sb["wabs"][:, 192:193], sb["c_identf"][:])
            return te.transpose(psum["pbr"][0:1, 128:256], sb["wabs"][:, 208:209], sb["c_identf"][:])
        o_tp = p.op("tensor", wtp, after=[o_tot, cevs[1], pbr_user], name="wtot_tp")

        w1r = p.op("vector", lambda v: v.tensor_reduce(
            sb["sc1"][0:1, 0:1], psum["pbr"][0:1, 0:128], AX.X, OP.add),
            after=[o_tp], name="wm_r1", small=True)
        w2r = p.op("vector", lambda v: v.tensor_reduce(
            sb["sc1"][0:1, 16:17], psum["pbr"][0:1, 128:256], AX.X, OP.add),
            after=[o_tp], name="wm_r2", small=True)
        w3 = p.op("gpsimd", lambda g: g.tensor_scalar(
            sb["sc1"][0:1, 4:5], sb["sc1"][0:1, 0:1], 1.0 / (D * DH), 1e-5, OP.mult, OP.max),
            after=[w1r], name="wm_mA", small=True)
        w4 = p.op("gpsimd", lambda g: g.tensor_scalar(
            sb["sc1"][0:1, 20:21], sb["sc1"][0:1, 16:17], 1.0 / (D * DH), 1e-5, OP.mult, OP.max),
            after=[w2r], name="wm_mB", small=True)
        w5 = p.op("scalar", lambda sc: sc.activation(sb["sc1"][0:1, 8:9], sb["sc1"][0:1, 4:5], AF.Ln),
                  after=[w3], name="wm_lnA", small=True)
        w6 = p.op("scalar", lambda sc: sc.activation(sb["sc1"][0:1, 12:13], sb["sc1"][0:1, 8:9],
                                                     AF.Exp, scale=-1.0),
                  after=[w5], name="wm_sA", small=True)
        w7 = p.op("scalar", lambda sc: sc.activation(sb["sc1"][0:1, 24:25], sb["sc1"][0:1, 20:21], AF.Ln),
                  after=[w4, w6], name="wm_lnB", small=True)
        w8 = p.op("scalar", lambda sc: sc.activation(sb["sc1"][0:1, 28:29], sb["sc1"][0:1, 24:25],
                                                     AF.Exp, scale=-1.0),
                  after=[w7], name="wm_sB", small=True)
        o_mean = p.barrier("vector", [w6, w8], name="wm_done")
        pbr_user = o_tp
        st_sc1 = p.dma("sc1st", lambda e: e.dma_start(out=sc1_dram, in_=sb["sc1"][:]),
                       after=[w6, w8], name=tag + "st_sc1")

    bco = []
    for j, off in enumerate([12, 28, 4, 20]):
        mm = p.op("tensor", lambda te, off=off: te.matmul(
            psum["pbr"][:, 0:1], sb["c_ones1f"][:, :], sb["sc1"][0:1, off:off + 1],
            start=True, stop=True), after=[w6, w8, pbr_user, cevs[3]], name=f"bc{j}_mm")
        ev = p.op("vector", lambda v, j=j: v.tensor_copy(
            sb["sccol"][:, 4 * j:4 * j + 1], psum["pbr"][:, 0:1]), after=[mm],
            name=f"bc{j}_ev", small=True)
        pbr_user = ev
        bco.append(ev)
    o_half = p.op("gpsimd", lambda g: g.tensor_scalar(
        sb["sccol"][:, 16:17], sb["sccol"][:, 12:13], 0.5, None, OP.mult),
        after=[bco[3]], name="m2half", small=True)

    ld_cw = p.dma("cwld", lambda e: e.dma_start(out=sb["taps"][:], in_=cw_ap), name="ld_cw")
    ld_cb = p.dma("cbld", lambda e: e.dma_start(out=sb["cb_sb"][:], in_=cb_ap), name="ld_cb")
    o_taps = p.op("vector", lambda v: v.tensor_scalar(
        sb["taps"][:], sb["taps"][:], sb["sccol"][:, 8:9], None, OP.mult),
        after=[ld_cw, bco[2]], name="taps")

    prev_ld = prev
    if shared is None:
        w1q_done = None
    slotq = [o_mean, o_mean]
    slotw = [None, None]
    for wi, (n_chunks, src, dst, width, scol) in enumerate(
            [(16, w1T_ap, w1q_dram, 2048, 0), (32, w2T_ap, w2q_dram, 1024, 1)]
            if shared is None else []):
        for i in range(n_chunks):
            sl = i % 2
            if wi == 0:
                srcap = src[128 * (i // 2):128 * (i // 2 + 1), 2048 * (i % 2):2048 * (i % 2 + 1)]
            else:
                srcap = src[128 * i:128 * (i + 1), :]
            ld = p.dma(f"wt{sl}", lambda e, srcap=srcap, width=width, sl=sl: e.dma_start(
                out=sb["scr_a"][:, 2048 * sl:2048 * sl + width], in_=srcap),
                after=[slotq[sl], o_mean], name=f"wq_ld{wi}_{i}")
            a = p.op("scalar", lambda s, width=width, scol=scol, sl=sl: s.activation(
                sb["scr_a"][:, 2048 * sl:2048 * sl + width],
                sb["scr_a"][:, 2048 * sl:2048 * sl + width], AF.Copy,
                scale=sb["sccol"][:, 4 * scol:4 * scol + 1]),
                after=[ld, bco[scol]], name=f"wq_sc{wi}_{i}")
            r = p.op("vector", lambda v, width=width, sl=sl: v.tensor_scalar(
                sb["scr_a"][:, 2048 * sl:2048 * sl + width],
                sb["scr_a"][:, 2048 * sl:2048 * sl + width], MAGIC, MAGIC,
                OP.add, OP.subtract), after=[a], name=f"wq_rnd{wi}_{i}")
            c = p.op("vector", lambda v, width=width, sl=sl: v.tensor_scalar(
                sb["wqbf"][:, 2048 * sl:2048 * sl + width],
                sb["scr_a"][:, 2048 * sl:2048 * sl + width], 1.0, -1.0,
                OP.min, OP.max), after=[r, slotw[sl]], name=f"wq_clip{wi}_{i}")
            slotq[sl] = c
            if wi == 0:
                dstap = dst[128 * (i // 2):128 * (i // 2 + 1), 2048 * (i % 2):2048 * (i % 2 + 1)]
            else:
                dstap = dst[128 * i:128 * (i + 1), :]
            st = p.dma(f"wqs{sl}", lambda e, dstap=dstap, width=width, sl=sl: e.dma_start(
                out=dstap, in_=sb["wqbf"][:, 2048 * sl:2048 * sl + width]),
                after=[c], name=f"wq_st{wi}_{i}")
            slotw[sl] = st
            if wi == 0 and i == n_chunks - 1:
                w1q_done = p.barrier("sync", [slotw[0], slotw[1]], name=tag + "w1qd")
    if shared is None:
        wq_done = p.barrier("sync", [slotw[0], slotw[1]], name=tag + "wqd")

    ld_w2q = None
    for part in range(4):
        ld_w2q = p.dma("w2qld", lambda e, part=part: e.dma_start(
            out=sb["w2q"][:, 8 * part:8 * (part + 1), :],
            in_=w2q_dram.rearrange("(hc p) d -> p hc d", p=128)[:, 8 * part:8 * (part + 1), :]),
            after=[wq_done], name=f"ld_w2q{part}")


    y_stores = []
    ring_user = [None] * 4
    hp_user = [None, None]
    conv_user = [None, None]
    sil_user = [None] * NH
    out2_user = None
    o1_prev = None
    prev_ep = [xt_done, xt_done]
    for q in range(4):
        c0 = 512 * q
        def mzf(v):
            v.memset(sb["runM"][:], 0.0)
            return v.memset(sb["runL"][:], 0.0)
        mz = p.op("gpsimd", mzf, after=[o1_prev], name=f"runM_z{q}")
        conv_ops = []
        for hc in range(NH):
            r = hc % 4
            ld = p.dma(f"r1s{r}", lambda e, hc=hc, r=r: e.dma_start(
                out=sb["ring1"][:, r, :, :],
                in_=w1q_dram.rearrange("(dc p) (hc j) -> p dc hc j", p=128, j=128)[:, :, hc, :]),
                after=[ring_user[r], w1q_done], name=f"r1_ld{q}_{hc}")
            b = hc % 2

            def mm1(te, hc=hc, r=r, b=b, c0=c0):
                for dc in range(ND):
                    te.matmul(psum["hp"][:, b, 0:512], sb["ring1"][:, r, dc, :],
                              sb["xt"][:, dc, c0:c0 + 512],
                              start=(dc == 0), stop=(dc == ND - 1))
                for dc in range(ND):
                    mm = te.matmul(psum["hp"][:, b, 512:514], sb["ring1"][:, r, dc, :],
                                   sb["xt"][:, dc, c0 + 512:c0 + 514],
                                   start=(dc == 0), stop=(dc == ND - 1))
                return mm
            mm = p.op("tensor", mm1, after=[ld, xt_evs[min(NT - 1, 4 * q + 4)], hp_user[b]],
                      name=f"mm1_{q}_{hc}")
            ring_user[r] = mm

            cs_ = hc % 2
            cva = p.op("scalar", lambda sc, hc=hc, b=b, cs_=cs_: sc.activation(
                sb["convu"][:, cs_, :], psum["hp"][:, b, 0:512], AF.Copy,
                scale=sb["taps"][:, 3 * hc:3 * hc + 1]),
                after=[mm, o_taps, conv_user[cs_]], name=f"cva{q}_{hc}")

            def conv(v, hc=hc, b=b, cs_=cs_):
                v.scalar_tensor_tensor(sb["convu"][:, cs_, :], psum["hp"][:, b, 1:513],
                                       sb["taps"][:, 3 * hc + 1:3 * hc + 2],
                                       sb["convu"][:, cs_, :], OP.mult, OP.add)
                return v.scalar_tensor_tensor(sb["convu"][:, cs_, :], psum["hp"][:, b, 2:514],
                                              sb["taps"][:, 3 * hc + 2:3 * hc + 3],
                                              sb["convu"][:, cs_, :], OP.mult, OP.add)
            cv = p.op("vector", conv, after=[cva], name=f"conv{q}_{hc}")
            hp_user[b] = cv

            si = p.op("scalar", lambda s, hc=hc: s.activation(
                sb["sil"][:, hc, :], sb["convu"][:, hc % 2, :], AF.Silu,
                bias=sb["cb_sb"][:, hc:hc + 1], scale=1.0),
                after=[cv, ld_cb, sil_user[hc]], name=f"silu{q}_{hc}")
            conv_user[hc % 2] = si

            def rmf(v, hc=hc):
                v.tensor_tensor(sb["runM"][:], sb["sil"][:, hc, :], sb["runM"][:], OP.max)
                return v.tensor_tensor(sb["runL"][:], sb["sil"][:, hc, :], sb["runL"][:], OP.min)
            rm = p.op("vector", rmf, after=[si, mz], name=f"rmax{q}_{hc}")
            conv_ops.append(rm)

        cmb = p.op("vector", lambda v: [v.tensor_scalar(sb["runL"][:], sb["runL"][:], -1.0, None, OP.mult),
                    v.tensor_tensor(sb["runM"][:], sb["runM"][:], sb["runL"][:], OP.max)][-1],
                    after=[conv_ops[-1]], name=f"rcomb{q}")

        def tpm_f(te, q=q):
            for j in range(4):
                mmi = te.transpose(psum["pbr"][:, 128 * j:128 * (j + 1)],
                                   sb["runM"][:, 128 * j:128 * (j + 1)], sb["c_identf"][:])
            return mmi
        o1 = p.op("tensor", tpm_f, after=[cmb, pbr_user], name=f"mtp{q}")
        o1_prev = o1

        mview = sb["mt_tok"][:, 0:16].rearrange("p (t u) -> p t u", u=4)[:, :, 0:1]
        lview = sb["mt_tok"][:, 16:32].rearrange("p (t u) -> p t u", u=4)[:, :, 0:1]
        def mredr(v, q=q):
            for j in range(4):
                r = v.tensor_reduce(sb["mt_tok"][:, 4 * j:4 * j + 1],
                                    psum["pbr"][:, 128 * j:128 * (j + 1)],
                                    AX.X, OP.max, apply_absolute_value=True)
            return r
        o2a = p.op("vector", mredr, after=[o1], name=f"mred{q}", small=True)
        o2b = p.op("gpsimd", lambda g: g.tensor_scalar(mview, mview, 1e-5, None, OP.max),
                   after=[o2a], name=f"mclip{q}", small=True)
        o2c = p.op("scalar", lambda sc: sc.activation(lview, mview, AF.Ln),
                   after=[o2b], name=f"mln{q}", small=True)
        o2d = p.op("scalar", lambda sc: sc.activation(lview, lview, AF.Exp, scale=-1.0),
                   after=[o2c], name=f"mexp{q}", small=True)
        o2e = p.op("vector", lambda v: v.tensor_scalar(lview, lview, 127.0, None, OP.mult),
                   after=[o2d], name=f"msq{q}", small=True)
        o2 = p.op("gpsimd", lambda g: g.tensor_scalar(mview, mview, 1.0 / 127.0, None, OP.mult),
                  after=[o2b, o2c], name=f"miso{q}", small=True)
        pbr_user = o2a

        def mtp2(te):
            te.transpose(psum["pbr"][0:4, 0:128],
                         sb["mt_tok"][:, 0:16].rearrange("p (t u) -> p t u", u=4)[:, :, 0:1].rearrange("p t u -> p (t u)"),
                         sb["c_identf"][:])
            return te.transpose(psum["pbr"][0:4, 128:256],
                                sb["mt_tok"][:, 16:32].rearrange("p (t u) -> p t u", u=4)[:, :, 0:1].rearrange("p t u -> p (t u)"),
                                sb["c_identf"][:])
        o3 = p.op("tensor", mtp2, after=[o2, o2e, pbr_user], name=f"mtp2{q}")
        o4 = p.op("vector", lambda v: v.tensor_copy(sb["sqT"][:, :], psum["pbr"][0:4, 0:256]),
                  after=[o3], name=f"sqT{q}")
        pbr_user = o4
        o5 = p.op("tensor", lambda te: [te.matmul(
            psum["pbr"][:, 128 * j:128 * (j + 1)], sb["c_selsf4"][0:4, j, :],
            sb["sqT"][0:4, 128:256], start=True, stop=True) for j in range(4)][-1],
            after=[o4, pbr_user, cevs[5]], name=f"sqrep_mm{q}")
        o6 = p.op("vector", lambda v: v.tensor_copy(sb["sq2rep"][:], psum["pbr"][:]),
                  after=[o5], name=f"sqrep_ev{q}")
        pbr_user = o6
        o7 = p.op("tensor", lambda te: [te.matmul(
            psum["pbr"][:, 128 * j:128 * (j + 1)], sb["c_selsf4"][0:4, j, :],
            sb["sqT"][0:4, 0:128], start=True, stop=True) for j in range(4)][-1],
            after=[o6, pbr_user], name=f"isorep_mm{q}")
        o8 = p.op("vector", lambda v: v.tensor_copy(sb["iso2rep"][:], psum["pbr"][:]),
                  after=[o7], name=f"isorep_ev{q}")
        pbr_user = o8

        qops = []
        qprev = o8
        for hc in range(NH):
            def quant(v, hc=hc):
                v.tensor_tensor(sb["qs_t"][:], sb["sil"][:, hc, :], sb["sq2rep"][:], OP.mult)
                v.tensor_scalar(sb["qs_t"][:], sb["qs_t"][:], MAGIC, MAGIC, OP.add, OP.subtract)
                return v.tensor_tensor(sb["sil"][:, hc, :], sb["qs_t"][:], sb["iso2rep"][:], OP.mult)
            qo = p.op("vector", quant, after=[qprev, conv_ops[hc]], name=f"quant{q}_{hc}")
            qprev = qo
            qops.append(qo)

        m2_last = None
        for tt in range(4):
            gt = 4 * q + tt

            def mm2(te, tt=tt):
                for piece in range(2):
                    for hc in range(NH):
                        mm = te.matmul(psum["out2"][:, 512 * piece:512 * (piece + 1)],
                                       sb["sil"][:, hc, 128 * tt:128 * (tt + 1)],
                                       sb["w2q"][:, hc, 512 * piece:512 * (piece + 1)],
                                       start=(hc == 0), stop=(hc == NH - 1))
                return mm
            m2 = p.op("tensor", mm2, after=[qops[-1], ld_w2q, out2_user], name=f"mm2_{q}_{tt}")
            m2_last = m2
            sl = tt % 2
            lx = p.dma(f"xe{sl}", lambda e, gt=gt, sl=sl: e.dma_start(
                out=sb["xbuf"][:, sl, :], in_=x_ap[128 * gt:128 * (gt + 1), :]),
                after=[prev_ep[0], prev_ep[1]], name=f"xe_ld{gt}")
            ep = p.op("vector", lambda v, sl=sl: v.scalar_tensor_tensor(
                sb["xbuf"][:, 1 - sl, :], psum["out2"][:, :], sb["sccol"][:, 16:17],
                sb["xbuf"][:, sl, :], OP.mult, OP.add),
                after=[m2, lx, o_half, prev_ep[0], prev_ep[1]], name=f"ep{gt}")
            out2_user = ep
            st = p.dma(f"yst{sl}", lambda e, gt=gt, sl=sl: e.dma_start(
                out=y_ap[128 * gt:128 * (gt + 1), :], in_=sb["xbuf"][:, 1 - sl, :]),
                after=[ep], name=f"y_st{gt}")
            prev_ep[sl] = st
            y_stores.append(st)
        for hc in range(NH):
            sil_user[hc] = m2_last

    if debug:
        dstores = []
        dstores.append(p.dma("dbg", lambda e: e.dma_start(out=dbg["d_sc1"], in_=sb["sc1"][:]), after=[o_mean], name="d_sc1"))
        dstores.append(p.dma("dbg", lambda e: e.dma_start(out=dbg["d_g"], in_=sb["st_g"][:]), after=[fac], name="d_g"))
        dstores.append(p.dma("dbg", lambda e: e.dma_start(out=dbg["d_bg"], in_=sb["st_bg"][:]), after=[fac], name="d_bg"))
        dstores.append(p.dma("dbg", lambda e: e.dma_start(out=dbg["d_iso"], in_=sb["st_iso"][:]), after=[fac], name="d_iso"))
        dstores.append(p.dma("dbg", lambda e: e.dma_start(out=dbg["d_xt"], in_=sb["xt"][:]), after=[xt_done], name="d_xt"))
        dstores.append(p.dma("dbg", lambda e: e.dma_start(out=dbg["d_w1q"], in_=w1q_dram), after=[wq_done], name="d_w1q"))
        dstores.append(p.dma("dbg", lambda e: e.dma_start(out=dbg["d_w2q"], in_=w2q_dram), after=[wq_done], name="d_w2q"))
        dstores.append(p.dma("dbg", lambda e: e.dma_start(out=dbg["d_sil"], in_=sb["sil"][:]), after=[y_stores[-1]], name="d_sil"))
        dstores.append(p.dma("dbg", lambda e: e.dma_start(out=dbg["d_sq2"], in_=sb["sq2rep"][:]), after=[y_stores[-1]], name="d_sq2"))
        dstores.append(p.dma("dbg", lambda e: e.dma_start(out=dbg["d_iso2"], in_=sb["iso2rep"][:]), after=[y_stores[-1]], name="d_iso2"))
        dstores.append(p.dma("dbg", lambda e: e.dma_start(out=dbg["d_taps"], in_=sb["taps"][:]), after=[o_taps], name="d_taps"))
        y_stores = y_stores + dstores
    p.barrier("sync", y_stores, name=tag + "final")
    stack.close()
    p.seal(name=tag + "seal")

def build_ffn0(debug=False):
    nc = bass.Bass("TRN2", target_bir_lowering=False)
    aps = {}
    aps["x"] = nc.dram_tensor("x", [SEQ, D], F32, kind="ExternalInput").ap()
    aps["y"] = nc.dram_tensor("y", [SEQ, D], F32, kind="ExternalOutput").ap()
    aps["w1T"] = nc.dram_tensor("w1T", [D, DH], F32, kind="ExternalInput").ap()
    aps["w2T"] = nc.dram_tensor("w2T", [DH, D], F32, kind="ExternalInput").ap()
    aps["cw"] = nc.dram_tensor("cw", [128, 96], F32, kind="ExternalInput").ap()
    aps["cb"] = nc.dram_tensor("cb", [128, 32], F32, kind="ExternalInput").ap()
    cs = declare_consts(nc)
    outer = contextlib.ExitStack()
    sbc = alloc_consts(nc, outer)
    p = Prog()
    cevs = load_consts(p, cs, sbc)
    emit_ffn0(nc, p, sbc, cevs, aps, "f0_", debug=debug)
    p.emit(nc, outer)
    return nc




def emit_attn0(nc, p, sbc, cevs, aps, tag, debug=False):
    x_ap, y_ap = aps["x"], aps["y"]
    wq_ap, wk_ap, wv_ap, lam_ap = aps["wq12"], aps["wk12"], aps["wv"], aps["lam4"]
    wqd = nc.dram_tensor(tag + "wq_scr", [D, 2 * D], BF16).ap()
    wkd = nc.dram_tensor(tag + "wk_scr", [D, 2 * D], BF16).ap()

    SCALE = float(D ** (-0.25))
    stack = contextlib.ExitStack()
    E = stack.enter_context
    sb = dict(sbc)
    sb["xbuf"] = E(nc.sbuf_tensor(tag + "xbuf", [128, 2, 1024], F32))
    sb["ybuf"] = E(nc.sbuf_tensor(tag + "ybuf", [128, 2, 1024], F32))
    sb["scr_sq"] = E(nc.sbuf_tensor(tag + "scr_sq", [128, 1024], F32))
    sb["scr_s"] = E(nc.sbuf_tensor(tag + "scr_s", [128, 256], F32))
    sb["xq_bf"] = E(nc.sbuf_tensor(tag + "xq_bf", [128, 1024], BF16))
    for n in ("st_sum", "st_ssq", "st_max", "st_min", "st_m", "st_rstd",
              "st_g", "st_bg", "st_iso"):
        sb[n] = E(nc.sbuf_tensor(tag + n, [128, 64], F32))
    sb["k12T"] = E(nc.sbuf_tensor(tag + "k12T", [128, 16, 2048], BF16))
    sb["v_sb"] = E(nc.sbuf_tensor(tag + "v_sb", [128, 16, 1024], BF16))
    sb["wv_bf"] = E(nc.sbuf_tensor(tag + "wv_bf", [128, ND, 1024], BF16))
    sb["xtp"] = E(nc.sbuf_tensor(tag + "xtp", [128, ND, 256], BF16))
    sb["wring"] = E(nc.sbuf_tensor(tag + "wring", [128, 4, ND, 128], BF16))
    sb["wcast"] = E(nc.sbuf_tensor(tag + "wcast", [128, 1, 2048], F32))
    sb["wcbf"] = E(nc.sbuf_tensor(tag + "wcbf", [128, 2048], BF16))
    sb["q12T"] = E(nc.sbuf_tensor(tag + "q12T", [128, 16, 256], BF16))
    sb["e1b"] = E(nc.sbuf_tensor(tag + "e1b", [128, 2048], BF16))
    sb["e2h"] = E(nc.sbuf_tensor(tag + "e2h", [128, 2048], BF16))
    sb["dT"] = E(nc.sbuf_tensor(tag + "dT", [128, 16, 256], BF16))
    sb["oT"] = E(nc.sbuf_tensor(tag + "oT", [128, ND, 256], BF16))
    sb["otok"] = E(nc.sbuf_tensor(tag + "otok", [128, 2, 1024], BF16))
    sb["sc1"] = E(nc.sbuf_tensor(tag + "sc1", [1, 64], F32))
    sb["lam4"] = E(nc.sbuf_tensor(tag + "lam4_sb", [128, 4, 8], F32))
    sb["lscr"] = E(nc.sbuf_tensor(tag + "lscr", [128, 64], F32))
    sb["lamcol"] = E(nc.sbuf_tensor(tag + "lamcol", [128, 4], F32))
    sb["stt"] = E(nc.sbuf_tensor(tag + "stt", [128, 128], F32))
    sb["r1sav"] = E(nc.sbuf_tensor(tag + "r1sav", [128, 8], F32))
    sb["socol"] = E(nc.sbuf_tensor(tag + "socol", [128, 8], F32))

    psum = {
        "psA": E(nc.psum_tensor(tag + "psA", [128, 1024], F32)),
        "psB": E(nc.psum_tensor(tag + "psB", [128, 1024], F32)),
        "ptp": E(nc.psum_tensor(tag + "ptp", [128, ND, 128], BF16)),
        "pq": E(nc.psum_tensor(tag + "pq", [128, 2, 512], F32)),
        "pbr": E(nc.psum_tensor(tag + "pbr", [128, 512], F32)),
    }

    U = {"pbr": None, "ptp": None, "psA": None, "psB": None,
         "pq0": None, "pq1": None, "xtp": None, "q12T": None, "dT": None,
         "oT": None, "otok0": None, "otok1": None, "e1b": None, "e2h": None,
         "ring": [None] * 4, "xn_tp": None, "xslot": [None, None]}

    fac = emit_stats_factors(p, sb, x_ap)
    U["xslot"] = [fac, fac]

    prev_ld = None
    cast_done = {}
    prev_cv = None
    for name, src, dst, cols in [("wq", wq_ap, wqd, 2048), ("wk", wk_ap, wkd, 2048)]:
        for i in range(ND):
            sl = 0
            ld = p.dma(f"wc{sl}", lambda e, i=i, sl=sl, src=src, cols=cols: e.dma_start(
                out=sb["wcast"][:, sl, 0:cols], in_=src[128 * i:128 * (i + 1), :]),
                after=[prev_ld, prev_cv], name=f"wcl_{name}{i}")
            cv = p.op("vector", lambda v, sl=sl, cols=cols: v.tensor_copy(
                sb["wcbf"][:, 0:cols], sb["wcast"][:, sl, 0:cols]),
                after=[ld], name=f"wcc_{name}{i}")
            prev_cv = cv
            std = p.dma("wcs", lambda e, i=i, dst=dst, cols=cols: e.dma_start(
                out=dst[128 * i:128 * (i + 1), :], in_=sb["wcbf"][:, 0:cols]),
                after=[cv], name=f"wcst_{name}{i}")
            prev_ld = std
            cast_done[name] = std
    for i in range(ND):
        sl = 0
        ld = p.dma(f"wc{sl}", lambda e, i=i, sl=sl: e.dma_start(
            out=sb["wcast"][:, sl, 0:1024], in_=wv_ap[128 * i:128 * (i + 1), :]),
            after=[prev_ld, prev_cv], name=f"wcl_wv{i}")
        cv = p.op("vector", lambda v, i=i, sl=sl: v.tensor_copy(
            sb["wv_bf"][:, i, :], sb["wcast"][:, sl, 0:1024]),
            after=[ld], name=f"wcc_wv{i}")
        prev_ld = ld
        prev_cv = cv
        cast_done["wv"] = cv

    ldl = p.dma("wld", lambda e: e.dma_start(out=sb["lam4"][:], in_=lam_ap), name="ld_lam")
    l1 = p.op("vector", lambda v: v.tensor_tensor(
        sb["lscr"][:, 0:8], sb["lam4"][:, 0, :], sb["lam4"][:, 1, :], OP.mult),
        after=[ldl], name="lam_m1", small=True)
    l2 = p.op("gpsimd", lambda g: g.tensor_tensor(
        sb["lscr"][:, 8:16], sb["lam4"][:, 2, :], sb["lam4"][:, 3, :], OP.mult),
        after=[ldl], name="lam_m2", small=True)
    l3 = p.op("vector", lambda v: v.tensor_reduce(
        sb["lscr"][:, 16:17], sb["lscr"][:, 0:8], AX.X, OP.add),
        after=[l1], name="lam_r1", small=True)
    l4 = p.op("vector", lambda v: v.tensor_reduce(
        sb["lscr"][:, 32:33], sb["lscr"][:, 8:16], AX.X, OP.add),
        after=[l2, l3], name="lam_r2", small=True)
    l5 = p.op("tensor", lambda te: [te.transpose(
        psum["pbr"][0:1, 0:128], sb["lscr"][:, 16:17], sb["c_identf"][:]),
        te.transpose(psum["pbr"][0:1, 128:256], sb["lscr"][:, 32:33], sb["c_identf"][:])][-1],
        after=[l4, cevs[1], U["pbr"]], name="lam_tp")
    l6 = p.op("vector", lambda v: v.tensor_reduce(
        sb["sc1"][0:1, 0:1], psum["pbr"][0:1, 0:128], AX.X, OP.add),
        after=[l5], name="lam_d1", small=True)
    l7 = p.op("vector", lambda v: v.tensor_reduce(
        sb["sc1"][0:1, 4:5], psum["pbr"][0:1, 128:256], AX.X, OP.add),
        after=[l5, l6], name="lam_d2", small=True)
    U["pbr"] = l7
    l8 = p.op("scalar", lambda sc: sc.activation(sb["sc1"][0:1, 8:9], sb["sc1"][0:1, 0:1], AF.Exp),
              after=[l6], name="lam_e1", small=True)
    l9 = p.op("scalar", lambda sc: sc.activation(sb["sc1"][0:1, 12:13], sb["sc1"][0:1, 4:5], AF.Exp),
              after=[l7, l8], name="lam_e2", small=True)
    la_ = p.op("vector", lambda v: v.tensor_tensor(
        sb["sc1"][0:1, 16:17], sb["sc1"][0:1, 8:9], sb["sc1"][0:1, 12:13], OP.subtract),
        after=[l8, l9], name="lam_sub", small=True)
    lb_ = p.op("gpsimd", lambda g: g.tensor_scalar(
        sb["sc1"][0:1, 20:21], sb["sc1"][0:1, 16:17], 0.8, None, OP.add),
        after=[la_], name="lam_add", small=True)
    lc_ = p.op("tensor", lambda te: te.matmul(
        psum["pbr"][:, 0:1], sb["c_ones1f"][:, :], sb["sc1"][0:1, 20:21],
        start=True, stop=True), after=[lb_, U["pbr"], cevs[3]], name="lam_bc")
    lamc = p.op("vector", lambda v: v.tensor_copy(sb["lamcol"][:, 0:1], psum["pbr"][:, 0:1]),
                after=[lc_], name="lam_ev", small=True)
    U["pbr"] = lamc


    def build_xn(tiles):
        ev = None
        for j, t in enumerate(tiles):
            sl = t % 2
            ld = p.dma(f"xn{sl}", lambda e, t=t, sl=sl: e.dma_start(
                out=sb["xbuf"][:, sl, :], in_=x_ap[128 * t:128 * (t + 1), :]),
                after=[U["xslot"][sl], fac], name=f"xn_ld{t}")
            a1 = p.op("scalar", lambda s, t=t, sl=sl: s.activation(
                sb["xq_bf"][:], sb["xbuf"][:, sl, :], AF.Identity,
                bias=sb["st_bg"][:, 4 * t:4 * t + 1], scale=sb["st_g"][:, 4 * t:4 * t + 1]),
                after=[ld, U["xn_tp"]], name=f"xn_a{t}")
            U["xslot"][sl] = a1
            tp = p.op("tensor", lambda te: [te.transpose(
                psum["ptp"][:, dc, :], sb["xq_bf"][:, 128 * dc:128 * (dc + 1)],
                sb["c_identbf"][:]) for dc in range(ND)][-1],
                after=[a1, U["ptp"], cevs[0], U["xtp"]], name=f"xn_tp{t}")
            ev = p.op("vector", lambda v, j=j: v.tensor_copy(
                sb["xtp"][:, :, 128 * j:128 * (j + 1)], psum["ptp"][:, :, :]),
                after=[tp], name=f"xn_ev{t}")
            U["xn_tp"] = tp
            U["ptp"] = ev
        return ev

    last_kv_ev = None
    for piece in range(8):
        t0 = 2 * piece
        xe = build_xn([t0, t0 + 1])
        for ec in range(16):
            r = ec % 4
            ld = p.dma(f"kr{r}", lambda e, ec=ec, r=r: e.dma_start(
                out=sb["wring"][:, r, :, :],
                in_=wkd.rearrange("(dc p) (ec j) -> p dc ec j", p=128, j=128)[:, :, ec, :]),
                after=[U["ring"][r], cast_done["wk"]], name=f"krl_{piece}_{ec}")
            bk = "psA" if ec % 2 == 0 else "psB"

            def kmm(te, r=r, bk=bk):
                for dc in range(ND):
                    mm = te.matmul(psum[bk][:, 0:256], sb["wring"][:, r, dc, :],
                                   sb["xtp"][:, dc, :], start=(dc == 0), stop=(dc == ND - 1))
                return mm
            mm = p.op("tensor", kmm, after=[ld, xe, U[bk]], name=f"kmm{piece}_{ec}")
            U["ring"][r] = mm
            U["xtp"] = mm
            ev = p.op("vector", lambda v, ec=ec, piece=piece, bk=bk: v.tensor_copy(
                sb["k12T"][:, ec, 256 * piece:256 * (piece + 1)], psum[bk][:, 0:256]),
                after=[mm], name=f"kev{piece}_{ec}")
            U[bk] = ev
            last_kv_ev = ev
        for j, t in enumerate((t0, t0 + 1)):
            for fp in range(2):
                pqs = "pq0" if fp == 0 else "pq1"

                def vmm(te, j=j, fp=fp):
                    for dc in range(ND):
                        mm = te.matmul(psum["pq"][:, fp, :],
                                       sb["xtp"][:, dc, 128 * j:128 * (j + 1)],
                                       sb["wv_bf"][:, dc, 512 * fp:512 * (fp + 1)],
                                       start=(dc == 0), stop=(dc == ND - 1))
                    return mm
                mm = p.op("tensor", vmm, after=[xe, cast_done["wv"], U[pqs]], name=f"vmm{t}_{fp}")
                U["xtp"] = mm
                ev = p.op("vector", lambda v, t=t, fp=fp: v.tensor_copy(
                    sb["v_sb"][:, t, 512 * fp:512 * (fp + 1)], psum["pq"][:, fp, :]),
                    after=[mm], name=f"vev{t}_{fp}")
                U[pqs] = ev
                last_kv_ev = ev

    y_stores = []
    prev_ep = [None, None]
    for B in range(8):
        xe = build_xn([2 * B, 2 * B + 1])
        q_last = None
        for ec in range(16):
            r = ec % 4
            ld = p.dma(f"kr{r}", lambda e, ec=ec, r=r: e.dma_start(
                out=sb["wring"][:, r, :, :],
                in_=wqd.rearrange("(dc p) (ec j) -> p dc ec j", p=128, j=128)[:, :, ec, :]),
                after=[U["ring"][r], cast_done["wq"]], name=f"qrl_{B}_{ec}")
            pqs = "pq0" if ec % 2 == 0 else "pq1"

            def qmm(te, r=r, fp=0 if True else 0, pqs=pqs):
                fp = 0 if pqs == "pq0" else 1
                for dc in range(ND):
                    mm = te.matmul(psum["pq"][:, fp, 0:256], sb["wring"][:, r, dc, :],
                                   sb["xtp"][:, dc, :], start=(dc == 0), stop=(dc == ND - 1))
                return mm
            mm = p.op("tensor", qmm, after=[ld, xe, U[pqs], last_kv_ev], name=f"qmm{B}_{ec}")
            U["ring"][r] = mm
            U["xtp"] = mm
            ev = p.op("vector", lambda v, ec=ec, pqs=pqs: v.tensor_copy(
                sb["q12T"][:, ec, :], psum["pq"][0:128, 0 if pqs == "pq0" else 1, 0:256]),
                after=[mm, U["q12T"]], name=f"qev{B}_{ec}")
            U[pqs] = ev
            q_last = ev
        dT_evs = []
        for tt in range(2):
            qsl = 128 * tt
            gt = 2 * B + tt
            s_ops = {}
            for which, ecbase in ((1, 0), (2, 8)):
                for half, ps in ((0, "psA"), (1, "psB")):
                    def smm2(te, qsl=qsl, ecbase=ecbase, half=half, ps=ps):
                        for pc in range(2):
                            for i, ec in enumerate(range(ecbase, ecbase + 8)):
                                mm = te.matmul(
                                    psum[ps][:, 512 * pc:512 * (pc + 1)],
                                    sb["q12T"][:, ec, qsl:qsl + 128],
                                    sb["k12T"][:, ec, 1024 * half + 512 * pc:1024 * half + 512 * (pc + 1)],
                                    start=(i == 0), stop=(i == 7))
                        return mm
                    dep = [q_last, U[ps]] if which == 1 else [s_ops[(1, half, "exp")], U[ps]]
                    mm = p.op("tensor", smm2, after=dep, name=f"smm{B}_{tt}_{which}_{half}")
                    st_off = (0 if which == 1 else 24) + 4 * half
                    rd = p.op("vector", lambda v, ps=ps, st_off=st_off: v.tensor_reduce(
                        sb["stt"][:, st_off:st_off + 1], psum[ps][:, :], AX.X, OP.max,
                        negate=True), after=[mm], name=f"smax{B}_{tt}_{which}_{half}", small=True)
                    s_ops[(which, half, "mm")] = mm
                    s_ops[(which, half, "rd")] = rd
                base = 0 if which == 1 else 24
                cb = p.op("vector", lambda g, base=base: g.tensor_tensor(
                    sb["stt"][:, base + 8:base + 9], sb["stt"][:, base:base + 1],
                    sb["stt"][:, base + 4:base + 5], OP.min),
                    after=[s_ops[(which, 0, "rd")], s_ops[(which, 1, "rd")]],
                    name=f"scmb{B}_{tt}_{which}", small=True)
                sc_ = p.op("vector", lambda v, base=base: v.tensor_scalar(
                    sb["stt"][:, base + 12:base + 13], sb["stt"][:, base + 8:base + 9],
                    SCALE, None, OP.mult), after=[cb], name=f"sbias{B}_{tt}_{which}", small=True)
                ebuf = "e1b" if which == 1 else "e2h"
                for half, ps in ((0, "psA"), (1, "psB")):
                    ex = p.op("scalar", lambda s, ps=ps, half=half, ebuf=ebuf, base=base: s.activation(
                        sb[ebuf][:, 1024 * half:1024 * (half + 1)], psum[ps][:, :], AF.Exp,
                        bias=sb["stt"][:, base + 12:base + 13], scale=SCALE,
                        accum_out=sb["stt"][:, base + 16 + 4 * half:base + 17 + 4 * half]),
                        after=[sc_, U[ebuf]], name=f"sexp{B}_{tt}_{which}_{half}")
                    s_ops[(which, half, "exp")] = ex
                    U[ps] = ex
                s_ops[(which, 1, "expboth")] = s_ops[(which, 1, "exp")]
                U[ebuf] = None
            r1c = p.op("vector", lambda v: v.tensor_tensor(
                sb["stt"][:, 48:49], sb["stt"][:, 16:17], sb["stt"][:, 20:21], OP.add),
                after=[s_ops[(1, 0, "exp")], s_ops[(1, 1, "exp")]], name=f"r1{B}_{tt}", small=True)
            r2c = p.op("gpsimd", lambda g: g.tensor_tensor(
                sb["stt"][:, 52:53], sb["stt"][:, 40:41], sb["stt"][:, 44:45], OP.add),
                after=[s_ops[(2, 0, "exp")], s_ops[(2, 1, "exp")]], name=f"r2{B}_{tt}", small=True)
            rv = p.op("scalar", lambda sc: sc.activation(
                sb["stt"][:, 56:57], sb["stt"][:, 52:53], AF.Ln),
                after=[r2c], name=f"rln{B}_{tt}", small=True)
            rv2 = p.op("scalar", lambda sc: sc.activation(
                sb["stt"][:, 60:61], sb["stt"][:, 56:57], AF.Exp, scale=-1.0),
                after=[rv], name=f"rexp{B}_{tt}", small=True)
            sv1 = p.op("vector", lambda v: v.tensor_tensor(
                sb["stt"][:, 64:65], sb["stt"][:, 48:49], sb["stt"][:, 60:61], OP.mult),
                after=[r1c, rv2], name=f"sv1{B}_{tt}", small=True)
            sv2 = p.op("gpsimd", lambda g: g.tensor_tensor(
                sb["stt"][:, 68:69], sb["stt"][:, 64:65], sb["lamcol"][:, 0:1], OP.mult),
                after=[sv1, lamc], name=f"sv2{B}_{tt}", small=True)
            negs = p.op("vector", lambda v: v.tensor_scalar(
                sb["stt"][:, 72:73], sb["stt"][:, 68:69], -1.0, None, OP.mult),
                after=[sv2], name=f"negs{B}_{tt}", small=True)
            r1s = p.op("gpsimd", lambda g, tt=tt: g.tensor_copy(
                sb["r1sav"][:, 4 * tt:4 * tt + 1], sb["stt"][:, 48:49]),
                after=[r1c], name=f"r1s{B}_{tt}", small=True)
            dcmb = p.op("vector", lambda v: v.scalar_tensor_tensor(
                sb["e1b"][:, :], sb["e2h"][:, :], sb["stt"][:, 72:73], sb["e1b"][:, :],
                OP.mult, OP.add), after=[negs, s_ops[(1, 1, "exp")], s_ops[(2, 1, "exp")]],
                name=f"dcmb{B}_{tt}")
            U["e1b"] = dcmb
            U["e2h"] = dcmb
            tp_prev = dcmb
            for half in range(2):
                tp = p.op("tensor", lambda te, half=half: [te.transpose(
                    psum["ptp"][:, kk, :],
                    sb["e1b"][:, 1024 * half + 128 * kk:1024 * half + 128 * (kk + 1)],
                    sb["c_identbf"][:]) for kk in range(8)][-1],
                    after=[tp_prev, U["ptp"]], name=f"dtp{B}_{tt}_{half}")
                ev = p.op("vector", lambda v, half=half, qsl=qsl: v.tensor_copy(
                    sb["dT"][:, 8 * half:8 * (half + 1), qsl:qsl + 128], psum["ptp"][:, :, :]),
                    after=[tp, U["dT"]], name=f"dtev{B}_{tt}_{half}")
                U["ptp"] = ev
                tp_prev = ev
                dT_evs.append(ev)
            U["e1b"] = dT_evs[-1]
            U["e2h"] = dcmb
            U["q12T"] = s_ops[(2, 1, "mm")]
        U["dT"] = None
        pv_evs = []
        for dc in range(ND):
            pqs = "pq0" if dc % 2 == 0 else "pq1"
            fp = 0 if dc % 2 == 0 else 1

            def pvmm(te, dc=dc, fp=fp):
                for kt in range(16):
                    mm = te.matmul(psum["pq"][:, fp, 0:256],
                                   sb["v_sb"][:, kt, 128 * dc:128 * (dc + 1)],
                                   sb["dT"][:, kt, :], start=(kt == 0), stop=(kt == 15))
                return mm
            mm = p.op("tensor", pvmm, after=dT_evs + [U[pqs], last_kv_ev], name=f"pv{B}_{dc}")
            ev = p.op("vector", lambda v, dc=dc, fp=fp: v.tensor_copy(
                sb["oT"][:, dc, :], psum["pq"][:, fp, 0:256]),
                after=[mm, U["oT"]], name=f"pvev{B}_{dc}")
            U[pqs] = ev
            pv_evs.append(ev)
        U["dT"] = pv_evs[-1]
        for tt in range(2):
            gt = 2 * B + tt
            tp = p.op("tensor", lambda te, tt=tt: [te.transpose(
                psum["ptp"][:, dc, :], sb["oT"][:, dc, 128 * tt:128 * (tt + 1)],
                sb["c_identbf"][:]) for dc in range(ND)][-1],
                after=pv_evs + [U["ptp"]], name=f"otp{B}_{tt}")
            oev = p.op("vector", lambda v, tt=tt: v.tensor_copy(
                sb["otok"][:, tt, :], psum["ptp"][:, :, :].rearrange("p a b -> p (a b)")),
                after=[tp, U["otok%d" % tt]], name=f"oev{B}_{tt}")
            U["ptp"] = oev
            U["oT"] = tp
            sq = p.op("scalar", lambda s, tt=tt: s.activation(
                sb["scr_sq"][:], sb["otok"][:, tt, :], AF.Square,
                accum_out=sb["stt"][:, 76 + 16 * tt:77 + 16 * tt]),
                after=[oev], name=f"osq{B}_{tt}")
            c1 = p.op("vector", lambda v, tt=tt: v.tensor_scalar(
                sb["stt"][:, 80 + 16 * tt:81 + 16 * tt], sb["stt"][:, 76 + 16 * tt:77 + 16 * tt],
                1.0 / D, None, OP.mult), after=[sq], name=f"oc1{B}_{tt}", small=True)
            c2 = p.op("gpsimd", lambda g, tt=tt: g.tensor_tensor(
                sb["stt"][:, 84 + 16 * tt:85 + 16 * tt], sb["r1sav"][:, 4 * tt:4 * tt + 1],
                sb["r1sav"][:, 4 * tt:4 * tt + 1], OP.mult),
                after=[r1s if tt == 1 else r1s, c1], name=f"oc2{B}_{tt}", small=True)
            c3 = p.op("vector", lambda v, tt=tt: v.scalar_tensor_tensor(
                sb["stt"][:, 88 + 16 * tt:89 + 16 * tt], sb["stt"][:, 84 + 16 * tt:85 + 16 * tt],
                1e-5, sb["stt"][:, 80 + 16 * tt:81 + 16 * tt], OP.mult, OP.add),
                after=[c2], name=f"oc3{B}_{tt}", small=True)
            c4 = p.op("scalar", lambda sc, tt=tt: sc.activation(
                sb["stt"][:, 92 + 16 * tt:93 + 16 * tt], sb["stt"][:, 88 + 16 * tt:89 + 16 * tt],
                AF.Ln), after=[c3], name=f"oc4{B}_{tt}", small=True)
            c5 = p.op("scalar", lambda sc, tt=tt: sc.activation(
                sb["stt"][:, 96 + 16 * tt:97 + 16 * tt], sb["stt"][:, 92 + 16 * tt:93 + 16 * tt],
                AF.Exp, scale=-0.5), after=[c4], name=f"oc5{B}_{tt}", small=True)
            c6 = p.op("vector", lambda v, tt=tt: v.tensor_scalar(
                sb["socol"][:, 4 * tt:4 * tt + 1], sb["stt"][:, 96 + 16 * tt:97 + 16 * tt],
                0.2, None, OP.mult), after=[c5], name=f"oc6{B}_{tt}", small=True)
            sl = gt % 2
            lx = p.dma(f"xe{sl}", lambda e, gt=gt, sl=sl: e.dma_start(
                out=sb["xbuf"][:, sl, :], in_=x_ap[128 * gt:128 * (gt + 1), :]),
                after=[prev_ep[sl], U["xslot"][sl]], name=f"xe_ld{gt}")
            ep = p.op("vector", lambda v, sl=sl, tt=tt: v.scalar_tensor_tensor(
                sb["ybuf"][:, sl, :], sb["otok"][:, tt, :], sb["socol"][:, 4 * tt:4 * tt + 1],
                sb["xbuf"][:, sl, :], OP.mult, OP.add),
                after=[c6, lx, oev, prev_ep[sl]], name=f"aep{gt}")
            U["otok%d" % tt] = ep
            U["xslot"][sl] = ep
            st = p.dma(f"yst{sl}", lambda e, gt=gt, sl=sl: e.dma_start(
                out=y_ap[128 * gt:128 * (gt + 1), :], in_=sb["ybuf"][:, sl, :]),
                after=[ep], name=f"y_st{gt}")
            prev_ep[sl] = st
            y_stores.append(st)

    p.barrier("sync", y_stores, name=tag + "final")
    stack.close()
    p.seal(name=tag + "seal")

def build_attn0(debug=False):
    nc = bass.Bass("TRN2", target_bir_lowering=False)
    aps = {}
    aps["x"] = nc.dram_tensor("x", [SEQ, D], F32, kind="ExternalInput").ap()
    aps["y"] = nc.dram_tensor("y", [SEQ, D], F32, kind="ExternalOutput").ap()
    aps["wq12"] = nc.dram_tensor("wq12", [D, 2 * D], F32, kind="ExternalInput").ap()
    aps["wk12"] = nc.dram_tensor("wk12", [D, 2 * D], F32, kind="ExternalInput").ap()
    aps["wv"] = nc.dram_tensor("wv", [D, D], F32, kind="ExternalInput").ap()
    aps["lam4"] = nc.dram_tensor("lam4", [128, 4, 8], F32, kind="ExternalInput").ap()
    cs = declare_consts(nc)
    outer = contextlib.ExitStack()
    sbc = alloc_consts(nc, outer)
    p = Prog()
    cevs = load_consts(p, cs, sbc)
    emit_attn0(nc, p, sbc, cevs, aps, "a0_", debug=debug)
    p.emit(nc, outer)
    return nc



def run_attn0(xs, inputs, consts):
    nc = build_attn0()
    lam4 = np.stack([np.asarray(inputs[k], np.float32).reshape(128, 8) for k in
                     ("L0_attn_lambda_q1", "L0_attn_lambda_k1",
                      "L0_attn_lambda_q2", "L0_attn_lambda_k2")], axis=1)
    w = {"wq12": np.ascontiguousarray(np.asarray(inputs["L0_attn_w_q12"], np.float32)),
         "wk12": np.ascontiguousarray(np.asarray(inputs["L0_attn_w_k12"], np.float32)),
         "wv": np.ascontiguousarray(np.asarray(inputs["L0_attn_w_v"], np.float32)),
         "lam4": np.ascontiguousarray(lam4)}
    in_maps = []
    for c in range(NCORES):
        m = {"x": xs[c]}
        m.update(w)
        m.update(consts)
        in_maps.append(m)
    res = run_bass_kernel_spmd(nc, in_maps, core_ids=list(range(NCORES)))
    return [res.results[c]["y"] for c in range(NCORES)]



def emit_ffn1(nc, p, sbc, cevs, aps, tag, debug=False, shared=None):
    x_ap, y_ap = aps["x"], aps["y"]
    k_ap, bias_ap, a_ap, b_ap = aps["kern"], aps["bias"], aps["la"], aps["lb"]
    dbg = {}
    if debug:
        dbg["d_kq"] = nc.dram_tensor("d_kq", [128, ND, 1024], BF16, kind="ExternalOutput").ap()
        dbg["d_sc1"] = nc.dram_tensor("d_sc1", [1, 64], F32, kind="ExternalOutput").ap()
        dbg["d_xt"] = nc.dram_tensor("d_xt", [128, ND, 2050], BF16, kind="ExternalOutput").ap()
        dbg["d_g"] = nc.dram_tensor("d_g", [128, 64], F32, kind="ExternalOutput").ap()
        dbg["d_bg"] = nc.dram_tensor("d_bg", [128, 64], F32, kind="ExternalOutput").ap()

    stack = contextlib.ExitStack()
    E = stack.enter_context
    sb = dict(sbc)
    sb["xbuf"] = E(nc.sbuf_tensor(tag + "xbuf", [128, 2, 1024], F32))
    sb["ybuf"] = E(nc.sbuf_tensor(tag + "ybuf", [128, 2, 1024], F32))
    sb["scr_sq"] = E(nc.sbuf_tensor(tag + "scr_sq", [128, 1024], F32))
    sb["kscr"] = E(nc.sbuf_tensor(tag + "kscr", [128, 1024], F32))
    sb["scr_s"] = E(nc.sbuf_tensor(tag + "scr_s", [128, 256], F32))
    sb["xq_bf"] = E(nc.sbuf_tensor(tag + "xq_bf", [128, 1024], BF16))
    for n in ("st_sum", "st_ssq", "st_max", "st_min", "st_m", "st_rstd",
              "st_g", "st_bg", "st_iso"):
        sb[n] = E(nc.sbuf_tensor(tag + n, [128, 64], F32))
    sb["xt"] = E(nc.sbuf_tensor(tag + "xt", [128, ND, 2050], BF16))
    sb["kf"] = E(nc.sbuf_tensor(tag + "kf", [128, ND, 1024], F32))
    sb["kq"] = E(nc.sbuf_tensor(tag + "kq", [128, ND, 1024], BF16))
    sb["a_bf"] = E(nc.sbuf_tensor(tag + "a_bf", [128, ND, 16], BF16))
    sb["a_f"] = E(nc.sbuf_tensor(tag + "a_f", [128, ND, 16], F32))
    sb["b_bf"] = E(nc.sbuf_tensor(tag + "b_bf", [16, 1024], BF16))
    sb["b_f"] = E(nc.sbuf_tensor(tag + "b_f", [16, 1024], F32))
    sb["bias_bf"] = E(nc.sbuf_tensor(tag + "bias_bf", [1, 1024], BF16))
    sb["bias_f"] = E(nc.sbuf_tensor(tag + "bias_f", [1, 1024], F32))
    sb["xa_bf"] = E(nc.sbuf_tensor(tag + "xa_bf", [128, 16], BF16))
    sb["xaT"] = E(nc.sbuf_tensor(tag + "xaT", [16, 128], BF16))
    sb["wabs"] = E(nc.sbuf_tensor(tag + "wabs", [128, 64], F32))
    sb["sc1"] = E(nc.sbuf_tensor(tag + "sc1", [1, 64], F32))
    sb["sccol"] = E(nc.sbuf_tensor(tag + "sccol", [128, 8], F32))

    psum = {
        "out2": E(nc.psum_tensor(tag + "out2", [128, 2, 1024], F32)),
        "ptp": E(nc.psum_tensor(tag + "ptp", [128, ND, 128], BF16)),
        "pbr": E(nc.psum_tensor(tag + "pbr", [128, 512], F32)),
    }

    pbr_user = None

    lda = p.dma("lda", lambda e: e.dma_start(
        out=sb["a_f"][:], in_=a_ap.rearrange("(dc p) r -> p dc r", p=128)), name="ld_a")
    ca = p.op("vector", lambda v: v.tensor_copy(sb["a_bf"][:], sb["a_f"][:]), after=[lda], name="cast_a")
    ldb = p.dma("ldb", lambda e: e.dma_start(out=sb["b_f"][:], in_=b_ap), name="ld_b")
    cb_ = p.op("vector", lambda v: v.tensor_copy(sb["b_bf"][:], sb["b_f"][:]), after=[ldb], name="cast_b")
    ldbias = p.dma("ldbias", lambda e: e.dma_start(out=sb["bias_f"][:], in_=bias_ap), name="ld_bias")
    cbias = p.op("vector", lambda v: v.tensor_copy(sb["bias_bf"][:], sb["bias_f"][:]),
                 after=[ldbias], name="cast_bias")

    if shared is None:
        kq_dram = nc.dram_tensor(tag + "kq_scr", [128, ND * 1024], BF16).ap()
        ldk = p.dma("kld", lambda e: e.dma_start(
            out=sb["kf"][:], in_=k_ap.rearrange("(dc p) f -> p dc f", p=128)), name="ld_k")
        kst = p.op("vector", lambda v: [v.tensor_reduce(
            sb["wabs"][:, 4 * dc:4 * dc + 1], sb["kf"][:, dc, :], AX.X, OP.max,
            apply_absolute_value=True) for dc in range(ND)][-1], after=[ldk], name="kst", small=True)
        wv = sb["wabs"][:, 0:32].rearrange("p (t u) -> p t u", u=4)[:, :, 0:1]
        kt1 = p.op("vector", lambda v: v.tensor_reduce(sb["wabs"][:, 40:41], wv, AX.XY, OP.max),
                   after=[kst], name="ktot", small=True)
        kt2 = p.op("tensor", lambda te: te.transpose(
            psum["pbr"][0:1, 0:128], sb["wabs"][:, 40:41], sb["c_identf"][:]),
            after=[kt1, cevs[1], pbr_user], name="ktot_tp")
        kt3 = p.op("vector", lambda v: v.tensor_reduce(
            sb["sc1"][0:1, 0:1], psum["pbr"][0:1, 0:128], AX.X, OP.max),
            after=[kt2], name="kmax", small=True)
        pbr_user = kt3
        kt4 = p.op("scalar", lambda sc: sc.activation(sb["sc1"][0:1, 4:5], sb["sc1"][0:1, 0:1], AF.Ln),
                   after=[kt3], name="kln", small=True)
        kt5 = p.op("scalar", lambda sc: sc.activation(sb["sc1"][0:1, 8:9], sb["sc1"][0:1, 4:5],
                                                      AF.Exp, scale=-1.0), after=[kt4], name="kexp", small=True)
        kt6 = p.op("vector", lambda v: v.tensor_scalar(sb["sc1"][0:1, 8:9], sb["sc1"][0:1, 8:9],
                                                       7.0, None, OP.mult), after=[kt5], name="ks", small=True)
        kt7 = p.op("gpsimd", lambda g: g.tensor_scalar(sb["sc1"][0:1, 12:13], sb["sc1"][0:1, 0:1],
                                                       1.0 / 7.0, None, OP.mult), after=[kt3], name="km", small=True)
        bcs = []
        for j, off in enumerate([8, 12]):
            mm = p.op("tensor", lambda te, off=off: te.matmul(
                psum["pbr"][:, 0:1], sb["c_ones1f"][:, :], sb["sc1"][0:1, off:off + 1],
                start=True, stop=True), after=[kt6, kt7, pbr_user, cevs[3]], name=f"kb{j}_mm")
            ev = p.op("vector", lambda v, j=j: v.tensor_copy(
                sb["sccol"][:, 4 * j:4 * j + 1], psum["pbr"][:, 0:1]), after=[mm],
                name=f"kb{j}_ev", small=True)
            pbr_user = ev
            bcs.append(ev)
        kq_done = None
        for dc in range(ND):
            a1 = p.op("scalar", lambda sc, dc=dc: sc.activation(
                sb["kscr"][:], sb["kf"][:, dc, :], AF.Copy, scale=sb["sccol"][:, 0:1]),
                after=[bcs[0], ldk, kq_done], name=f"kq_a{dc}")
            r1 = p.op("vector", lambda v: v.tensor_scalar(
                sb["kscr"][:], sb["kscr"][:], MAGIC, MAGIC, OP.add, OP.subtract),
                after=[a1], name=f"kq_r{dc}")
            kq_done = p.op("scalar", lambda sc, dc=dc: sc.activation(
                sb["kq"][:, dc, :], sb["kscr"][:], AF.Copy, scale=sb["sccol"][:, 4:5]),
                after=[r1, bcs[1]], name=f"kq_c{dc}")

        st_kq = p.dma("kqst", lambda e: e.dma_start(
            out=kq_dram, in_=sb["kq"][:].rearrange("p a b -> p (a b)")),
            after=[kq_done], name=tag + "st_kq")
    else:
        kq_dram = shared["kq"]
        kq_done = p.dma("kqld2", lambda e: e.dma_start(
            out=sb["kq"][:].rearrange("p a b -> p (a b)"), in_=kq_dram),
            name=tag + "ld_kq")
    xt_done, xt_evs = emit_xphase(p, sb, psum, x_ap, quant=False, cident=cevs[0])

    y_stores = []
    out2_user = [None, None]
    ptp_user = xt_done
    prev_ep = [xt_done, xt_done]
    xa_user = None
    xaT_user = None
    for t in range(NT):
        bk = t % 2
        mm_a = p.op("tensor", lambda te, t=t: [te.matmul(
            psum["pbr"][:, 0:16], sb["xt"][:, dc, 1 + 128 * t:1 + 128 * (t + 1)],
            sb["a_bf"][:, dc, :], start=(dc == 0), stop=(dc == ND - 1))
            for dc in range(ND)][-1], after=[xt_done, ca, pbr_user], name=f"mma{t}")
        ev_a = p.op("vector", lambda v: v.tensor_copy(sb["xa_bf"][:], psum["pbr"][:, 0:16]),
                    after=[mm_a, xa_user], name=f"eva{t}")
        pbr_user = ev_a
        tp_a = p.op("tensor", lambda te: te.transpose(
            psum["ptp"][0:16, 0, 0:128], sb["xa_bf"][:], sb["c_identbf"][:]),
            after=[ev_a, ptp_user], name=f"tpa{t}")
        xa_user = tp_a
        ev_t = p.op("vector", lambda v: v.tensor_copy(sb["xaT"][:], psum["ptp"][0:16, 0, 0:128]),
                    after=[tp_a, xaT_user], name=f"evt{t}")
        ptp_user = ev_t

        def mm2(te, t=t, bk=bk):
            for piece in range(2):
                po = psum["out2"][:, bk, 512 * piece:512 * (piece + 1)]
                for dc in range(ND):
                    te.matmul(po, sb["xt"][:, dc, 1 + 128 * t:1 + 128 * (t + 1)],
                              sb["kq"][:, dc, 512 * piece:512 * (piece + 1)],
                              start=(dc == 0), stop=False)
                te.matmul(po, sb["c_ones1bf"][:, :], sb["bias_bf"][0:1, 512 * piece:512 * (piece + 1)],
                          start=False, stop=False)
                mm = te.matmul(po, sb["xaT"][:, :], sb["b_bf"][0:16, 512 * piece:512 * (piece + 1)],
                               start=False, stop=True)
            return mm
        m2 = p.op("tensor", mm2, after=[ev_t, kq_done, cbias, cb_, out2_user[bk]], name=f"mm2_{t}")
        xaT_user = m2
        sl = t % 2
        lx = p.dma(f"xe{sl}", lambda e, t=t, sl=sl: e.dma_start(
            out=sb["xbuf"][:, sl, :], in_=x_ap[128 * t:128 * (t + 1), :]),
            after=[prev_ep[sl]], name=f"xe_ld{t}")
        ep = p.op("vector", lambda v, sl=sl, bk=bk: v.scalar_tensor_tensor(
            sb["ybuf"][:, sl, :], psum["out2"][:, bk, :], 0.5,
            sb["xbuf"][:, sl, :], OP.mult, OP.add),
            after=[m2, lx], name=f"ep{t}")
        out2_user[bk] = ep
        st = p.dma(f"yst{sl}", lambda e, t=t, sl=sl: e.dma_start(
            out=y_ap[128 * t:128 * (t + 1), :], in_=sb["ybuf"][:, sl, :]),
            after=[ep], name=f"y_st{t}")
        prev_ep[sl] = st
        y_stores.append(st)

    if debug:
        y_stores = y_stores + [
            p.dma("dbg", lambda e: e.dma_start(out=dbg["d_kq"], in_=sb["kq"][:]), after=[kq_done], name="d_kq"),
            p.dma("dbg", lambda e: e.dma_start(out=dbg["d_sc1"], in_=sb["sc1"][:]), after=[kt6, kt7], name="d_sc1"),
            p.dma("dbg", lambda e: e.dma_start(out=dbg["d_xt"], in_=sb["xt"][:]), after=[xt_done], name="d_xt"),
            p.dma("dbg", lambda e: e.dma_start(out=dbg["d_g"], in_=sb["st_g"][:]), after=[fac], name="d_g"),
            p.dma("dbg", lambda e: e.dma_start(out=dbg["d_bg"], in_=sb["st_bg"][:]), after=[fac], name="d_bg"),
        ]
    p.barrier("sync", y_stores, name=tag + "final")
    stack.close()
    p.seal(name=tag + "seal")
    return {"kq": kq_dram}

def build_ffn1(debug=False):
    nc = bass.Bass("TRN2", target_bir_lowering=False)
    aps = {}
    aps["x"] = nc.dram_tensor("x", [SEQ, D], F32, kind="ExternalInput").ap()
    aps["y"] = nc.dram_tensor("y", [SEQ, D], F32, kind="ExternalOutput").ap()
    aps["kern"] = nc.dram_tensor("kern", [D, D], F32, kind="ExternalInput").ap()
    aps["bias"] = nc.dram_tensor("bias", [1, D], F32, kind="ExternalInput").ap()
    aps["la"] = nc.dram_tensor("la", [D, 16], F32, kind="ExternalInput").ap()
    aps["lb"] = nc.dram_tensor("lb", [16, D], F32, kind="ExternalInput").ap()
    cs = declare_consts(nc)
    outer = contextlib.ExitStack()
    sbc = alloc_consts(nc, outer)
    p = Prog()
    cevs = load_consts(p, cs, sbc)
    emit_ffn1(nc, p, sbc, cevs, aps, "f1_", debug=debug)
    p.emit(nc, outer)
    return nc



def run_ffn1(xs, inputs, consts):
    nc = build_ffn1()
    w = {"kern": np.ascontiguousarray(np.asarray(inputs["L1_ffn_kernel"], np.float32)),
         "bias": np.ascontiguousarray(np.asarray(inputs["L1_ffn_bias"], np.float32).reshape(1, D)),
         "la": np.ascontiguousarray(np.asarray(inputs["L1_ffn_a"], np.float32)),
         "lb": np.ascontiguousarray(np.asarray(inputs["L1_ffn_b"], np.float32))}
    in_maps = []
    for c in range(NCORES):
        m = {"x": xs[c]}
        m.update(w)
        m.update(consts)
        in_maps.append(m)
    res = run_bass_kernel_spmd(nc, in_maps, core_ids=list(range(NCORES)))
    return [res.results[c]["y"] for c in range(NCORES)]



def emit_attn1(nc, p, sbc, cevs, aps, tag, debug=False):
    x_ap, y_ap = aps["x"], aps["y"]
    wq_ap, wk_ap, wv_ap, wo_ap = aps["Wq"], aps["Wk"], aps["Wv"], aps["Wo"]
    bq_ap, bk_ap, bv_ap, bo_ap = aps["bq"], aps["bk"], aps["bv"], aps["bo"]
    qnw_ap, knw_ap = aps["qnw_t"], aps["knw_t"]

    H = 16
    ISQ = 0.125
    stack = contextlib.ExitStack()
    E = stack.enter_context
    sb = dict(sbc)
    sb["xbuf"] = E(nc.sbuf_tensor(tag + "xbuf", [128, 2, 1024], F32))
    sb["scr_sq"] = E(nc.sbuf_tensor(tag + "scr_sq", [128, 1040], F32))
    sb["scr_s"] = E(nc.sbuf_tensor(tag + "scr_s", [128, 256], F32))
    sb["xq_bf"] = E(nc.sbuf_tensor(tag + "xq_bf", [128, 1024], BF16))
    for n in ("st_sum", "st_ssq", "st_max", "st_min", "st_m", "st_rstd",
              "st_g", "st_bg", "st_iso"):
        sb[n] = E(nc.sbuf_tensor(tag + n, [128, 64], F32))
    sb["xtp"] = E(nc.sbuf_tensor(tag + "xtp", [128, ND, 128], BF16))
    sb["wcur"] = E(nc.sbuf_tensor(tag + "wcur", [128, ND, 1040], BF16))
    sb["wo_bf"] = E(nc.sbuf_tensor(tag + "wo_bf", [128, ND, 1024], BF16))
    sb["wcast"] = E(nc.sbuf_tensor(tag + "wcast", [128, 1040], F32))
    sb["brow"] = E(nc.sbuf_tensor(tag + "brow", [1, 1, 1040], BF16))
    sb["brow_f"] = E(nc.sbuf_tensor(tag + "brow_f", [1, 1040], F32))
    sb["qnrep"] = E(nc.sbuf_tensor(tag + "qnrep", [128, 1024], BF16))
    sb["knrep"] = E(nc.sbuf_tensor(tag + "knrep", [128, 1024], BF16))
    sb["nwrow"] = E(nc.sbuf_tensor(tag + "nwrow", [1, 1024], F32))
    sb["qnT"] = E(nc.sbuf_tensor(tag + "qnT", [128, ND, 2048], BF16))
    sb["knT"] = E(nc.sbuf_tensor(tag + "knT", [128, ND, 2048], BF16))
    sb["vaug"] = E(nc.sbuf_tensor(tag + "vaug", [128, 16, 16, 65], BF16))
    sb["gsig"] = E(nc.sbuf_tensor(tag + "gsig", [128, 16, 16], BF16))
    sb["eT0"] = E(nc.sbuf_tensor(tag + "eT0", [128, 16, 256], BF16))
    sb["eT1"] = E(nc.sbuf_tensor(tag + "eT1", [128, 16, 256], BF16))
    sb["negmT"] = E(nc.sbuf_tensor(tag + "negmT", [128, 2048], BF16))
    sb["mqt"] = E(nc.sbuf_tensor(tag + "mqt", [128, 136], F32))
    sb["otokS"] = E(nc.sbuf_tensor(tag + "otokS", [128, 2, 16, 65], BF16))
    sb["ob65"] = E(nc.sbuf_tensor(tag + "ob65", [65, 256], BF16))
    sb["ofT"] = E(nc.sbuf_tensor(tag + "ofT", [128, ND, 256], BF16))
    sb["qn_s"] = E(nc.sbuf_tensor(tag + "qn_s", [128, 1024], BF16))
    sb["stt"] = E(nc.sbuf_tensor(tag + "stt", [128, 128], F32))
    sb["fcol"] = E(nc.sbuf_tensor(tag + "fcol", [128, 2, 16], F32))

    psum = {
        "psA": E(nc.psum_tensor(tag + "psA", [128, 1024], F32)),
        "psB": E(nc.psum_tensor(tag + "psB", [128, 1024], F32)),
        "psC": E(nc.psum_tensor(tag + "psC", [128, 1024], F32)),
        "ptp": E(nc.psum_tensor(tag + "ptp", [128, ND, 128], BF16)),
        "pbr": E(nc.psum_tensor(tag + "pbr", [128, 512], F32)),
    }

    U = {"pbr": None, "ptp": None, "psA": None, "psB": None, "xtp": None,
         "xn_tp": None, "xslot": [None, None], "eT": None, "qnT": None,
         "knT": None, "vaug": None, "negmT": None, "otokS": None, "ofT": None,
         "mqt": None, "wcur": None, "qn_s": None, "ob65": None,
         "otokS_last": None, "psB0": None, "psB1": None,
         "psA0": None, "psA1": None, "psC0": None, "psC1": None,
         "eT0": None, "eT1": None}

    fac = emit_stats_factors(p, sb, x_ap)
    U["xslot"] = [fac, fac]

    bevs = {}
    prev_b = [None]
    _baps = {0: (bq_ap, 1040), 1: (bk_ap, 1024), 2: (bv_ap, 1024), 3: (bo_ap, 1024)}

    def load_bias(j, extra=()):
        bap, cols = _baps[j]
        ld = p.dma("brl", lambda e, bap=bap, cols=cols: e.dma_start(
            out=sb["brow_f"][:, 0:cols], in_=bap),
            after=[prev_b[0]] + list(extra), name=f"br_ld{j}")
        cvt = p.op("vector", lambda v, cols=cols: v.tensor_copy(
            sb["brow"][:, 0, 0:cols], sb["brow_f"][:, 0:cols]), after=[ld],
            name=f"br_c{j}", small=True)
        bevs[j] = cvt
        prev_b[0] = cvt
        return cvt
    nwevs = {}
    for j, nap in enumerate([qnw_ap, knw_ap]):
        dest = sb["qnrep"] if j == 0 else sb["knrep"]
        ld = p.dma("brl", lambda e, nap=nap: e.dma_start(
            out=sb["nwrow"][:], in_=nap), after=[prev_b, nwevs.get(j - 1)], name=f"nw_ld{j}")
        last = None
        for pc in range(2):
            mm = p.op("tensor", lambda te, pc=pc: te.matmul(
                psum["pbr"][:, 0:512], sb["c_ones1f"][:, :],
                sb["nwrow"][0:1, 512 * pc:512 * (pc + 1)], start=True, stop=True),
                after=[ld, U["pbr"], cevs[3]], name=f"nw_mm{j}_{pc}")
            ev = p.op("vector", lambda v, pc=pc, dest=dest: v.tensor_copy(
                dest[:, 512 * pc:512 * (pc + 1)], psum["pbr"][:, 0:512]),
                after=[mm], name=f"nw_ev{j}_{pc}")
            U["pbr"] = ev
            last = ev
        nwevs[j] = last
    wo_done = None
    for i in range(ND):
        ld = p.dma("wcl", lambda e, i=i: e.dma_start(
            out=sb["wcast"][:, 0:1024], in_=wo_ap[128 * i:128 * (i + 1), :]),
            after=[wo_done], name=f"wo_ld{i}")
        wo_done = p.op("vector", lambda v, i=i: v.tensor_copy(
            sb["wo_bf"][:, i, :], sb["wcast"][:, 0:1024]), after=[ld], name=f"wo_c{i}")


    def build_xn_tile(t):
        sl = t % 2
        ld = p.dma(f"xn{sl}", lambda e, t=t, sl=sl: e.dma_start(
            out=sb["xbuf"][:, sl, :], in_=x_ap[128 * t:128 * (t + 1), :]),
            after=[U["xslot"][sl], fac], name=f"xn_ld{t}")
        a1 = p.op("scalar", lambda s, t=t, sl=sl: s.activation(
            sb["xq_bf"][:], sb["xbuf"][:, sl, :], AF.Identity,
            bias=sb["st_bg"][:, 4 * t:4 * t + 1], scale=sb["st_g"][:, 4 * t:4 * t + 1]),
            after=[ld, U["xn_tp"]], name=f"xn_a{t}")
        U["xslot"][sl] = a1
        tp = p.op("tensor", lambda te: [te.transpose(
            psum["ptp"][:, dc, :], sb["xq_bf"][:, 128 * dc:128 * (dc + 1)],
            sb["c_identbf"][:]) for dc in range(ND)][-1],
            after=[a1, U["ptp"], cevs[0], U["xtp"]], name=f"xn_tpq{t}")
        ev = p.op("vector", lambda v: v.tensor_copy(sb["xtp"][:, :, :], psum["ptp"][:, :, :]),
                  after=[tp], name=f"xn_evq{t}")
        U["xn_tp"] = tp
        U["ptp"] = ev
        return ev

    vz = p.op("vector", lambda v: v.memset(sb["vaug"][:, :, :, 64:65], 1.0), name="v_ones")

    def load_w(name, wap, cols, prev):
        last = None
        for i in range(ND):
            ld = p.dma("wcl", lambda e, i=i, wap=wap, cols=cols: e.dma_start(
                out=sb["wcast"][:, 0:cols], in_=wap[128 * i:128 * (i + 1), :]),
                after=[prev], name=f"wl_{name}{i}")
            cvt = p.op("vector", lambda v, i=i, cols=cols: v.tensor_copy(
                sb["wcur"][:, i, 0:cols], sb["wcast"][:, 0:cols]),
                after=[ld, U["wcur"]], name=f"wc_{name}{i}")
            prev = cvt
            last = cvt
        return last

    qkv_last = None
    wprev = wo_done
    rn_prev = None
    for phase, (wap, cols, bj, nrep, dstT) in enumerate(
            [(wq_ap, 1040, 0, "qnrep", "qnT"), (wk_ap, 1024, 1, "knrep", "knT"),
             (wv_ap, 1024, 2, None, None)]):
        wld = load_w("qkv%d" % phase, wap, cols, wprev)
        wprev = wld
        load_bias(bj, extra=[qkv_last])
        last_mm = None
        for t in range(NT):
            xe = build_xn_tile(t)
            npieces = [(0, 512), (512, 512)] + ([(1024, 16)] if cols == 1040 else [])

            def qmm(te, t=t, npieces=npieces, bj=bj):
                for off, n in npieces:
                    tgt = psum["psA"][:, off:off + n] if off < 1024 else psum["pbr"][:, 0:16]
                    for dc in range(ND):
                        te.matmul(tgt, sb["xtp"][:, dc, :],
                                  sb["wcur"][:, dc, off:off + n], start=(dc == 0), stop=False)
                    mm = te.matmul(tgt, sb["c_ones1bf"][:, :], sb["brow"][0:1, 0, off:off + n],
                                   start=False, stop=True)
                return mm
            mm = p.op("tensor", qmm,
                      after=[xe, wld, bevs[bj], cevs[2], U["psA"]] +
                            ([U["pbr"]] if cols == 1040 else []),
                      name=f"qkv{phase}_mm{t}")
            U["xtp"] = mm
            last_mm = mm
            if phase == 2:
                ev = p.op("vector", lambda v, t=t: v.tensor_copy(
                    sb["vaug"][:, t, :, 0:64],
                    psum["psA"][:, 0:1024].rearrange("p (h i) -> p h i", i=64)),
                    after=[mm, vz, U["vaug"]], name=f"vev{t}")
                U["psA"] = ev
                U["vaug"] = None
                qkv_last = ev
                continue
            if phase == 0:
                gs = p.op("scalar", lambda s, t=t: s.activation(
                    sb["gsig"][:, t, :], psum["pbr"][:, 0:16], AF.Sigmoid),
                    after=[mm], name=f"gsig{t}")
                U["pbr"] = gs
            sq = p.op("scalar", lambda s: s.activation(
                sb["scr_sq"][:, 0:1024], psum["psA"][:, 0:1024], AF.Square),
                after=[mm, rn_prev], name=f"rn_sq{phase}_{t}")
            sg = p.op("vector", lambda v: v.tensor_reduce(
                sb["stt"][:, 0:16], sb["scr_sq"][:, 0:1024].rearrange("p (h i) -> p h i", i=64),
                AX.X, OP.add), after=[sq, rn_prev], name=f"rn_red{phase}_{t}", small=True)
            s1 = p.op("gpsimd", lambda g: g.tensor_scalar(
                sb["stt"][:, 16:32], sb["stt"][:, 0:16], 1.0 / 64.0, 1e-6, OP.mult, OP.add),
                after=[sg], name=f"rn_m{phase}_{t}", small=True)
            s2 = p.op("scalar", lambda sc: sc.activation(
                sb["stt"][:, 32:48], sb["stt"][:, 16:32], AF.Ln),
                after=[s1], name=f"rn_l{phase}_{t}", small=True)
            s3 = p.op("scalar", lambda sc: sc.activation(
                sb["stt"][:, 48:64], sb["stt"][:, 32:48], AF.Exp, scale=-0.5),
                after=[s2], name=f"rn_e{phase}_{t}", small=True)

            def rnmul(v):
                for h in range(H):
                    r = v.tensor_scalar(sb["qn_s"][:, 64 * h:64 * (h + 1)],
                                        psum["psA"][:, 64 * h:64 * (h + 1)],
                                        sb["stt"][:, 48 + h:49 + h], None, OP.mult)
                return r
            s4 = p.op("vector", rnmul, after=[s3, U["qn_s"]], name=f"rn_x{phase}_{t}")
            U["psA"] = s4
            s5 = p.op("vector", lambda v, nrep=nrep: v.tensor_tensor(
                sb["xq_bf"][:, 0:1024], sb["qn_s"][:], sb[nrep][:], OP.mult),
                after=[s4, nwevs[0 if nrep == "qnrep" else 1], U["xn_tp"]],
                name=f"rn_y{phase}_{t}")
            U["qn_s"] = s5
            rn_prev = s4
            tp = p.op("tensor", lambda te: [te.transpose(
                psum["ptp"][:, dc, :], sb["xq_bf"][:, 128 * dc:128 * (dc + 1)],
                sb["c_identbf"][:]) for dc in range(ND)][-1],
                after=[s5, U["ptp"]], name=f"rn_tp{phase}_{t}")
            U["xn_tp"] = tp
            ev = p.op("vector", lambda v, t=t, dstT=dstT: v.tensor_copy(
                sb[dstT][:, :, 128 * t:128 * (t + 1)], psum["ptp"][:, :, :]),
                after=[tp, U[dstT]], name=f"rn_ev{phase}_{t}")
            U["ptp"] = ev
            qkv_last = ev
        U["wcur"] = last_mm
    load_bias(3, extra=[qkv_last])
    qkv_bar = qkv_last

    U["psA0"] = U["psA"]
    U["psA1"] = U["psA"]
    U["psB0"] = U["psB"]
    U["psB1"] = U["psB"]
    stat_state = {"last": None}

    def emit_stats_qt(qt):
        stat_last = stat_state["last"]
        for h in range(H):
            rb = 64 * (h % 2)
            ch = h // 2
            for half, ps in ((0, "psA"), (1, "psB")):
                def smm2(te, rb=rb, ch=ch, qt=qt, half=half, ps=ps):
                    for pc in range(2):
                        mm = te.matmul(psum[ps][:, 512 * pc:512 * (pc + 1)],
                                       sb["qnT"][rb:rb + 64, ch, 128 * qt:128 * (qt + 1)],
                                       sb["knT"][rb:rb + 64, ch,
                                                 1024 * half + 512 * pc:1024 * half + 512 * (pc + 1)],
                                       start=True, stop=True)
                    return mm
                mm = p.op("tensor", smm2, after=[qkv_bar, U[ps + "0"], U[ps + "1"]],
                          name=f"st_mm{qt}_{h}_{half}")
                rd = p.op("vector", lambda v, ps=ps, half=half, h=h: v.tensor_reduce(
                    sb["mqt"][:, 8 * h + 4 * half:8 * h + 4 * half + 1], psum[ps][:, :],
                    AX.X, OP.max, negate=True),
                    after=[mm, U["mqt"]], name=f"st_rd{qt}_{h}_{half}", small=True)
                U[ps + "0"] = rd
                U[ps + "1"] = rd
                stat_last = rd
        mva = sb["mqt"][:, 0:128].rearrange("p (t u) -> p t u", u=8)[:, :, 0:1]
        mvb = sb["mqt"][:, 4:132].rearrange("p (t u) -> p t u", u=8)[:, :, 0:1]
        cmb = p.op("vector", lambda v, mva=mva, mvb=mvb: v.tensor_tensor(
            mva, mva, mvb, OP.min),
            after=[U["psA0"], U["psB0"]], name=f"st_cmb{qt}", small=True)
        tqv = sb["mqt"][:, 0:128].rearrange("p (t u) -> p t u", u=8)[:, :, 0:1].rearrange(
            "p t u -> p (t u)")
        tp = p.op("tensor", lambda te, tqv=tqv: te.transpose(
            psum["pbr"][0:16, 0:128], tqv, sb["c_identf"][:]),
            after=[cmb, U["pbr"], cevs[1]], name=f"st_tp{qt}")
        ev1 = p.op("vector", lambda v, qt=qt: v.tensor_copy(
            sb["negmT"][0:16, 128 * qt:128 * (qt + 1)], psum["pbr"][0:16, 0:128]),
            after=[tp, U["negmT"]], name=f"st_ev{qt}")
        ev2 = p.op("vector", lambda v, qt=qt: v.tensor_copy(
            sb["negmT"][64:80, 128 * qt:128 * (qt + 1)], psum["pbr"][0:16, 0:128]),
            after=[ev1], name=f"st_ev2{qt}")
        U["pbr"] = ev2
        U["mqt"] = tp
        stat_state["last"] = ev2
        return ev2

    U["psC0"] = None
    U["psC1"] = None
    y_stores = []
    prev_ep = [None, None]
    for sp in range(8):
        q0 = 256 * sp
        emit_stats_qt(2 * sp)
        negm_bar = emit_stats_qt(2 * sp + 1)
        for hp in range(H // 2):
            ch = hp
            exp_evs = {0: [], 1: []}
            for kt in range(NT):
                rot = kt % 2
                pst = "psB" if rot == 0 else "psC"
                for half, rb in ((0, 0), (1, 64)):
                    h = 2 * hp + half

                    def stmm(te, rb=rb, ch=ch, kt=kt, q0=q0, pst=pst, h=h, half=half):
                        te.matmul(psum[pst][:, 512 * half:512 * half + 256],
                                  sb["knT"][rb:rb + 64, ch, 128 * kt:128 * (kt + 1)],
                                  sb["qnT"][rb:rb + 64, ch, q0:q0 + 256],
                                  start=True, stop=False)
                        return te.matmul(psum[pst][:, 512 * half:512 * half + 256],
                                         sb["c_sels"][rb:rb + 16, h, :],
                                         sb["negmT"][rb:rb + 16, q0:q0 + 256],
                                         start=False, stop=True)
                    mm = p.op("tensor", stmm, after=[negm_bar, U[pst + str(half)], cevs[4]],
                              name=f"sT{sp}_{h}_{kt}")
                    ex = p.op("scalar", lambda s, kt=kt, pst=pst, half=half, hb=half: s.activation(
                        sb["eT%d" % hb][:, kt, :], psum[pst][:, 512 * half:512 * half + 256],
                        AF.Exp, scale=ISQ), after=[mm, U["eT%d" % half]],
                        name=f"sTe{sp}_{h}_{kt}")
                    U[pst + str(half)] = ex
                    exp_evs[half].append(ex)
            for half in (0, 1):
                h = 2 * hp + half
                pva = half

                def pvmm(te, h=h, pva=pva, half=half):
                    for kt in range(NT):
                        mm = te.matmul(psum["psA"][0:65, 512 * pva:512 * pva + 256],
                                       sb["vaug"][:, kt, h, :],
                                       sb["eT%d" % half][:, kt, :], start=(kt == 0), stop=(kt == 15))
                    return mm
                pv = p.op("tensor", pvmm, after=exp_evs[half] + [U["psA%d" % pva], qkv_bar],
                          name=f"pv{sp}_{h}")
                U["eT%d" % half] = pv
                ob = p.op("vector", lambda v, pva=pva: v.tensor_copy(
                    sb["ob65"][:, :], psum["psA"][0:65, 512 * pva:512 * pva + 256]),
                    after=[pv, U["ob65"]], name=f"obev{sp}_{h}")
                U["psA%d" % pva] = ob
                for tt2 in range(2):
                    tp = p.op("tensor", lambda te, tt2=tt2: te.transpose(
                        psum["ptp"][:, 0, 0:65],
                        sb["ob65"][:, 128 * tt2:128 * (tt2 + 1)], sb["c_identbf"][0:65, 0:65]),
                        after=[ob, U["ptp"]], name=f"otp{sp}_{h}_{tt2}")
                    ev = p.op("vector", lambda v, tt2=tt2, h=h: v.tensor_copy(
                        sb["otokS"][:, tt2, h, :], psum["ptp"][:, 0, 0:65]),
                        after=[tp, U["otokS"]], name=f"otev{sp}_{h}_{tt2}")
                    U["ptp"] = ev
                    U["ob65"] = tp
                    U["otokS_last"] = ev
        ot_bar = U["otokS_last"]
        of_evs = []
        for tt2 in range(2):
            gt = 2 * sp + tt2
            rsv = sb["otokS"][:, tt2, :, 64:65]
            rc = p.op("vector", lambda v, tt2=tt2, rsv=rsv: v.reciprocal(
                sb["fcol"][:, tt2, :], rsv), after=[ot_bar], name=f"frec{sp}_{tt2}", small=True)
            fm = p.op("gpsimd", lambda g, tt2=tt2, gt=gt: g.tensor_tensor(
                sb["fcol"][:, tt2, :], sb["fcol"][:, tt2, :], sb["gsig"][:, gt, :], OP.mult),
                after=[rc, qkv_bar], name=f"fmul{sp}_{tt2}", small=True)

            def fscale(v, tt2=tt2):
                for h in range(H):
                    r = v.tensor_scalar(sb["xq_bf"][:, 64 * h:64 * (h + 1)],
                                        sb["otokS"][:, tt2, h, 0:64],
                                        sb["fcol"][:, tt2, h:h + 1], None, OP.mult)
                return r
            fs = p.op("vector", fscale, after=[fm, ot_bar, U.get("xqo")], name=f"fsc{sp}_{tt2}")
            tpf = p.op("tensor", lambda te: [te.transpose(
                psum["ptp"][:, dc, :],
                sb["xq_bf"][:, 128 * dc:128 * (dc + 1)],
                sb["c_identbf"][:]) for dc in range(ND)][-1],
                after=[fs, U["ptp"]], name=f"ftp{sp}_{tt2}")
            U["xqo"] = tpf
            fe = p.op("vector", lambda v, tt2=tt2: v.tensor_copy(
                sb["ofT"][:, :, 128 * tt2:128 * (tt2 + 1)], psum["ptp"][:, :, :]),
                after=[tpf, U["ofT"]], name=f"fev{sp}_{tt2}")
            U["ptp"] = fe
            U["otokS"] = fs
            of_evs.append(fe)
        for tt2 in range(2):
            gt = 2 * sp + tt2

            def omm(te, tt2=tt2):
                for pc in range(2):
                    for dc in range(ND):
                        te.matmul(psum["psB"][:, 512 * pc:512 * (pc + 1)],
                                  sb["ofT"][:, dc, 128 * tt2:128 * (tt2 + 1)],
                                  sb["wo_bf"][:, dc, 512 * pc:512 * (pc + 1)],
                                  start=(dc == 0), stop=False)
                    mm = te.matmul(psum["psB"][:, 512 * pc:512 * (pc + 1)],
                                   sb["c_ones1bf"][:, :],
                                   sb["brow"][0:1, 0, 512 * pc:512 * (pc + 1)],
                                   start=False, stop=True)
                return mm
            m2 = p.op("tensor", omm, after=of_evs + [wo_done, bevs[3], U["psB0"], U["psB1"], U["psB"]], name=f"omm{gt}")
            U["ofT"] = m2
            sl = gt % 2
            lx = p.dma(f"xe{sl}", lambda e, gt=gt, sl=sl: e.dma_start(
                out=sb["xbuf"][:, sl, :], in_=x_ap[128 * gt:128 * (gt + 1), :]),
                after=[prev_ep[0], prev_ep[1], U["xslot"][sl]], name=f"xe_ld{gt}")
            ep = p.op("vector", lambda v, sl=sl: v.scalar_tensor_tensor(
                sb["xbuf"][:, 1 - sl, :], psum["psB"][:, 0:1024], 1.0,
                sb["xbuf"][:, sl, :], OP.mult, OP.add),
                after=[m2, lx, prev_ep[0], prev_ep[1], U["xslot"][1 - sl]], name=f"aep{gt}")
            U["psB"] = ep
            U["psB0"] = ep
            U["psB1"] = ep
            U["psC0"] = ep
            U["psC1"] = ep
            U["psA"] = ep
            U["xslot"][sl] = ep
            U["xslot"][1 - sl] = ep
            st = p.dma(f"yst{sl}", lambda e, gt=gt, sl=sl: e.dma_start(
                out=y_ap[128 * gt:128 * (gt + 1), :], in_=sb["xbuf"][:, 1 - sl, :]),
                after=[ep], name=f"y_st{gt}")
            prev_ep[sl] = st
            y_stores.append(st)

    p.barrier("sync", y_stores, name=tag + "final")
    stack.close()
    p.seal(name=tag + "seal")

def build_attn1(debug=False):
    nc = bass.Bass("TRN2", target_bir_lowering=False)
    aps = {}
    aps["x"] = nc.dram_tensor("x", [SEQ, D], F32, kind="ExternalInput").ap()
    aps["y"] = nc.dram_tensor("y", [SEQ, D], F32, kind="ExternalOutput").ap()
    aps["Wq"] = nc.dram_tensor("Wq", [D, 1040], F32, kind="ExternalInput").ap()
    aps["Wk"] = nc.dram_tensor("Wk", [D, D], F32, kind="ExternalInput").ap()
    aps["Wv"] = nc.dram_tensor("Wv", [D, D], F32, kind="ExternalInput").ap()
    aps["Wo"] = nc.dram_tensor("Wo", [D, D], F32, kind="ExternalInput").ap()
    aps["bq"] = nc.dram_tensor("bq", [1, 1040], F32, kind="ExternalInput").ap()
    aps["bk"] = nc.dram_tensor("bk", [1, D], F32, kind="ExternalInput").ap()
    aps["bv"] = nc.dram_tensor("bv", [1, D], F32, kind="ExternalInput").ap()
    aps["bo"] = nc.dram_tensor("bo", [1, D], F32, kind="ExternalInput").ap()
    aps["qnw_t"] = nc.dram_tensor("qnw_t", [1, D], F32, kind="ExternalInput").ap()
    aps["knw_t"] = nc.dram_tensor("knw_t", [1, D], F32, kind="ExternalInput").ap()
    cs = declare_consts(nc)
    outer = contextlib.ExitStack()
    sbc = alloc_consts(nc, outer)
    p = Prog()
    cevs = load_consts(p, cs, sbc)
    emit_attn1(nc, p, sbc, cevs, aps, "a1_", debug=debug)
    p.emit(nc, outer)
    return nc



def run_attn1(xs, inputs, consts):
    nc = build_attn1()
    w = {"Wq": np.ascontiguousarray(np.asarray(inputs["L1_attn_Wq"], np.float32)),
         "Wk": np.ascontiguousarray(np.asarray(inputs["L1_attn_Wk"], np.float32)),
         "Wv": np.ascontiguousarray(np.asarray(inputs["L1_attn_Wv"], np.float32)),
         "Wo": np.ascontiguousarray(np.asarray(inputs["L1_attn_Wo"], np.float32)),
         "bq": np.asarray(inputs["L1_attn_bq"], np.float32).reshape(1, 1040),
         "bk": np.asarray(inputs["L1_attn_bk"], np.float32).reshape(1, D),
         "bv": np.asarray(inputs["L1_attn_bv"], np.float32).reshape(1, D),
         "bo": np.asarray(inputs["L1_attn_bo"], np.float32).reshape(1, D),
         "qnw_t": np.ascontiguousarray(np.tile(np.asarray(inputs["L1_attn_qn_w"], np.float32), 16).reshape(1, D)),
         "knw_t": np.ascontiguousarray(np.tile(np.asarray(inputs["L1_attn_kn_w"], np.float32), 16).reshape(1, D))}
    in_maps = []
    for c in range(NCORES):
        m = {"x": xs[c]}
        m.update(w)
        m.update(consts)
        in_maps.append(m)
    res = run_bass_kernel_spmd(nc, in_maps, core_ids=list(range(NCORES)))
    return [res.results[c]["y"] for c in range(NCORES)]



def build_fused():
    nc = bass.Bass("TRN2", target_bir_lowering=False)
    F = F32
    ap = {}
    ap["x"] = nc.dram_tensor("x", [SEQ, D], F, kind="ExternalInput").ap()
    ap["y"] = nc.dram_tensor("y", [SEQ, D], F, kind="ExternalOutput").ap()
    xs0 = nc.dram_tensor("xscr0", [SEQ, D], F).ap()
    xs1 = nc.dram_tensor("xscr1", [SEQ, D], F).ap()
    ap["w1T"] = nc.dram_tensor("w1T", [D, DH], F, kind="ExternalInput").ap()
    ap["w2T"] = nc.dram_tensor("w2T", [DH, D], F, kind="ExternalInput").ap()
    ap["cw"] = nc.dram_tensor("cw", [128, 96], F, kind="ExternalInput").ap()
    ap["cb"] = nc.dram_tensor("cb", [128, 32], F, kind="ExternalInput").ap()
    ap["wq12"] = nc.dram_tensor("wq12", [D, 2 * D], F, kind="ExternalInput").ap()
    ap["wk12"] = nc.dram_tensor("wk12", [D, 2 * D], F, kind="ExternalInput").ap()
    ap["wv"] = nc.dram_tensor("wv", [D, D], F, kind="ExternalInput").ap()
    ap["lam4"] = nc.dram_tensor("lam4", [128, 4, 8], F, kind="ExternalInput").ap()
    ap["kern"] = nc.dram_tensor("kern", [D, D], F, kind="ExternalInput").ap()
    ap["bias"] = nc.dram_tensor("bias", [1, D], F, kind="ExternalInput").ap()
    ap["la"] = nc.dram_tensor("la", [D, 16], F, kind="ExternalInput").ap()
    ap["lb"] = nc.dram_tensor("lb", [16, D], F, kind="ExternalInput").ap()
    ap["Wq"] = nc.dram_tensor("Wq", [D, 1040], F, kind="ExternalInput").ap()
    ap["Wk"] = nc.dram_tensor("Wk", [D, D], F, kind="ExternalInput").ap()
    ap["Wv"] = nc.dram_tensor("Wv", [D, D], F, kind="ExternalInput").ap()
    ap["Wo"] = nc.dram_tensor("Wo", [D, D], F, kind="ExternalInput").ap()
    ap["bq"] = nc.dram_tensor("bq", [1, 1040], F, kind="ExternalInput").ap()
    ap["bk"] = nc.dram_tensor("bk", [1, D], F, kind="ExternalInput").ap()
    ap["bv"] = nc.dram_tensor("bv", [1, D], F, kind="ExternalInput").ap()
    ap["bo"] = nc.dram_tensor("bo", [1, D], F, kind="ExternalInput").ap()
    ap["qnw_t"] = nc.dram_tensor("qnw_t", [1, D], F, kind="ExternalInput").ap()
    ap["knw_t"] = nc.dram_tensor("knw_t", [1, D], F, kind="ExternalInput").ap()
    cs = declare_consts(nc)
    outer = contextlib.ExitStack()
    sbc = alloc_consts(nc, outer)
    p = Prog()
    cevs = load_consts(p, cs, sbc)
    chain = [ap["x"], xs0, xs1, xs0, xs1, xs0, ap["y"]]

    def io(i):
        d = dict(ap)
        d["x"], d["y"] = chain[i], chain[i + 1]
        return d
    sh0 = emit_ffn0(nc, p, sbc, cevs, io(0), "s1_")
    emit_attn0(nc, p, sbc, cevs, io(1), "s2_")
    emit_ffn0(nc, p, sbc, cevs, io(2), "s3_", shared=sh0)
    sh1 = emit_ffn1(nc, p, sbc, cevs, io(3), "s4_")
    emit_attn1(nc, p, sbc, cevs, io(4), "s5_")
    emit_ffn1(nc, p, sbc, cevs, io(5), "s6_", shared=sh1)
    p.emit(nc, outer)
    return nc


def _fused_inputs(inputs):
    w = _layout_ffn0_weights(inputs)
    lam4 = np.stack([np.asarray(inputs[k], np.float32).reshape(128, 8) for k in
                     ("L0_attn_lambda_q1", "L0_attn_lambda_k1",
                      "L0_attn_lambda_q2", "L0_attn_lambda_k2")], axis=1)
    w.update({
        "wq12": np.ascontiguousarray(np.asarray(inputs["L0_attn_w_q12"], np.float32)),
        "wk12": np.ascontiguousarray(np.asarray(inputs["L0_attn_w_k12"], np.float32)),
        "wv": np.ascontiguousarray(np.asarray(inputs["L0_attn_w_v"], np.float32)),
        "lam4": np.ascontiguousarray(lam4),
        "kern": np.ascontiguousarray(np.asarray(inputs["L1_ffn_kernel"], np.float32)),
        "bias": np.ascontiguousarray(np.asarray(inputs["L1_ffn_bias"], np.float32).reshape(1, D)),
        "la": np.ascontiguousarray(np.asarray(inputs["L1_ffn_a"], np.float32)),
        "lb": np.ascontiguousarray(np.asarray(inputs["L1_ffn_b"], np.float32)),
        "Wq": np.ascontiguousarray(np.asarray(inputs["L1_attn_Wq"], np.float32)),
        "Wk": np.ascontiguousarray(np.asarray(inputs["L1_attn_Wk"], np.float32)),
        "Wv": np.ascontiguousarray(np.asarray(inputs["L1_attn_Wv"], np.float32)),
        "Wo": np.ascontiguousarray(np.asarray(inputs["L1_attn_Wo"], np.float32)),
        "bq": np.asarray(inputs["L1_attn_bq"], np.float32).reshape(1, 1040),
        "bk": np.asarray(inputs["L1_attn_bk"], np.float32).reshape(1, D),
        "bv": np.asarray(inputs["L1_attn_bv"], np.float32).reshape(1, D),
        "bo": np.asarray(inputs["L1_attn_bo"], np.float32).reshape(1, D),
        "qnw_t": np.ascontiguousarray(np.tile(np.asarray(inputs["L1_attn_qn_w"], np.float32), 16).reshape(1, D)),
        "knw_t": np.ascontiguousarray(np.tile(np.asarray(inputs["L1_attn_kn_w"], np.float32), 16).reshape(1, D)),
    })
    w.update(host_consts())
    return w



def _layout_ffn0_weights(inputs):
    w1T = np.ascontiguousarray(inputs["L0_ffn_w1"].T).astype(np.float32)
    w2T = np.ascontiguousarray(inputs["L0_ffn_w2"].T).astype(np.float32)
    cw = np.ascontiguousarray(
        np.asarray(inputs["L0_ffn_conv_w"]).reshape(32, 128, 3).transpose(1, 0, 2).reshape(128, 96)
    ).astype(np.float32)
    cb = np.ascontiguousarray(np.asarray(inputs["L0_ffn_conv_b"]).reshape(32, 128).T).astype(np.float32)
    return {"w1T": w1T, "w2T": w2T, "cw": cw, "cb": cb}


def run_ffn0(xs, inputs, consts):
    nc = build_ffn0()
    w = _layout_ffn0_weights(inputs)
    in_maps = []
    for c in range(NCORES):
        m = {"x": xs[c]}
        m.update(w)
        m.update(consts)
        in_maps.append(m)
    res = run_bass_kernel_spmd(nc, in_maps, core_ids=list(range(NCORES)))
    return [res.results[c]["y"] for c in range(NCORES)]


def kernel(**inputs):
    x = np.ascontiguousarray(np.asarray(inputs["L0_attn_x"], dtype=np.float32))
    nc = build_fused()
    w = _fused_inputs(inputs)
    in_maps = []
    for c in range(NCORES):
        m = {"x": np.ascontiguousarray(x[c])}
        m.update(w)
        in_maps.append(m)
    res = run_bass_kernel_spmd(nc, in_maps, core_ids=list(range(NCORES)))
    return np.stack([res.results[c]["y"] for c in range(NCORES)]).reshape(NCORES, SEQ, D)


# version 14
# speedup vs baseline: 1.0531x; 1.0118x over previous
import contextlib
import numpy as np
import ml_dtypes
import concourse.bass as bass
import concourse.mybir as mybir
from concourse.alu_op_type import AluOpType as OP
from concourse.bass_utils import run_bass_kernel_spmd

F32 = mybir.dt.float32
BF16 = mybir.dt.bfloat16
F16 = mybir.dt.float16
AF = mybir.ActivationFunctionType
AX = mybir.AxisListType

MAGIC = float(1.5 * 2 ** 23)
NCORES = 8
SEQ = 2048
D = 1024
DH = 4096
NT = SEQ // 128
ND = D // 128
NH = DH // 128
ENGS = ("sync", "tensor", "vector", "scalar", "gpsimd")


class _Op:
    __slots__ = ("eng", "fn", "after", "kind", "key", "sig", "val", "name", "small")

    def __init__(self, eng, fn, after, kind, key, name, small=False):
        flat = []
        for a in after:
            if a is None:
                continue
            if isinstance(a, (list, tuple)):
                flat.extend(x for x in a if x is not None)
            else:
                flat.append(a)
        self.eng, self.fn, self.after, self.kind, self.key, self.name = (
            eng, fn, tuple(flat), kind, key, name)
        self.sig = False
        self.val = None
        self.small = small


class Prog:

    def __init__(self):
        self.seq = {e: [] for e in ENGS}
        self.all = []
        self.last_dma = {}

    def op(self, eng, fn, after=(), name="", small=False):
        o = _Op(eng, fn, after, "c", None, name, small)
        self.seq[eng].append(o)
        self.all.append(o)
        return o

    def dma(self, key, fn, after=(), eng="sync", name=""):
        o = _Op(eng, fn, after, "d", key, name)
        self.seq[eng].append(o)
        self.all.append(o)
        self.last_dma[key] = o
        return o

    def seal(self, name="seal"):
        lasts = [self.seq[e][-1] for e in ENGS if self.seq[e]]
        lasts += list(self.last_dma.values())
        for e in ENGS:
            self.op(e, lambda eng: eng.nop(), after=lasts, name=f"{name}_{e}")

    def barrier(self, eng, after, name="bar"):
        return self.op(eng, lambda e: e.nop(), after, name)

    def emit(self, nc, stack):
        for o in self.all:
            if o.kind == "d":
                o.sig = True
            for d in o.after:
                d.sig = True
        sems = {}
        for e in ENGS:
            if any(o.sig and o.kind == "c" for o in self.seq[e]):
                sems["c:" + e] = stack.enter_context(nc.semaphore(f"se_{e}"))
        dkeys = sorted({o.key for o in self.all if o.kind == "d" and o.sig})
        for k in dkeys:
            sems["d:" + k] = stack.enter_context(nc.semaphore(f"sd_{k}"))
        ccount = {e: 0 for e in ENGS}
        dcount = {}
        for e in ENGS:
            for o in self.seq[e]:
                if not o.sig:
                    continue
                if o.kind == "c":
                    ccount[e] += 1
                    o.val = ("c:" + e, ccount[e], 1)
                else:
                    dcount[o.key] = dcount.get(o.key, 0) + 16
                    o.val = ("d:" + o.key, dcount[o.key], 16)
        for o in self.all:
            for d in o.after:
                assert d.val is not None, f"dep of {o.name} not assigned"

        block = stack.enter_context(nc.Block())
        prog = self

        def mk_body(ename):
            def body(eng):
                waited = {}
                for o in prog.seq[ename]:
                    for d in o.after:
                        if d.eng == ename and d.kind == "c" and not d.small:
                            continue
                        sname, v, _ = d.val
                        if waited.get(sname, 0) >= v:
                            continue
                        eng.wait_ge(sems[sname], v)
                        waited[sname] = v
                    inst = o.fn(eng) if o.fn is not None else None
                    if o.sig:
                        assert inst is not None, f"signaling op {o.name} has no inst"
                        sname, _, inc = o.val
                        inst.then_inc(sems[sname], inc)
            return body

        for e in ENGS:
            if self.seq[e]:
                getattr(block, e)(mk_body(e))
        stack.close()



def host_consts():
    sels = np.zeros((128, 16, 128), dtype=ml_dtypes.bfloat16)
    for j in range(16):
        sels[j, j, :] = 1.0
        sels[64 + j, j, :] = 1.0
    selsf4 = np.zeros((128, 4, 128), dtype=np.float32)
    for j in range(4):
        selsf4[j, j, :] = 1.0
    return {"identbf": np.eye(128, dtype=ml_dtypes.bfloat16),
            "identf": np.eye(128, dtype=np.float32),
            "ones1bf": np.ones((1, 128), dtype=ml_dtypes.bfloat16),
            "ones1f": np.ones((1, 128), dtype=np.float32),
            "sels": sels, "selsf4": selsf4}


CONST_SPECS = [("identbf", [128, 128], BF16), ("identf", [128, 128], F32),
               ("ones1bf", [1, 128], BF16), ("ones1f", [1, 128], F32),
               ("sels", [128, 16, 128], BF16), ("selsf4", [128, 4, 128], F32)]


def declare_consts(nc, names=None):
    return {name: nc.dram_tensor(name, shape, dt, kind="ExternalInput").ap()
            for name, shape, dt in CONST_SPECS if names is None or name in names}


def alloc_consts(nc, stack, names=None):
    return {"c_" + name: stack.enter_context(nc.sbuf_tensor("c_" + name, shape, dt))
            for name, shape, dt in CONST_SPECS if names is None or name in names}


def load_consts(p, cs, sb):
    evs = {}
    for i, (name, _, _) in enumerate(CONST_SPECS):
        if name in cs:
            evs[i] = p.dma("c_" + name, lambda e, n=name: e.dma_start(
                out=sb["c_" + n][:], in_=cs[n]), name=f"ld_{name}")
        else:
            evs[i] = None
    return evs



def emit_xphase(p, sb, psum, x_ap, quant, cident, eps=1e-5):
    xs = sb["xbuf"]
    ptp = psum["ptp"]
    zp = p.op("vector", lambda v: v.memset(sb["xt"][:, :, 0:1], 0.0), name="xtpad0")
    zp2 = p.op("vector", lambda v: v.memset(sb["xt"][:, :, 2049:2050], 0.0), after=[zp])
    slot_free = [None, None]
    prev_tp = None
    prev_evac = zp2
    prev_c1b = None
    evac_list = []
    for t in range(NT):
        sl = t % 2
        C = lambda buf, off, t: sb[buf][:, off + 4 * t:off + 4 * t + 1]
        ld = p.dma(f"xs{sl}", lambda e, t=t, sl=sl: e.dma_start(
            out=xs[:, sl, :], in_=x_ap[128 * t:128 * (t + 1), :]),
            after=[slot_free[sl]], name=f"x_ld{t}")
        oa = p.op("scalar", lambda s, sl=sl, t=t: s.activation(
            sb["scr_sq"][:, 0:1024], xs[:, sl, :], AF.Square,
            accum_out=C("st_ssq", 0, t)), after=[ld, prev_c1b], name=f"xssq{t}")

        def dv(v, t=t, sl=sl):
            r = v.tensor_reduce(C("st_sum", 0, t), xs[:, sl, :], AX.X, OP.add)
            if quant:
                v.tensor_reduce(C("st_max", 0, t), xs[:, sl, :], AX.X, OP.max)
                r = v.tensor_reduce(C("st_min", 0, t), xs[:, sl, :], AX.X, OP.min)
            return r
        od = p.op("vector", dv, after=[ld], name=f"xred{t}", small=True)
        c1 = p.op("gpsimd", lambda g, t=t: g.tensor_scalar(C("st_m", 0, t), C("st_sum", 0, t),
                  1.0 / D, None, OP.mult), after=[od], name=f"xm{t}", small=True)
        c2 = p.op("vector", lambda v, t=t: v.tensor_tensor(C("scr_s", 0, t), C("st_m", 0, t),
                  C("st_m", 0, t), OP.mult), after=[c1], name=f"xm2{t}", small=True)
        c3 = p.op("gpsimd", lambda g, t=t, eps=eps: g.tensor_scalar(C("scr_s", 64, t),
                  C("st_ssq", 0, t), 1.0 / D, eps, OP.mult, OP.add),
                  after=[oa, c1], name=f"xv1{t}", small=True)
        c4 = p.op("vector", lambda v, t=t: v.tensor_tensor(C("scr_s", 64, t), C("scr_s", 64, t),
                  C("scr_s", 0, t), OP.subtract), after=[c2, c3], name=f"xv2{t}", small=True)
        c5 = p.op("scalar", lambda sc, t=t: sc.activation(C("scr_s", 0, t), C("scr_s", 64, t),
                  AF.Ln), after=[c4], name=f"xln{t}", small=True)
        c6 = p.op("scalar", lambda sc, t=t: sc.activation(C("st_rstd", 0, t), C("scr_s", 0, t),
                  AF.Exp, scale=-0.5), after=[c5], name=f"xrstd{t}", small=True)
        if quant:
            c7 = p.op("vector", lambda v, t=t: v.tensor_tensor(C("scr_s", 128, t),
                      C("st_max", 0, t), C("st_m", 0, t), OP.subtract),
                      after=[c1], name=f"xd1{t}", small=True)
            c8 = p.op("gpsimd", lambda g, t=t: g.tensor_tensor(C("scr_s", 192, t),
                      C("st_m", 0, t), C("st_min", 0, t), OP.subtract),
                      after=[c1], name=f"xd2{t}", small=True)
            c9 = p.op("vector", lambda v, t=t: v.tensor_tensor(C("scr_s", 128, t),
                      C("scr_s", 128, t), C("scr_s", 192, t), OP.max),
                      after=[c7, c8], name=f"xd3{t}", small=True)
            ca = p.op("gpsimd", lambda g, t=t: g.tensor_scalar(C("scr_s", 128, t),
                      C("scr_s", 128, t), 1e-5, None, OP.max),
                      after=[c9], name=f"xd4{t}", small=True)
            cb = p.op("scalar", lambda sc, t=t: sc.activation(C("scr_s", 192, t),
                      C("scr_s", 128, t), AF.Ln), after=[ca, c6], name=f"xgl{t}", small=True)
            cc = p.op("scalar", lambda sc, t=t: sc.activation(C("scr_s", 192, t),
                      C("scr_s", 192, t), AF.Exp, scale=-1.0),
                      after=[cb], name=f"xge{t}", small=True)
            cd = p.op("vector", lambda v, t=t: v.tensor_scalar(C("st_g", 0, t),
                      C("scr_s", 192, t), 127.0, None, OP.mult),
                      after=[cc], name=f"xg{t}", small=True)
            ce = p.op("gpsimd", lambda g, t=t: g.tensor_tensor(C("scr_s", 192, t),
                      C("st_rstd", 0, t), C("scr_s", 128, t), OP.mult),
                      after=[c6, ca, cd], name=f"xi1{t}", small=True)
            cf = p.op("gpsimd", lambda g, t=t: g.tensor_scalar(C("st_iso", 0, t),
                      C("scr_s", 192, t), 1.0 / 127.0, None, OP.mult),
                      after=[ce], name=f"xiso{t}", small=True)
            cg = p.op("vector", lambda v, t=t: v.tensor_tensor(C("scr_s", 0, t),
                      C("st_m", 0, t), C("st_g", 0, t), OP.mult),
                      after=[cd], name=f"xb1{t}", small=True)
            ch = p.op("vector", lambda v, t=t: v.tensor_scalar(C("st_bg", 0, t),
                      C("scr_s", 0, t), -1.0, None, OP.mult),
                      after=[cg, cf], name=f"xbg{t}", small=True)
        else:
            cd = p.op("vector", lambda v, t=t: v.tensor_copy(C("st_g", 0, t), C("st_rstd", 0, t)),
                      after=[c6], name=f"xg{t}", small=True)
            cg = p.op("gpsimd", lambda g, t=t: g.tensor_tensor(C("scr_s", 0, t),
                      C("st_m", 0, t), C("st_rstd", 0, t), OP.mult),
                      after=[c6], name=f"xb1{t}", small=True)
            ch = p.op("vector", lambda v, t=t: v.tensor_scalar(C("st_bg", 0, t),
                      C("scr_s", 0, t), -1.0, None, OP.mult),
                      after=[cg, cd], name=f"xbg{t}", small=True)
        a1 = p.op("scalar", lambda sc, t=t, sl=sl: sc.activation(
            sb["scr_sq"][:, 0:1024], xs[:, sl, :], AF.Identity,
            bias=C("st_bg", 0, t), scale=C("st_g", 0, t)),
            after=[ch, ld], name=f"xq_a1_{t}")
        slot_free[sl] = a1
        if quant:
            r1 = p.op("vector", lambda v: v.tensor_scalar(
                sb["scr_sq"][:, 0:1024], sb["scr_sq"][:, 0:1024], MAGIC, MAGIC,
                OP.add, OP.subtract), after=[a1], name=f"xq_rnd{t}")
            c1b = p.op("scalar", lambda sc, t=t: sc.activation(
                sb["xq_bf"][:], sb["scr_sq"][:, 0:1024], AF.Copy, scale=C("st_iso", 0, t)),
                after=[r1, prev_tp], name=f"xq_c1_{t}")
        else:
            c1b = p.op("vector", lambda v: v.tensor_copy(sb["xq_bf"][:], sb["scr_sq"][:, 0:1024]),
                      after=[a1, prev_tp], name=f"xq_c1_{t}")
        prev_c1b = c1b
        tp = p.op("tensor", lambda te, t=t: [te.transpose(
            ptp[:, dc, :], sb["xq_bf"][:, 128 * dc:128 * (dc + 1)], sb["c_identbf"][:])
            for dc in range(ND)][-1], after=[c1b, prev_evac, cident], name=f"xq_tp{t}")
        ev = p.op("vector", lambda v, t=t: v.tensor_copy(
            sb["xt"][:, :, 1 + 128 * t: 1 + 128 * (t + 1)], ptp[:, :, :]),
            after=[tp], name=f"xq_ev{t}")
        prev_tp, prev_evac = tp, ev
        evac_list.append(ev)
    return prev_evac, evac_list


def emit_stats_factors(p, sb, x_ap):
    xs = sb["xbuf"]
    slot_free = [None, None]
    last = None
    for t in range(NT):
        sl = t % 2
        C = lambda buf, off, t: sb[buf][:, off + 4 * t:off + 4 * t + 1]
        ld = p.dma(f"xs{sl}", lambda e, t=t, sl=sl: e.dma_start(
            out=xs[:, sl, :], in_=x_ap[128 * t:128 * (t + 1), :]),
            after=(slot_free[sl] or []), name=f"sf_ld{t}")
        oa = p.op("scalar", lambda s, sl=sl, t=t: s.activation(
            sb["scr_sq"][:, 0:1024], xs[:, sl, :], AF.Square,
            accum_out=C("st_ssq", 0, t)), after=[ld], name=f"sf_ssq{t}")
        od = p.op("vector", lambda v, t=t, sl=sl: v.tensor_reduce(
            C("st_sum", 0, t), xs[:, sl, :], AX.X, OP.add),
            after=[ld], name=f"sf_red{t}", small=True)
        c1 = p.op("gpsimd", lambda g, t=t: g.tensor_scalar(C("st_m", 0, t), C("st_sum", 0, t),
                  1.0 / D, None, OP.mult), after=[od], name=f"sf_m{t}", small=True)
        c2 = p.op("vector", lambda v, t=t: v.tensor_tensor(C("scr_s", 0, t), C("st_m", 0, t),
                  C("st_m", 0, t), OP.mult), after=[c1], name=f"sf_m2{t}", small=True)
        c3 = p.op("gpsimd", lambda g, t=t: g.tensor_scalar(C("scr_s", 64, t),
                  C("st_ssq", 0, t), 1.0 / D, 1e-5, OP.mult, OP.add),
                  after=[oa, c1], name=f"sf_v1{t}", small=True)
        c4 = p.op("vector", lambda v, t=t: v.tensor_tensor(C("scr_s", 64, t), C("scr_s", 64, t),
                  C("scr_s", 0, t), OP.subtract), after=[c2, c3], name=f"sf_v2{t}", small=True)
        c5 = p.op("scalar", lambda sc, t=t: sc.activation(C("scr_s", 0, t), C("scr_s", 64, t),
                  AF.Ln), after=[c4], name=f"sf_ln{t}", small=True)
        c6 = p.op("scalar", lambda sc, t=t: sc.activation(C("st_g", 0, t), C("scr_s", 0, t),
                  AF.Exp, scale=-0.5), after=[c5], name=f"sf_rstd{t}", small=True)
        cg = p.op("gpsimd", lambda g, t=t: g.tensor_tensor(C("scr_s", 0, t),
                  C("st_m", 0, t), C("st_g", 0, t), OP.mult),
                  after=[c6], name=f"sf_b1{t}", small=True)
        ch = p.op("vector", lambda v, t=t: v.tensor_scalar(C("st_bg", 0, t),
                  C("scr_s", 0, t), -1.0, None, OP.mult),
                  after=[cg], name=f"sf_bg{t}", small=True)
        slot_free[sl] = [oa, od]
        last = ch
    return last



def emit_ffn0(nc, p, sbc, cevs, aps, tag, debug=False, shared=None):
    x_ap, y_ap = aps["x"], aps["y"]
    w1T_ap, w2T_ap, cw_ap, cb_ap = aps["w1T"], aps["w2T"], aps["cw"], aps["cb"]
    dbg = {}
    if debug:
        dbg["d_sc1"] = nc.dram_tensor("d_sc1", [1, 64], F32, kind="ExternalOutput").ap()
        dbg["d_g"] = nc.dram_tensor("d_g", [128, 64], F32, kind="ExternalOutput").ap()
        dbg["d_bg"] = nc.dram_tensor("d_bg", [128, 64], F32, kind="ExternalOutput").ap()
        dbg["d_iso"] = nc.dram_tensor("d_iso", [128, 64], F32, kind="ExternalOutput").ap()
        dbg["d_xt"] = nc.dram_tensor("d_xt", [128, ND, 2050], BF16, kind="ExternalOutput").ap()
        dbg["d_w1q"] = nc.dram_tensor("d_w1q", [D, DH], BF16, kind="ExternalOutput").ap()
        dbg["d_w2q"] = nc.dram_tensor("d_w2q", [DH, D], BF16, kind="ExternalOutput").ap()
        dbg["d_sil"] = nc.dram_tensor("d_sil", [128, NH, 512], F16, kind="ExternalOutput").ap()
        dbg["d_sq2"] = nc.dram_tensor("d_sq2", [128, 512], F32, kind="ExternalOutput").ap()
        dbg["d_iso2"] = nc.dram_tensor("d_iso2", [128, 512], F32, kind="ExternalOutput").ap()
        dbg["d_taps"] = nc.dram_tensor("d_taps", [128, 96], F32, kind="ExternalOutput").ap()
    if shared is None:
        w1q_dram = nc.dram_tensor(tag + "w1q_scr", [D, DH], BF16).ap()
        w2q_dram = nc.dram_tensor(tag + "w2q_scr", [DH, D], BF16).ap()
        sc1_dram = nc.dram_tensor(tag + "sc1_scr", [1, 64], F32).ap()
    else:
        w1q_dram = shared["w1q"]
        w2q_dram = shared["w2q"]
        sc1_dram = shared["sc1"]

    stack = contextlib.ExitStack()
    E = stack.enter_context
    sb = dict(sbc)
    sb["xbuf"] = E(nc.sbuf_tensor(tag + "xbuf", [128, 2, 1024], F32))
    sb["scr_a"] = E(nc.sbuf_tensor(tag + "scr_a", [128, 4096], F32))
    sb["wsdump"] = E(nc.sbuf_tensor(tag + "wsdump", [128, 2048], BF16))
    sb["scr_sq"] = E(nc.sbuf_tensor(tag + "scr_sq", [128, 1024], F32))
    sb["scr_s"] = E(nc.sbuf_tensor(tag + "scr_s", [128, 256], F32))
    sb["xq_bf"] = E(nc.sbuf_tensor(tag + "xq_bf", [128, 1024], BF16))
    for n in ("st_sum", "st_ssq", "st_max", "st_min", "st_m", "st_rstd",
              "st_g", "st_bg", "st_iso"):
        sb[n] = E(nc.sbuf_tensor(tag + n, [128, 64], F32))
    sb["xt"] = E(nc.sbuf_tensor(tag + "xt", [128, ND, 2050], BF16))
    sb["w2q"] = E(nc.sbuf_tensor(tag + "w2q", [128, NH, 1024], BF16))
    sb["ring1"] = E(nc.sbuf_tensor(tag + "ring1", [128, 4, ND, 128], BF16))
    sb["sil"] = E(nc.sbuf_tensor(tag + "sil", [128, NH, 512], F16))
    sb["runM"] = E(nc.sbuf_tensor(tag + "runM", [128, 512], F32))
    sb["runL"] = E(nc.sbuf_tensor(tag + "runL", [128, 512], F32))
    sb["convu"] = E(nc.sbuf_tensor(tag + "convu", [128, 2, 512], F32))
    sb["qs_t"] = E(nc.sbuf_tensor(tag + "qs_t", [128, 512], F32))
    sb["taps"] = E(nc.sbuf_tensor(tag + "taps", [128, 96], F32))
    sb["cb_sb"] = E(nc.sbuf_tensor(tag + "cb_sb", [128, 32], F32))
    sb["wabs"] = E(nc.sbuf_tensor(tag + "wabs", [128, 256], F32))
    sb["wqbf"] = E(nc.sbuf_tensor(tag + "wqbf", [128, 4096], BF16))
    sb["sc1"] = E(nc.sbuf_tensor(tag + "sc1", [1, 64], F32))
    sb["sccol"] = E(nc.sbuf_tensor(tag + "sccol", [128, 20], F32))
    sb["sqT"] = E(nc.sbuf_tensor(tag + "sqT", [4, 256], F32))
    sb["sq2rep"] = E(nc.sbuf_tensor(tag + "sq2rep", [128, 512], F32))
    sb["iso2rep"] = E(nc.sbuf_tensor(tag + "iso2rep", [128, 512], F32))
    sb["mt_tok"] = E(nc.sbuf_tensor(tag + "mt_tok", [128, 32], F32))

    psum = {
        "hp": E(nc.psum_tensor(tag + "hp", [128, 2, 1024], F32)),
        "out2": E(nc.psum_tensor(tag + "out2", [128, 1024], F32)),
        "ptp": E(nc.psum_tensor(tag + "ptp", [128, ND, 128], BF16)),
        "pbr": E(nc.psum_tensor(tag + "pbr", [128, 512], F32)),
    }

    pbr_user = None
    ptp_user = None

    xt_done, xt_evs = emit_xphase(p, sb, psum, x_ap, quant=True, cident=cevs[0])
    ptp_user = xt_done

    prev = None
    slotu = [None, None]
    for i in range(16 if shared is None else 0):
        sl = i % 2
        ld = p.dma(f"wt{sl}", lambda e, i=i, sl=sl: e.dma_start(
            out=sb["scr_a"][:, 2048 * sl:2048 * (sl + 1)],
            in_=w1T_ap[128 * (i // 2):128 * (i // 2 + 1), 2048 * (i % 2):2048 * (i % 2 + 1)]),
            after=[slotu[sl]], name=f"w1s_ld{i}")
        prev = p.op("scalar", lambda a, i=i, sl=sl: a.activation(
            sb["wsdump"][:], sb["scr_a"][:, 2048 * sl:2048 * (sl + 1)], AF.Abs,
            accum_out=sb["wabs"][:, 4 * i:4 * i + 1]), after=[ld], name=f"w1s_abs{i}")
        slotu[sl] = prev
    for i in range(32 if shared is None else 0):
        sl = i % 2
        ld = p.dma(f"wt{sl}", lambda e, i=i, sl=sl: e.dma_start(
            out=sb["scr_a"][:, 2048 * sl:2048 * sl + 1024], in_=w2T_ap[128 * i:128 * (i + 1), :]),
            after=[slotu[sl]], name=f"w2s_ld{i}")
        prev = p.op("scalar", lambda a, i=i, sl=sl: a.activation(
            sb["wsdump"][:, 0:1024], sb["scr_a"][:, 2048 * sl:2048 * sl + 1024], AF.Abs,
            accum_out=sb["wabs"][:, 64 + 4 * i:65 + 4 * i]), after=[ld], name=f"w2s_abs{i}")
        slotu[sl] = prev

    if shared is not None:
        ld_sc1 = p.dma("sc1ld", lambda e: e.dma_start(out=sb["sc1"][:], in_=sc1_dram),
                       name=tag + "ld_sc1")
        w6 = w8 = ld_sc1
        o_mean = ld_sc1
        w1q_done = None
        wq_done = None
    def wtot(v):
        w1v = sb["wabs"][:, 0:64].rearrange("p (t u) -> p t u", u=4)[:, :, 0:1]
        w2v = sb["wabs"][:, 64:192].rearrange("p (t u) -> p t u", u=4)[:, :, 0:1]
        v.tensor_reduce(sb["wabs"][:, 208:209], w2v, AX.XY, OP.add)
        return v.tensor_reduce(sb["wabs"][:, 192:193], w1v, AX.XY, OP.add)
    if shared is None:
        o_tot = p.op("vector", wtot, after=[prev], name="wtot", small=True)

        def wtp(te):
            te.transpose(psum["pbr"][0:1, 0:128], sb["wabs"][:, 192:193], sb["c_identf"][:])
            return te.transpose(psum["pbr"][0:1, 128:256], sb["wabs"][:, 208:209], sb["c_identf"][:])
        o_tp = p.op("tensor", wtp, after=[o_tot, cevs[1], pbr_user], name="wtot_tp")

        w1r = p.op("vector", lambda v: v.tensor_reduce(
            sb["sc1"][0:1, 0:1], psum["pbr"][0:1, 0:128], AX.X, OP.add),
            after=[o_tp], name="wm_r1", small=True)
        w2r = p.op("vector", lambda v: v.tensor_reduce(
            sb["sc1"][0:1, 16:17], psum["pbr"][0:1, 128:256], AX.X, OP.add),
            after=[o_tp], name="wm_r2", small=True)
        w3 = p.op("gpsimd", lambda g: g.tensor_scalar(
            sb["sc1"][0:1, 4:5], sb["sc1"][0:1, 0:1], 1.0 / (D * DH), 1e-5, OP.mult, OP.max),
            after=[w1r], name="wm_mA", small=True)
        w4 = p.op("gpsimd", lambda g: g.tensor_scalar(
            sb["sc1"][0:1, 20:21], sb["sc1"][0:1, 16:17], 1.0 / (D * DH), 1e-5, OP.mult, OP.max),
            after=[w2r], name="wm_mB", small=True)
        w5 = p.op("scalar", lambda sc: sc.activation(sb["sc1"][0:1, 8:9], sb["sc1"][0:1, 4:5], AF.Ln),
                  after=[w3], name="wm_lnA", small=True)
        w6 = p.op("scalar", lambda sc: sc.activation(sb["sc1"][0:1, 12:13], sb["sc1"][0:1, 8:9],
                                                     AF.Exp, scale=-1.0),
                  after=[w5], name="wm_sA", small=True)
        w7 = p.op("scalar", lambda sc: sc.activation(sb["sc1"][0:1, 24:25], sb["sc1"][0:1, 20:21], AF.Ln),
                  after=[w4, w6], name="wm_lnB", small=True)
        w8 = p.op("scalar", lambda sc: sc.activation(sb["sc1"][0:1, 28:29], sb["sc1"][0:1, 24:25],
                                                     AF.Exp, scale=-1.0),
                  after=[w7], name="wm_sB", small=True)
        o_mean = p.barrier("vector", [w6, w8], name="wm_done")
        pbr_user = o_tp
        st_sc1 = p.dma("sc1st", lambda e: e.dma_start(out=sc1_dram, in_=sb["sc1"][:]),
                       after=[w6, w8], name=tag + "st_sc1")

    bco = []
    for j, off in enumerate([12, 28, 4, 20]):
        mm = p.op("tensor", lambda te, off=off: te.matmul(
            psum["pbr"][:, 0:1], sb["c_ones1f"][:, :], sb["sc1"][0:1, off:off + 1],
            start=True, stop=True), after=[w6, w8, pbr_user, cevs[3]], name=f"bc{j}_mm")
        ev = p.op("vector", lambda v, j=j: v.tensor_copy(
            sb["sccol"][:, 4 * j:4 * j + 1], psum["pbr"][:, 0:1]), after=[mm],
            name=f"bc{j}_ev", small=True)
        pbr_user = ev
        bco.append(ev)
    o_half = p.op("gpsimd", lambda g: g.tensor_scalar(
        sb["sccol"][:, 16:17], sb["sccol"][:, 12:13], 0.5, None, OP.mult),
        after=[bco[3]], name="m2half", small=True)

    ld_cw = p.dma("cwld", lambda e: e.dma_start(out=sb["taps"][:], in_=cw_ap), name="ld_cw")
    ld_cb = p.dma("cbld", lambda e: e.dma_start(out=sb["cb_sb"][:], in_=cb_ap), name="ld_cb")
    o_taps = p.op("vector", lambda v: v.tensor_scalar(
        sb["taps"][:], sb["taps"][:], sb["sccol"][:, 8:9], None, OP.mult),
        after=[ld_cw, bco[2]], name="taps")

    prev_ld = prev
    if shared is None:
        w1q_done = None
    slotq = [o_mean, o_mean]
    slotw = [None, None]
    for wi, (n_chunks, src, dst, width, scol) in enumerate(
            [(16, w1T_ap, w1q_dram, 2048, 0), (32, w2T_ap, w2q_dram, 1024, 1)]
            if shared is None else []):
        for i in range(n_chunks):
            sl = i % 2
            if wi == 0:
                srcap = src[128 * (i // 2):128 * (i // 2 + 1), 2048 * (i % 2):2048 * (i % 2 + 1)]
            else:
                srcap = src[128 * i:128 * (i + 1), :]
            ld = p.dma(f"wt{sl}", lambda e, srcap=srcap, width=width, sl=sl: e.dma_start(
                out=sb["scr_a"][:, 2048 * sl:2048 * sl + width], in_=srcap),
                after=[slotq[sl], o_mean], name=f"wq_ld{wi}_{i}")
            a = p.op("scalar", lambda s, width=width, scol=scol, sl=sl: s.activation(
                sb["scr_a"][:, 2048 * sl:2048 * sl + width],
                sb["scr_a"][:, 2048 * sl:2048 * sl + width], AF.Copy,
                scale=sb["sccol"][:, 4 * scol:4 * scol + 1]),
                after=[ld, bco[scol]], name=f"wq_sc{wi}_{i}")
            r = p.op("vector", lambda v, width=width, sl=sl: v.tensor_scalar(
                sb["scr_a"][:, 2048 * sl:2048 * sl + width],
                sb["scr_a"][:, 2048 * sl:2048 * sl + width], MAGIC, MAGIC,
                OP.add, OP.subtract), after=[a], name=f"wq_rnd{wi}_{i}")
            c = p.op("vector", lambda v, width=width, sl=sl: v.tensor_scalar(
                sb["wqbf"][:, 2048 * sl:2048 * sl + width],
                sb["scr_a"][:, 2048 * sl:2048 * sl + width], 1.0, -1.0,
                OP.min, OP.max), after=[r, slotw[sl]], name=f"wq_clip{wi}_{i}")
            slotq[sl] = c
            if wi == 0:
                dstap = dst[128 * (i // 2):128 * (i // 2 + 1), 2048 * (i % 2):2048 * (i % 2 + 1)]
            else:
                dstap = dst[128 * i:128 * (i + 1), :]
            st = p.dma(f"wqs{sl}", lambda e, dstap=dstap, width=width, sl=sl: e.dma_start(
                out=dstap, in_=sb["wqbf"][:, 2048 * sl:2048 * sl + width]),
                after=[c], name=f"wq_st{wi}_{i}")
            slotw[sl] = st
            if wi == 0 and i == n_chunks - 1:
                w1q_done = p.barrier("sync", [slotw[0], slotw[1]], name=tag + "w1qd")
    if shared is None:
        wq_done = p.barrier("sync", [slotw[0], slotw[1]], name=tag + "wqd")

    ld_w2q = None
    for part in range(4):
        ld_w2q = p.dma("w2qld", lambda e, part=part: e.dma_start(
            out=sb["w2q"][:, 8 * part:8 * (part + 1), :],
            in_=w2q_dram.rearrange("(hc p) d -> p hc d", p=128)[:, 8 * part:8 * (part + 1), :]),
            after=[wq_done], name=f"ld_w2q{part}")


    y_stores = []
    ring_user = [None] * 4
    hp_user = [None, None]
    conv_user = [None, None]
    sil_user = [None] * NH
    out2_user = None
    o1_prev = None
    prev_ep = [xt_done, xt_done]
    for q in range(4):
        c0 = 512 * q
        def mzf(v):
            v.memset(sb["runM"][:], 0.0)
            return v.memset(sb["runL"][:], 0.0)
        mz = p.op("gpsimd", mzf, after=[o1_prev], name=f"runM_z{q}")
        conv_ops = []
        for hc in range(NH):
            r = hc % 4
            ld = p.dma(f"r1s{r}", lambda e, hc=hc, r=r: e.dma_start(
                out=sb["ring1"][:, r, :, :],
                in_=w1q_dram.rearrange("(dc p) (hc j) -> p dc hc j", p=128, j=128)[:, :, hc, :]),
                after=[ring_user[r], w1q_done], name=f"r1_ld{q}_{hc}")
            b = hc % 2

            def mm1(te, hc=hc, r=r, b=b, c0=c0):
                for dc in range(ND):
                    te.matmul(psum["hp"][:, b, 0:512], sb["ring1"][:, r, dc, :],
                              sb["xt"][:, dc, c0:c0 + 512],
                              start=(dc == 0), stop=(dc == ND - 1))
                for dc in range(ND):
                    mm = te.matmul(psum["hp"][:, b, 512:514], sb["ring1"][:, r, dc, :],
                                   sb["xt"][:, dc, c0 + 512:c0 + 514],
                                   start=(dc == 0), stop=(dc == ND - 1))
                return mm
            mm = p.op("tensor", mm1, after=[ld, xt_evs[min(NT - 1, 4 * q + 4)], hp_user[b]],
                      name=f"mm1_{q}_{hc}")
            ring_user[r] = mm

            cs_ = hc % 2
            cva = p.op("scalar", lambda sc, hc=hc, b=b, cs_=cs_: sc.activation(
                sb["convu"][:, cs_, :], psum["hp"][:, b, 0:512], AF.Copy,
                scale=sb["taps"][:, 3 * hc:3 * hc + 1]),
                after=[mm, o_taps, conv_user[cs_]], name=f"cva{q}_{hc}")

            def conv(v, hc=hc, b=b, cs_=cs_):
                v.scalar_tensor_tensor(sb["convu"][:, cs_, :], psum["hp"][:, b, 1:513],
                                       sb["taps"][:, 3 * hc + 1:3 * hc + 2],
                                       sb["convu"][:, cs_, :], OP.mult, OP.add)
                return v.scalar_tensor_tensor(sb["convu"][:, cs_, :], psum["hp"][:, b, 2:514],
                                              sb["taps"][:, 3 * hc + 2:3 * hc + 3],
                                              sb["convu"][:, cs_, :], OP.mult, OP.add)
            cv = p.op("vector", conv, after=[cva], name=f"conv{q}_{hc}")
            hp_user[b] = cv

            si = p.op("scalar", lambda s, hc=hc: s.activation(
                sb["sil"][:, hc, :], sb["convu"][:, hc % 2, :], AF.Silu,
                bias=sb["cb_sb"][:, hc:hc + 1], scale=1.0),
                after=[cv, ld_cb, sil_user[hc]], name=f"silu{q}_{hc}")
            conv_user[hc % 2] = si

            def rmf(v, hc=hc):
                v.tensor_tensor(sb["runM"][:], sb["sil"][:, hc, :], sb["runM"][:], OP.max)
                return v.tensor_tensor(sb["runL"][:], sb["sil"][:, hc, :], sb["runL"][:], OP.min)
            rm = p.op("vector", rmf, after=[si, mz], name=f"rmax{q}_{hc}")
            conv_ops.append(rm)

        cmb = p.op("vector", lambda v: [v.tensor_scalar(sb["runL"][:], sb["runL"][:], -1.0, None, OP.mult),
                    v.tensor_tensor(sb["runM"][:], sb["runM"][:], sb["runL"][:], OP.max)][-1],
                    after=[conv_ops[-1]], name=f"rcomb{q}")

        def tpm_f(te, q=q):
            for j in range(4):
                mmi = te.transpose(psum["pbr"][:, 128 * j:128 * (j + 1)],
                                   sb["runM"][:, 128 * j:128 * (j + 1)], sb["c_identf"][:])
            return mmi
        o1 = p.op("tensor", tpm_f, after=[cmb, pbr_user], name=f"mtp{q}")
        o1_prev = o1

        mview = sb["mt_tok"][:, 0:16].rearrange("p (t u) -> p t u", u=4)[:, :, 0:1]
        lview = sb["mt_tok"][:, 16:32].rearrange("p (t u) -> p t u", u=4)[:, :, 0:1]
        def mredr(v, q=q):
            for j in range(4):
                r = v.tensor_reduce(sb["mt_tok"][:, 4 * j:4 * j + 1],
                                    psum["pbr"][:, 128 * j:128 * (j + 1)],
                                    AX.X, OP.max, apply_absolute_value=True)
            return r
        o2a = p.op("vector", mredr, after=[o1], name=f"mred{q}", small=True)
        o2b = p.op("gpsimd", lambda g: g.tensor_scalar(mview, mview, 1e-5, None, OP.max),
                   after=[o2a], name=f"mclip{q}", small=True)
        o2c = p.op("scalar", lambda sc: sc.activation(lview, mview, AF.Ln),
                   after=[o2b], name=f"mln{q}", small=True)
        o2d = p.op("scalar", lambda sc: sc.activation(lview, lview, AF.Exp, scale=-1.0),
                   after=[o2c], name=f"mexp{q}", small=True)
        o2e = p.op("vector", lambda v: v.tensor_scalar(lview, lview, 127.0, None, OP.mult),
                   after=[o2d], name=f"msq{q}", small=True)
        o2 = p.op("gpsimd", lambda g: g.tensor_scalar(mview, mview, 1.0 / 127.0, None, OP.mult),
                  after=[o2b, o2c], name=f"miso{q}", small=True)
        pbr_user = o2a

        def mtp2(te):
            te.transpose(psum["pbr"][0:4, 0:128],
                         sb["mt_tok"][:, 0:16].rearrange("p (t u) -> p t u", u=4)[:, :, 0:1].rearrange("p t u -> p (t u)"),
                         sb["c_identf"][:])
            return te.transpose(psum["pbr"][0:4, 128:256],
                                sb["mt_tok"][:, 16:32].rearrange("p (t u) -> p t u", u=4)[:, :, 0:1].rearrange("p t u -> p (t u)"),
                                sb["c_identf"][:])
        o3 = p.op("tensor", mtp2, after=[o2, o2e, pbr_user], name=f"mtp2{q}")
        o4 = p.op("vector", lambda v: v.tensor_copy(sb["sqT"][:, :], psum["pbr"][0:4, 0:256]),
                  after=[o3], name=f"sqT{q}")
        pbr_user = o4
        o5 = p.op("tensor", lambda te: [te.matmul(
            psum["pbr"][:, 128 * j:128 * (j + 1)], sb["c_selsf4"][0:4, j, :],
            sb["sqT"][0:4, 128:256], start=True, stop=True) for j in range(4)][-1],
            after=[o4, pbr_user, cevs[5]], name=f"sqrep_mm{q}")
        o6 = p.op("vector", lambda v: v.tensor_copy(sb["sq2rep"][:], psum["pbr"][:]),
                  after=[o5], name=f"sqrep_ev{q}")
        pbr_user = o6
        o7 = p.op("tensor", lambda te: [te.matmul(
            psum["pbr"][:, 128 * j:128 * (j + 1)], sb["c_selsf4"][0:4, j, :],
            sb["sqT"][0:4, 0:128], start=True, stop=True) for j in range(4)][-1],
            after=[o6, pbr_user], name=f"isorep_mm{q}")
        o8 = p.op("vector", lambda v: v.tensor_copy(sb["iso2rep"][:], psum["pbr"][:]),
                  after=[o7], name=f"isorep_ev{q}")
        pbr_user = o8

        qops = []
        qprev = o8
        for hc in range(NH):
            def quant(v, hc=hc):
                v.tensor_tensor(sb["qs_t"][:], sb["sil"][:, hc, :], sb["sq2rep"][:], OP.mult)
                v.tensor_scalar(sb["qs_t"][:], sb["qs_t"][:], MAGIC, MAGIC, OP.add, OP.subtract)
                return v.tensor_tensor(sb["sil"][:, hc, :], sb["qs_t"][:], sb["iso2rep"][:], OP.mult)
            qo = p.op("vector", quant, after=[qprev, conv_ops[hc]], name=f"quant{q}_{hc}")
            qprev = qo
            qops.append(qo)

        m2_last = None
        for tt in range(4):
            gt = 4 * q + tt

            def mm2(te, tt=tt):
                for piece in range(2):
                    for hc in range(NH):
                        mm = te.matmul(psum["out2"][:, 512 * piece:512 * (piece + 1)],
                                       sb["sil"][:, hc, 128 * tt:128 * (tt + 1)],
                                       sb["w2q"][:, hc, 512 * piece:512 * (piece + 1)],
                                       start=(hc == 0), stop=(hc == NH - 1))
                return mm
            m2 = p.op("tensor", mm2, after=[qops[-1], ld_w2q, out2_user], name=f"mm2_{q}_{tt}")
            m2_last = m2
            sl = tt % 2
            lx = p.dma(f"xe{sl}", lambda e, gt=gt, sl=sl: e.dma_start(
                out=sb["xbuf"][:, sl, :], in_=x_ap[128 * gt:128 * (gt + 1), :]),
                after=[prev_ep[0], prev_ep[1]], name=f"xe_ld{gt}")
            ep = p.op("vector", lambda v, sl=sl: v.scalar_tensor_tensor(
                sb["xbuf"][:, 1 - sl, :], psum["out2"][:, :], sb["sccol"][:, 16:17],
                sb["xbuf"][:, sl, :], OP.mult, OP.add),
                after=[m2, lx, o_half, prev_ep[0], prev_ep[1]], name=f"ep{gt}")
            out2_user = ep
            st = p.dma(f"yst{sl}", lambda e, gt=gt, sl=sl: e.dma_start(
                out=y_ap[128 * gt:128 * (gt + 1), :], in_=sb["xbuf"][:, 1 - sl, :]),
                after=[ep], name=f"y_st{gt}")
            prev_ep[sl] = st
            y_stores.append(st)
        for hc in range(NH):
            sil_user[hc] = m2_last

    if debug:
        dstores = []
        dstores.append(p.dma("dbg", lambda e: e.dma_start(out=dbg["d_sc1"], in_=sb["sc1"][:]), after=[o_mean], name="d_sc1"))
        dstores.append(p.dma("dbg", lambda e: e.dma_start(out=dbg["d_g"], in_=sb["st_g"][:]), after=[fac], name="d_g"))
        dstores.append(p.dma("dbg", lambda e: e.dma_start(out=dbg["d_bg"], in_=sb["st_bg"][:]), after=[fac], name="d_bg"))
        dstores.append(p.dma("dbg", lambda e: e.dma_start(out=dbg["d_iso"], in_=sb["st_iso"][:]), after=[fac], name="d_iso"))
        dstores.append(p.dma("dbg", lambda e: e.dma_start(out=dbg["d_xt"], in_=sb["xt"][:]), after=[xt_done], name="d_xt"))
        dstores.append(p.dma("dbg", lambda e: e.dma_start(out=dbg["d_w1q"], in_=w1q_dram), after=[wq_done], name="d_w1q"))
        dstores.append(p.dma("dbg", lambda e: e.dma_start(out=dbg["d_w2q"], in_=w2q_dram), after=[wq_done], name="d_w2q"))
        dstores.append(p.dma("dbg", lambda e: e.dma_start(out=dbg["d_sil"], in_=sb["sil"][:]), after=[y_stores[-1]], name="d_sil"))
        dstores.append(p.dma("dbg", lambda e: e.dma_start(out=dbg["d_sq2"], in_=sb["sq2rep"][:]), after=[y_stores[-1]], name="d_sq2"))
        dstores.append(p.dma("dbg", lambda e: e.dma_start(out=dbg["d_iso2"], in_=sb["iso2rep"][:]), after=[y_stores[-1]], name="d_iso2"))
        dstores.append(p.dma("dbg", lambda e: e.dma_start(out=dbg["d_taps"], in_=sb["taps"][:]), after=[o_taps], name="d_taps"))
        y_stores = y_stores + dstores
    p.barrier("sync", y_stores, name=tag + "final")
    stack.close()
    p.seal(name=tag + "seal")

def build_ffn0(debug=False):
    nc = bass.Bass("TRN2", target_bir_lowering=False)
    aps = {}
    aps["x"] = nc.dram_tensor("x", [SEQ, D], F32, kind="ExternalInput").ap()
    aps["y"] = nc.dram_tensor("y", [SEQ, D], F32, kind="ExternalOutput").ap()
    aps["w1T"] = nc.dram_tensor("w1T", [D, DH], F32, kind="ExternalInput").ap()
    aps["w2T"] = nc.dram_tensor("w2T", [DH, D], F32, kind="ExternalInput").ap()
    aps["cw"] = nc.dram_tensor("cw", [128, 96], F32, kind="ExternalInput").ap()
    aps["cb"] = nc.dram_tensor("cb", [128, 32], F32, kind="ExternalInput").ap()
    cs = declare_consts(nc)
    outer = contextlib.ExitStack()
    sbc = alloc_consts(nc, outer)
    p = Prog()
    cevs = load_consts(p, cs, sbc)
    emit_ffn0(nc, p, sbc, cevs, aps, "f0_", debug=debug)
    p.emit(nc, outer)
    return nc




def emit_attn0(nc, p, sbc, cevs, aps, tag, debug=False):
    x_ap, y_ap = aps["x"], aps["y"]
    wq_ap, wk_ap, wv_ap, lam_ap = aps["wq12"], aps["wk12"], aps["wv"], aps["lam4"]
    wqd = nc.dram_tensor(tag + "wq_scr", [D, 2 * D], BF16).ap()
    wkd = nc.dram_tensor(tag + "wk_scr", [D, 2 * D], BF16).ap()

    SCALE = float(D ** (-0.25))
    stack = contextlib.ExitStack()
    E = stack.enter_context
    sb = dict(sbc)
    sb["xbuf"] = E(nc.sbuf_tensor(tag + "xbuf", [128, 2, 1024], F32))
    sb["ybuf"] = E(nc.sbuf_tensor(tag + "ybuf", [128, 2, 1024], F32))
    sb["scr_sq"] = E(nc.sbuf_tensor(tag + "scr_sq", [128, 1024], F32))
    sb["scr_s"] = E(nc.sbuf_tensor(tag + "scr_s", [128, 256], F32))
    sb["xq_bf"] = E(nc.sbuf_tensor(tag + "xq_bf", [128, 1024], BF16))
    for n in ("st_sum", "st_ssq", "st_max", "st_min", "st_m", "st_rstd",
              "st_g", "st_bg", "st_iso"):
        sb[n] = E(nc.sbuf_tensor(tag + n, [128, 64], F32))
    sb["k12T"] = E(nc.sbuf_tensor(tag + "k12T", [128, 16, 2048], BF16))
    sb["v_sb"] = E(nc.sbuf_tensor(tag + "v_sb", [128, 16, 1024], BF16))
    sb["wv_bf"] = E(nc.sbuf_tensor(tag + "wv_bf", [128, ND, 1024], BF16))
    sb["xtp"] = E(nc.sbuf_tensor(tag + "xtp", [128, ND, 256], BF16))
    sb["wring"] = E(nc.sbuf_tensor(tag + "wring", [128, 4, ND, 128], BF16))
    sb["wcast"] = E(nc.sbuf_tensor(tag + "wcast", [128, 1, 2048], F32))
    sb["wcbf"] = E(nc.sbuf_tensor(tag + "wcbf", [128, 2048], BF16))
    sb["q12T"] = E(nc.sbuf_tensor(tag + "q12T", [128, 16, 256], BF16))
    sb["e1b"] = E(nc.sbuf_tensor(tag + "e1b", [128, 2048], BF16))
    sb["e2h"] = E(nc.sbuf_tensor(tag + "e2h", [128, 2048], BF16))
    sb["dT"] = E(nc.sbuf_tensor(tag + "dT", [128, 16, 256], BF16))
    sb["oT"] = E(nc.sbuf_tensor(tag + "oT", [128, ND, 256], BF16))
    sb["otok"] = E(nc.sbuf_tensor(tag + "otok", [128, 2, 1024], BF16))
    sb["sc1"] = E(nc.sbuf_tensor(tag + "sc1", [1, 64], F32))
    sb["lam4"] = E(nc.sbuf_tensor(tag + "lam4_sb", [128, 4, 8], F32))
    sb["lscr"] = E(nc.sbuf_tensor(tag + "lscr", [128, 64], F32))
    sb["lamcol"] = E(nc.sbuf_tensor(tag + "lamcol", [128, 4], F32))
    sb["stt"] = E(nc.sbuf_tensor(tag + "stt", [128, 128], F32))
    sb["r1sav"] = E(nc.sbuf_tensor(tag + "r1sav", [128, 8], F32))
    sb["socol"] = E(nc.sbuf_tensor(tag + "socol", [128, 8], F32))

    psum = {
        "psA": E(nc.psum_tensor(tag + "psA", [128, 1024], F32)),
        "psB": E(nc.psum_tensor(tag + "psB", [128, 1024], F32)),
        "ptp": E(nc.psum_tensor(tag + "ptp", [128, ND, 128], BF16)),
        "pq": E(nc.psum_tensor(tag + "pq", [128, 2, 512], F32)),
        "pbr": E(nc.psum_tensor(tag + "pbr", [128, 512], F32)),
    }

    U = {"pbr": None, "ptp": None, "psA": None, "psB": None,
         "pq0": None, "pq1": None, "xtp": None, "q12T": None, "dT": None,
         "oT": None, "otok0": None, "otok1": None, "e1b": None, "e2h": None,
         "ring": [None] * 4, "xn_tp": None, "xslot": [None, None]}

    fac = emit_stats_factors(p, sb, x_ap)
    U["xslot"] = [fac, fac]

    prev_ld = None
    cast_done = {}
    prev_cv = None
    for name, src, dst, cols in [("wq", wq_ap, wqd, 2048), ("wk", wk_ap, wkd, 2048)]:
        for i in range(ND):
            sl = 0
            ld = p.dma(f"wc{sl}", lambda e, i=i, sl=sl, src=src, cols=cols: e.dma_start(
                out=sb["wcast"][:, sl, 0:cols], in_=src[128 * i:128 * (i + 1), :]),
                after=[prev_ld, prev_cv], name=f"wcl_{name}{i}")
            cv = p.op("vector", lambda v, sl=sl, cols=cols: v.tensor_copy(
                sb["wcbf"][:, 0:cols], sb["wcast"][:, sl, 0:cols]),
                after=[ld], name=f"wcc_{name}{i}")
            prev_cv = cv
            std = p.dma("wcs", lambda e, i=i, dst=dst, cols=cols: e.dma_start(
                out=dst[128 * i:128 * (i + 1), :], in_=sb["wcbf"][:, 0:cols]),
                after=[cv], name=f"wcst_{name}{i}")
            prev_ld = std
            cast_done[name] = std
    for i in range(ND):
        sl = 0
        ld = p.dma(f"wc{sl}", lambda e, i=i, sl=sl: e.dma_start(
            out=sb["wcast"][:, sl, 0:1024], in_=wv_ap[128 * i:128 * (i + 1), :]),
            after=[prev_ld, prev_cv], name=f"wcl_wv{i}")
        cv = p.op("vector", lambda v, i=i, sl=sl: v.tensor_copy(
            sb["wv_bf"][:, i, :], sb["wcast"][:, sl, 0:1024]),
            after=[ld], name=f"wcc_wv{i}")
        prev_ld = ld
        prev_cv = cv
        cast_done["wv"] = cv

    ldl = p.dma("wld", lambda e: e.dma_start(out=sb["lam4"][:], in_=lam_ap), name="ld_lam")
    l1 = p.op("vector", lambda v: v.tensor_tensor(
        sb["lscr"][:, 0:8], sb["lam4"][:, 0, :], sb["lam4"][:, 1, :], OP.mult),
        after=[ldl], name="lam_m1", small=True)
    l2 = p.op("gpsimd", lambda g: g.tensor_tensor(
        sb["lscr"][:, 8:16], sb["lam4"][:, 2, :], sb["lam4"][:, 3, :], OP.mult),
        after=[ldl], name="lam_m2", small=True)
    l3 = p.op("vector", lambda v: v.tensor_reduce(
        sb["lscr"][:, 16:17], sb["lscr"][:, 0:8], AX.X, OP.add),
        after=[l1], name="lam_r1", small=True)
    l4 = p.op("vector", lambda v: v.tensor_reduce(
        sb["lscr"][:, 32:33], sb["lscr"][:, 8:16], AX.X, OP.add),
        after=[l2, l3], name="lam_r2", small=True)
    l5 = p.op("tensor", lambda te: [te.transpose(
        psum["pbr"][0:1, 0:128], sb["lscr"][:, 16:17], sb["c_identf"][:]),
        te.transpose(psum["pbr"][0:1, 128:256], sb["lscr"][:, 32:33], sb["c_identf"][:])][-1],
        after=[l4, cevs[1], U["pbr"]], name="lam_tp")
    l6 = p.op("vector", lambda v: v.tensor_reduce(
        sb["sc1"][0:1, 0:1], psum["pbr"][0:1, 0:128], AX.X, OP.add),
        after=[l5], name="lam_d1", small=True)
    l7 = p.op("vector", lambda v: v.tensor_reduce(
        sb["sc1"][0:1, 4:5], psum["pbr"][0:1, 128:256], AX.X, OP.add),
        after=[l5, l6], name="lam_d2", small=True)
    U["pbr"] = l7
    l8 = p.op("scalar", lambda sc: sc.activation(sb["sc1"][0:1, 8:9], sb["sc1"][0:1, 0:1], AF.Exp),
              after=[l6], name="lam_e1", small=True)
    l9 = p.op("scalar", lambda sc: sc.activation(sb["sc1"][0:1, 12:13], sb["sc1"][0:1, 4:5], AF.Exp),
              after=[l7, l8], name="lam_e2", small=True)
    la_ = p.op("vector", lambda v: v.tensor_tensor(
        sb["sc1"][0:1, 16:17], sb["sc1"][0:1, 8:9], sb["sc1"][0:1, 12:13], OP.subtract),
        after=[l8, l9], name="lam_sub", small=True)
    lb_ = p.op("gpsimd", lambda g: g.tensor_scalar(
        sb["sc1"][0:1, 20:21], sb["sc1"][0:1, 16:17], 0.8, None, OP.add),
        after=[la_], name="lam_add", small=True)
    lc_ = p.op("tensor", lambda te: te.matmul(
        psum["pbr"][:, 0:1], sb["c_ones1f"][:, :], sb["sc1"][0:1, 20:21],
        start=True, stop=True), after=[lb_, U["pbr"], cevs[3]], name="lam_bc")
    lamc = p.op("vector", lambda v: v.tensor_copy(sb["lamcol"][:, 0:1], psum["pbr"][:, 0:1]),
                after=[lc_], name="lam_ev", small=True)
    U["pbr"] = lamc


    def build_xn(tiles):
        ev = None
        for j, t in enumerate(tiles):
            sl = t % 2
            ld = p.dma(f"xn{sl}", lambda e, t=t, sl=sl: e.dma_start(
                out=sb["xbuf"][:, sl, :], in_=x_ap[128 * t:128 * (t + 1), :]),
                after=[U["xslot"][sl], fac], name=f"xn_ld{t}")
            a1 = p.op("scalar", lambda s, t=t, sl=sl: s.activation(
                sb["xq_bf"][:], sb["xbuf"][:, sl, :], AF.Identity,
                bias=sb["st_bg"][:, 4 * t:4 * t + 1], scale=sb["st_g"][:, 4 * t:4 * t + 1]),
                after=[ld, U["xn_tp"]], name=f"xn_a{t}")
            U["xslot"][sl] = a1
            tp = p.op("tensor", lambda te: [te.transpose(
                psum["ptp"][:, dc, :], sb["xq_bf"][:, 128 * dc:128 * (dc + 1)],
                sb["c_identbf"][:]) for dc in range(ND)][-1],
                after=[a1, U["ptp"], cevs[0], U["xtp"]], name=f"xn_tp{t}")
            ev = p.op("vector", lambda v, j=j: v.tensor_copy(
                sb["xtp"][:, :, 128 * j:128 * (j + 1)], psum["ptp"][:, :, :]),
                after=[tp], name=f"xn_ev{t}")
            U["xn_tp"] = tp
            U["ptp"] = ev
        return ev

    last_kv_ev = None
    for piece in range(8):
        t0 = 2 * piece
        xe = build_xn([t0, t0 + 1])
        for ec in range(16):
            r = ec % 4
            ld = p.dma(f"kr{r}", lambda e, ec=ec, r=r: e.dma_start(
                out=sb["wring"][:, r, :, :],
                in_=wkd.rearrange("(dc p) (ec j) -> p dc ec j", p=128, j=128)[:, :, ec, :]),
                after=[U["ring"][r], cast_done["wk"]], name=f"krl_{piece}_{ec}")
            bk = "psA" if ec % 2 == 0 else "psB"
            off = 512 * ((ec // 2) % 2)
            kslot = "kv%d" % (ec % 4)

            def kmm(te, r=r, bk=bk, off=off):
                for dc in range(ND):
                    mm = te.matmul(psum[bk][:, off:off + 256], sb["wring"][:, r, dc, :],
                                   sb["xtp"][:, dc, :], start=(dc == 0), stop=(dc == ND - 1))
                return mm
            mm = p.op("tensor", kmm, after=[ld, xe, U.get(kslot)], name=f"kmm{piece}_{ec}")
            U["ring"][r] = mm
            U["xtp"] = mm
            ev = p.op("vector", lambda v, ec=ec, piece=piece, bk=bk, off=off: v.tensor_copy(
                sb["k12T"][:, ec, 256 * piece:256 * (piece + 1)], psum[bk][:, off:off + 256]),
                after=[mm], name=f"kev{piece}_{ec}")
            U[kslot] = ev
            U[bk] = ev
            last_kv_ev = ev
        for j, t in enumerate((t0, t0 + 1)):
            for fp in range(2):
                pqs = "pq0" if fp == 0 else "pq1"

                def vmm(te, j=j, fp=fp):
                    for dc in range(ND):
                        mm = te.matmul(psum["pq"][:, fp, :],
                                       sb["xtp"][:, dc, 128 * j:128 * (j + 1)],
                                       sb["wv_bf"][:, dc, 512 * fp:512 * (fp + 1)],
                                       start=(dc == 0), stop=(dc == ND - 1))
                    return mm
                mm = p.op("tensor", vmm, after=[xe, cast_done["wv"], U[pqs]], name=f"vmm{t}_{fp}")
                U["xtp"] = mm
                ev = p.op("vector", lambda v, t=t, fp=fp: v.tensor_copy(
                    sb["v_sb"][:, t, 512 * fp:512 * (fp + 1)], psum["pq"][:, fp, :]),
                    after=[mm], name=f"vev{t}_{fp}")
                U[pqs] = ev
                last_kv_ev = ev

    y_stores = []
    prev_ep = [None, None]
    for B in range(8):
        xe = build_xn([2 * B, 2 * B + 1])
        q_last = None
        for ec in range(16):
            r = ec % 4
            ld = p.dma(f"kr{r}", lambda e, ec=ec, r=r: e.dma_start(
                out=sb["wring"][:, r, :, :],
                in_=wqd.rearrange("(dc p) (ec j) -> p dc ec j", p=128, j=128)[:, :, ec, :]),
                after=[U["ring"][r], cast_done["wq"]], name=f"qrl_{B}_{ec}")
            pqs = "pq0" if ec % 2 == 0 else "pq1"

            def qmm(te, r=r, fp=0 if True else 0, pqs=pqs):
                fp = 0 if pqs == "pq0" else 1
                for dc in range(ND):
                    mm = te.matmul(psum["pq"][:, fp, 0:256], sb["wring"][:, r, dc, :],
                                   sb["xtp"][:, dc, :], start=(dc == 0), stop=(dc == ND - 1))
                return mm
            mm = p.op("tensor", qmm, after=[ld, xe, U[pqs], last_kv_ev], name=f"qmm{B}_{ec}")
            U["ring"][r] = mm
            U["xtp"] = mm
            ev = p.op("vector", lambda v, ec=ec, pqs=pqs: v.tensor_copy(
                sb["q12T"][:, ec, :], psum["pq"][0:128, 0 if pqs == "pq0" else 1, 0:256]),
                after=[mm, U["q12T"]], name=f"qev{B}_{ec}")
            U[pqs] = ev
            q_last = ev
        dT_evs = []
        for tt in range(2):
            qsl = 128 * tt
            gt = 2 * B + tt
            s_ops = {}
            for which, ecbase in ((1, 0), (2, 8)):
                for half, ps in ((0, "psA"), (1, "psB")):
                    def smm2(te, qsl=qsl, ecbase=ecbase, half=half, ps=ps):
                        for pc in range(2):
                            for i, ec in enumerate(range(ecbase, ecbase + 8)):
                                mm = te.matmul(
                                    psum[ps][:, 512 * pc:512 * (pc + 1)],
                                    sb["q12T"][:, ec, qsl:qsl + 128],
                                    sb["k12T"][:, ec, 1024 * half + 512 * pc:1024 * half + 512 * (pc + 1)],
                                    start=(i == 0), stop=(i == 7))
                        return mm
                    dep = [q_last, U[ps]] if which == 1 else [s_ops[(1, half, "exp")], U[ps]]
                    mm = p.op("tensor", smm2, after=dep, name=f"smm{B}_{tt}_{which}_{half}")
                    st_off = (0 if which == 1 else 24) + 4 * half
                    rd = p.op("vector", lambda v, ps=ps, st_off=st_off: v.tensor_reduce(
                        sb["stt"][:, st_off:st_off + 1], psum[ps][:, :], AX.X, OP.max,
                        negate=True), after=[mm], name=f"smax{B}_{tt}_{which}_{half}", small=True)
                    s_ops[(which, half, "mm")] = mm
                    s_ops[(which, half, "rd")] = rd
                base = 0 if which == 1 else 24
                cb = p.op("vector", lambda g, base=base: g.tensor_tensor(
                    sb["stt"][:, base + 8:base + 9], sb["stt"][:, base:base + 1],
                    sb["stt"][:, base + 4:base + 5], OP.min),
                    after=[s_ops[(which, 0, "rd")], s_ops[(which, 1, "rd")]],
                    name=f"scmb{B}_{tt}_{which}", small=True)
                sc_ = p.op("vector", lambda v, base=base: v.tensor_scalar(
                    sb["stt"][:, base + 12:base + 13], sb["stt"][:, base + 8:base + 9],
                    SCALE, None, OP.mult), after=[cb], name=f"sbias{B}_{tt}_{which}", small=True)
                ebuf = "e1b" if which == 1 else "e2h"
                for half, ps in ((0, "psA"), (1, "psB")):
                    ex = p.op("scalar", lambda s, ps=ps, half=half, ebuf=ebuf, base=base: s.activation(
                        sb[ebuf][:, 1024 * half:1024 * (half + 1)], psum[ps][:, :], AF.Exp,
                        bias=sb["stt"][:, base + 12:base + 13], scale=SCALE,
                        accum_out=sb["stt"][:, base + 16 + 4 * half:base + 17 + 4 * half]),
                        after=[sc_, U[ebuf]], name=f"sexp{B}_{tt}_{which}_{half}")
                    s_ops[(which, half, "exp")] = ex
                    U[ps] = ex
                s_ops[(which, 1, "expboth")] = s_ops[(which, 1, "exp")]
                U[ebuf] = None
            r1c = p.op("vector", lambda v: v.tensor_tensor(
                sb["stt"][:, 48:49], sb["stt"][:, 16:17], sb["stt"][:, 20:21], OP.add),
                after=[s_ops[(1, 0, "exp")], s_ops[(1, 1, "exp")]], name=f"r1{B}_{tt}", small=True)
            r2c = p.op("gpsimd", lambda g: g.tensor_tensor(
                sb["stt"][:, 52:53], sb["stt"][:, 40:41], sb["stt"][:, 44:45], OP.add),
                after=[s_ops[(2, 0, "exp")], s_ops[(2, 1, "exp")]], name=f"r2{B}_{tt}", small=True)
            rv = p.op("scalar", lambda sc: sc.activation(
                sb["stt"][:, 56:57], sb["stt"][:, 52:53], AF.Ln),
                after=[r2c], name=f"rln{B}_{tt}", small=True)
            rv2 = p.op("scalar", lambda sc: sc.activation(
                sb["stt"][:, 60:61], sb["stt"][:, 56:57], AF.Exp, scale=-1.0),
                after=[rv], name=f"rexp{B}_{tt}", small=True)
            sv1 = p.op("vector", lambda v: v.tensor_tensor(
                sb["stt"][:, 64:65], sb["stt"][:, 48:49], sb["stt"][:, 60:61], OP.mult),
                after=[r1c, rv2], name=f"sv1{B}_{tt}", small=True)
            sv2 = p.op("gpsimd", lambda g: g.tensor_tensor(
                sb["stt"][:, 68:69], sb["stt"][:, 64:65], sb["lamcol"][:, 0:1], OP.mult),
                after=[sv1, lamc], name=f"sv2{B}_{tt}", small=True)
            negs = p.op("vector", lambda v: v.tensor_scalar(
                sb["stt"][:, 72:73], sb["stt"][:, 68:69], -1.0, None, OP.mult),
                after=[sv2], name=f"negs{B}_{tt}", small=True)
            r1s = p.op("gpsimd", lambda g, tt=tt: g.tensor_copy(
                sb["r1sav"][:, 4 * tt:4 * tt + 1], sb["stt"][:, 48:49]),
                after=[r1c], name=f"r1s{B}_{tt}", small=True)
            dcmb = p.op("vector", lambda v: v.scalar_tensor_tensor(
                sb["e1b"][:, :], sb["e2h"][:, :], sb["stt"][:, 72:73], sb["e1b"][:, :],
                OP.mult, OP.add), after=[negs, s_ops[(1, 1, "exp")], s_ops[(2, 1, "exp")]],
                name=f"dcmb{B}_{tt}")
            U["e1b"] = dcmb
            U["e2h"] = dcmb
            tp_prev = dcmb
            for half in range(2):
                tp = p.op("tensor", lambda te, half=half: [te.transpose(
                    psum["ptp"][:, kk, :],
                    sb["e1b"][:, 1024 * half + 128 * kk:1024 * half + 128 * (kk + 1)],
                    sb["c_identbf"][:]) for kk in range(8)][-1],
                    after=[tp_prev, U["ptp"]], name=f"dtp{B}_{tt}_{half}")
                ev = p.op("vector", lambda v, half=half, qsl=qsl: v.tensor_copy(
                    sb["dT"][:, 8 * half:8 * (half + 1), qsl:qsl + 128], psum["ptp"][:, :, :]),
                    after=[tp, U["dT"]], name=f"dtev{B}_{tt}_{half}")
                U["ptp"] = ev
                tp_prev = ev
                dT_evs.append(ev)
            U["e1b"] = dT_evs[-1]
            U["e2h"] = dcmb
            U["q12T"] = s_ops[(2, 1, "mm")]
        U["dT"] = None
        pv_evs = []
        for dc in range(ND):
            pqs = "pq0" if dc % 2 == 0 else "pq1"
            fp = 0 if dc % 2 == 0 else 1

            def pvmm(te, dc=dc, fp=fp):
                for kt in range(16):
                    mm = te.matmul(psum["pq"][:, fp, 0:256],
                                   sb["v_sb"][:, kt, 128 * dc:128 * (dc + 1)],
                                   sb["dT"][:, kt, :], start=(kt == 0), stop=(kt == 15))
                return mm
            mm = p.op("tensor", pvmm, after=dT_evs + [U[pqs], last_kv_ev], name=f"pv{B}_{dc}")
            ev = p.op("vector", lambda v, dc=dc, fp=fp: v.tensor_copy(
                sb["oT"][:, dc, :], psum["pq"][:, fp, 0:256]),
                after=[mm, U["oT"]], name=f"pvev{B}_{dc}")
            U[pqs] = ev
            pv_evs.append(ev)
        U["dT"] = pv_evs[-1]
        for tt in range(2):
            gt = 2 * B + tt
            tp = p.op("tensor", lambda te, tt=tt: [te.transpose(
                psum["ptp"][:, dc, :], sb["oT"][:, dc, 128 * tt:128 * (tt + 1)],
                sb["c_identbf"][:]) for dc in range(ND)][-1],
                after=pv_evs + [U["ptp"]], name=f"otp{B}_{tt}")
            oev = p.op("vector", lambda v, tt=tt: v.tensor_copy(
                sb["otok"][:, tt, :], psum["ptp"][:, :, :].rearrange("p a b -> p (a b)")),
                after=[tp, U["otok%d" % tt]], name=f"oev{B}_{tt}")
            U["ptp"] = oev
            U["oT"] = tp
            sq = p.op("scalar", lambda s, tt=tt: s.activation(
                sb["scr_sq"][:], sb["otok"][:, tt, :], AF.Square,
                accum_out=sb["stt"][:, 76 + 16 * tt:77 + 16 * tt]),
                after=[oev], name=f"osq{B}_{tt}")
            c1 = p.op("vector", lambda v, tt=tt: v.tensor_scalar(
                sb["stt"][:, 80 + 16 * tt:81 + 16 * tt], sb["stt"][:, 76 + 16 * tt:77 + 16 * tt],
                1.0 / D, None, OP.mult), after=[sq], name=f"oc1{B}_{tt}", small=True)
            c2 = p.op("gpsimd", lambda g, tt=tt: g.tensor_tensor(
                sb["stt"][:, 84 + 16 * tt:85 + 16 * tt], sb["r1sav"][:, 4 * tt:4 * tt + 1],
                sb["r1sav"][:, 4 * tt:4 * tt + 1], OP.mult),
                after=[r1s if tt == 1 else r1s, c1], name=f"oc2{B}_{tt}", small=True)
            c3 = p.op("vector", lambda v, tt=tt: v.scalar_tensor_tensor(
                sb["stt"][:, 88 + 16 * tt:89 + 16 * tt], sb["stt"][:, 84 + 16 * tt:85 + 16 * tt],
                1e-5, sb["stt"][:, 80 + 16 * tt:81 + 16 * tt], OP.mult, OP.add),
                after=[c2], name=f"oc3{B}_{tt}", small=True)
            c4 = p.op("scalar", lambda sc, tt=tt: sc.activation(
                sb["stt"][:, 92 + 16 * tt:93 + 16 * tt], sb["stt"][:, 88 + 16 * tt:89 + 16 * tt],
                AF.Ln), after=[c3], name=f"oc4{B}_{tt}", small=True)
            c5 = p.op("scalar", lambda sc, tt=tt: sc.activation(
                sb["stt"][:, 96 + 16 * tt:97 + 16 * tt], sb["stt"][:, 92 + 16 * tt:93 + 16 * tt],
                AF.Exp, scale=-0.5), after=[c4], name=f"oc5{B}_{tt}", small=True)
            c6 = p.op("vector", lambda v, tt=tt: v.tensor_scalar(
                sb["socol"][:, 4 * tt:4 * tt + 1], sb["stt"][:, 96 + 16 * tt:97 + 16 * tt],
                0.2, None, OP.mult), after=[c5], name=f"oc6{B}_{tt}", small=True)
            sl = gt % 2
            lx = p.dma(f"xe{sl}", lambda e, gt=gt, sl=sl: e.dma_start(
                out=sb["xbuf"][:, sl, :], in_=x_ap[128 * gt:128 * (gt + 1), :]),
                after=[prev_ep[sl], U["xslot"][sl]], name=f"xe_ld{gt}")
            ep = p.op("vector", lambda v, sl=sl, tt=tt: v.scalar_tensor_tensor(
                sb["ybuf"][:, sl, :], sb["otok"][:, tt, :], sb["socol"][:, 4 * tt:4 * tt + 1],
                sb["xbuf"][:, sl, :], OP.mult, OP.add),
                after=[c6, lx, oev, prev_ep[sl]], name=f"aep{gt}")
            U["otok%d" % tt] = ep
            U["xslot"][sl] = ep
            st = p.dma(f"yst{sl}", lambda e, gt=gt, sl=sl: e.dma_start(
                out=y_ap[128 * gt:128 * (gt + 1), :], in_=sb["ybuf"][:, sl, :]),
                after=[ep], name=f"y_st{gt}")
            prev_ep[sl] = st
            y_stores.append(st)

    p.barrier("sync", y_stores, name=tag + "final")
    stack.close()
    p.seal(name=tag + "seal")

def build_attn0(debug=False):
    nc = bass.Bass("TRN2", target_bir_lowering=False)
    aps = {}
    aps["x"] = nc.dram_tensor("x", [SEQ, D], F32, kind="ExternalInput").ap()
    aps["y"] = nc.dram_tensor("y", [SEQ, D], F32, kind="ExternalOutput").ap()
    aps["wq12"] = nc.dram_tensor("wq12", [D, 2 * D], F32, kind="ExternalInput").ap()
    aps["wk12"] = nc.dram_tensor("wk12", [D, 2 * D], F32, kind="ExternalInput").ap()
    aps["wv"] = nc.dram_tensor("wv", [D, D], F32, kind="ExternalInput").ap()
    aps["lam4"] = nc.dram_tensor("lam4", [128, 4, 8], F32, kind="ExternalInput").ap()
    cs = declare_consts(nc)
    outer = contextlib.ExitStack()
    sbc = alloc_consts(nc, outer)
    p = Prog()
    cevs = load_consts(p, cs, sbc)
    emit_attn0(nc, p, sbc, cevs, aps, "a0_", debug=debug)
    p.emit(nc, outer)
    return nc



def run_attn0(xs, inputs, consts):
    nc = build_attn0()
    lam4 = np.stack([np.asarray(inputs[k], np.float32).reshape(128, 8) for k in
                     ("L0_attn_lambda_q1", "L0_attn_lambda_k1",
                      "L0_attn_lambda_q2", "L0_attn_lambda_k2")], axis=1)
    w = {"wq12": np.ascontiguousarray(np.asarray(inputs["L0_attn_w_q12"], np.float32)),
         "wk12": np.ascontiguousarray(np.asarray(inputs["L0_attn_w_k12"], np.float32)),
         "wv": np.ascontiguousarray(np.asarray(inputs["L0_attn_w_v"], np.float32)),
         "lam4": np.ascontiguousarray(lam4)}
    in_maps = []
    for c in range(NCORES):
        m = {"x": xs[c]}
        m.update(w)
        m.update(consts)
        in_maps.append(m)
    res = run_bass_kernel_spmd(nc, in_maps, core_ids=list(range(NCORES)))
    return [res.results[c]["y"] for c in range(NCORES)]



def emit_ffn1(nc, p, sbc, cevs, aps, tag, debug=False, shared=None):
    x_ap, y_ap = aps["x"], aps["y"]
    k_ap, bias_ap, a_ap, b_ap = aps["kern"], aps["bias"], aps["la"], aps["lb"]
    dbg = {}
    if debug:
        dbg["d_kq"] = nc.dram_tensor("d_kq", [128, ND, 1024], BF16, kind="ExternalOutput").ap()
        dbg["d_sc1"] = nc.dram_tensor("d_sc1", [1, 64], F32, kind="ExternalOutput").ap()
        dbg["d_xt"] = nc.dram_tensor("d_xt", [128, ND, 2050], BF16, kind="ExternalOutput").ap()
        dbg["d_g"] = nc.dram_tensor("d_g", [128, 64], F32, kind="ExternalOutput").ap()
        dbg["d_bg"] = nc.dram_tensor("d_bg", [128, 64], F32, kind="ExternalOutput").ap()

    stack = contextlib.ExitStack()
    E = stack.enter_context
    sb = dict(sbc)
    sb["xbuf"] = E(nc.sbuf_tensor(tag + "xbuf", [128, 2, 1024], F32))
    sb["ybuf"] = E(nc.sbuf_tensor(tag + "ybuf", [128, 2, 1024], F32))
    sb["scr_sq"] = E(nc.sbuf_tensor(tag + "scr_sq", [128, 1024], F32))
    sb["kscr"] = E(nc.sbuf_tensor(tag + "kscr", [128, 1024], F32))
    sb["scr_s"] = E(nc.sbuf_tensor(tag + "scr_s", [128, 256], F32))
    sb["xq_bf"] = E(nc.sbuf_tensor(tag + "xq_bf", [128, 1024], BF16))
    for n in ("st_sum", "st_ssq", "st_max", "st_min", "st_m", "st_rstd",
              "st_g", "st_bg", "st_iso"):
        sb[n] = E(nc.sbuf_tensor(tag + n, [128, 64], F32))
    sb["xt"] = E(nc.sbuf_tensor(tag + "xt", [128, ND, 2050], BF16))
    sb["kf"] = E(nc.sbuf_tensor(tag + "kf", [128, ND, 1024], F32))
    sb["kq"] = E(nc.sbuf_tensor(tag + "kq", [128, ND, 1024], BF16))
    sb["a_bf"] = E(nc.sbuf_tensor(tag + "a_bf", [128, ND, 16], BF16))
    sb["a_f"] = E(nc.sbuf_tensor(tag + "a_f", [128, ND, 16], F32))
    sb["b_bf"] = E(nc.sbuf_tensor(tag + "b_bf", [16, 1024], BF16))
    sb["b_f"] = E(nc.sbuf_tensor(tag + "b_f", [16, 1024], F32))
    sb["bias_bf"] = E(nc.sbuf_tensor(tag + "bias_bf", [1, 1024], BF16))
    sb["bias_f"] = E(nc.sbuf_tensor(tag + "bias_f", [1, 1024], F32))
    sb["xa_bf"] = E(nc.sbuf_tensor(tag + "xa_bf", [128, 16], BF16))
    sb["xaT"] = E(nc.sbuf_tensor(tag + "xaT", [16, 128], BF16))
    sb["wabs"] = E(nc.sbuf_tensor(tag + "wabs", [128, 64], F32))
    sb["sc1"] = E(nc.sbuf_tensor(tag + "sc1", [1, 64], F32))
    sb["sccol"] = E(nc.sbuf_tensor(tag + "sccol", [128, 8], F32))

    psum = {
        "out2": E(nc.psum_tensor(tag + "out2", [128, 2, 1024], F32)),
        "ptp": E(nc.psum_tensor(tag + "ptp", [128, ND, 128], BF16)),
        "pbr": E(nc.psum_tensor(tag + "pbr", [128, 512], F32)),
    }

    pbr_user = None

    lda = p.dma("lda", lambda e: e.dma_start(
        out=sb["a_f"][:], in_=a_ap.rearrange("(dc p) r -> p dc r", p=128)), name="ld_a")
    ca = p.op("vector", lambda v: v.tensor_copy(sb["a_bf"][:], sb["a_f"][:]), after=[lda], name="cast_a")
    ldb = p.dma("ldb", lambda e: e.dma_start(out=sb["b_f"][:], in_=b_ap), name="ld_b")
    cb_ = p.op("vector", lambda v: v.tensor_copy(sb["b_bf"][:], sb["b_f"][:]), after=[ldb], name="cast_b")
    ldbias = p.dma("ldbias", lambda e: e.dma_start(out=sb["bias_f"][:], in_=bias_ap), name="ld_bias")
    cbias = p.op("vector", lambda v: v.tensor_copy(sb["bias_bf"][:], sb["bias_f"][:]),
                 after=[ldbias], name="cast_bias")

    if shared is None:
        kq_dram = nc.dram_tensor(tag + "kq_scr", [128, ND * 1024], BF16).ap()
        ldk = p.dma("kld", lambda e: e.dma_start(
            out=sb["kf"][:], in_=k_ap.rearrange("(dc p) f -> p dc f", p=128)), name="ld_k")
        kst = p.op("vector", lambda v: [v.tensor_reduce(
            sb["wabs"][:, 4 * dc:4 * dc + 1], sb["kf"][:, dc, :], AX.X, OP.max,
            apply_absolute_value=True) for dc in range(ND)][-1], after=[ldk], name="kst", small=True)
        wv = sb["wabs"][:, 0:32].rearrange("p (t u) -> p t u", u=4)[:, :, 0:1]
        kt1 = p.op("vector", lambda v: v.tensor_reduce(sb["wabs"][:, 40:41], wv, AX.XY, OP.max),
                   after=[kst], name="ktot", small=True)
        kt2 = p.op("tensor", lambda te: te.transpose(
            psum["pbr"][0:1, 0:128], sb["wabs"][:, 40:41], sb["c_identf"][:]),
            after=[kt1, cevs[1], pbr_user], name="ktot_tp")
        kt3 = p.op("vector", lambda v: v.tensor_reduce(
            sb["sc1"][0:1, 0:1], psum["pbr"][0:1, 0:128], AX.X, OP.max),
            after=[kt2], name="kmax", small=True)
        pbr_user = kt3
        kt4 = p.op("scalar", lambda sc: sc.activation(sb["sc1"][0:1, 4:5], sb["sc1"][0:1, 0:1], AF.Ln),
                   after=[kt3], name="kln", small=True)
        kt5 = p.op("scalar", lambda sc: sc.activation(sb["sc1"][0:1, 8:9], sb["sc1"][0:1, 4:5],
                                                      AF.Exp, scale=-1.0), after=[kt4], name="kexp", small=True)
        kt6 = p.op("vector", lambda v: v.tensor_scalar(sb["sc1"][0:1, 8:9], sb["sc1"][0:1, 8:9],
                                                       7.0, None, OP.mult), after=[kt5], name="ks", small=True)
        kt7 = p.op("gpsimd", lambda g: g.tensor_scalar(sb["sc1"][0:1, 12:13], sb["sc1"][0:1, 0:1],
                                                       1.0 / 7.0, None, OP.mult), after=[kt3], name="km", small=True)
        bcs = []
        for j, off in enumerate([8, 12]):
            mm = p.op("tensor", lambda te, off=off: te.matmul(
                psum["pbr"][:, 0:1], sb["c_ones1f"][:, :], sb["sc1"][0:1, off:off + 1],
                start=True, stop=True), after=[kt6, kt7, pbr_user, cevs[3]], name=f"kb{j}_mm")
            ev = p.op("vector", lambda v, j=j: v.tensor_copy(
                sb["sccol"][:, 4 * j:4 * j + 1], psum["pbr"][:, 0:1]), after=[mm],
                name=f"kb{j}_ev", small=True)
            pbr_user = ev
            bcs.append(ev)
        kq_done = None
        for dc in range(ND):
            a1 = p.op("scalar", lambda sc, dc=dc: sc.activation(
                sb["kscr"][:], sb["kf"][:, dc, :], AF.Copy, scale=sb["sccol"][:, 0:1]),
                after=[bcs[0], ldk, kq_done], name=f"kq_a{dc}")
            r1 = p.op("vector", lambda v: v.tensor_scalar(
                sb["kscr"][:], sb["kscr"][:], MAGIC, MAGIC, OP.add, OP.subtract),
                after=[a1], name=f"kq_r{dc}")
            kq_done = p.op("scalar", lambda sc, dc=dc: sc.activation(
                sb["kq"][:, dc, :], sb["kscr"][:], AF.Copy, scale=sb["sccol"][:, 4:5]),
                after=[r1, bcs[1]], name=f"kq_c{dc}")

        st_kq = p.dma("kqst", lambda e: e.dma_start(
            out=kq_dram, in_=sb["kq"][:].rearrange("p a b -> p (a b)")),
            after=[kq_done], name=tag + "st_kq")
    else:
        kq_dram = shared["kq"]
        kq_done = p.dma("kqld2", lambda e: e.dma_start(
            out=sb["kq"][:].rearrange("p a b -> p (a b)"), in_=kq_dram),
            name=tag + "ld_kq")
    xt_done, xt_evs = emit_xphase(p, sb, psum, x_ap, quant=False, cident=cevs[0])

    y_stores = []
    out2_user = [None, None]
    ptp_user = xt_done
    prev_ep = [xt_done, xt_done]
    xa_user = None
    xaT_user = None
    for t in range(NT):
        bk = t % 2
        mm_a = p.op("tensor", lambda te, t=t: [te.matmul(
            psum["pbr"][:, 0:16], sb["xt"][:, dc, 1 + 128 * t:1 + 128 * (t + 1)],
            sb["a_bf"][:, dc, :], start=(dc == 0), stop=(dc == ND - 1))
            for dc in range(ND)][-1], after=[xt_done, ca, pbr_user], name=f"mma{t}")
        ev_a = p.op("vector", lambda v: v.tensor_copy(sb["xa_bf"][:], psum["pbr"][:, 0:16]),
                    after=[mm_a, xa_user], name=f"eva{t}")
        pbr_user = ev_a
        tp_a = p.op("tensor", lambda te: te.transpose(
            psum["ptp"][0:16, 0, 0:128], sb["xa_bf"][:], sb["c_identbf"][:]),
            after=[ev_a, ptp_user], name=f"tpa{t}")
        xa_user = tp_a
        ev_t = p.op("vector", lambda v: v.tensor_copy(sb["xaT"][:], psum["ptp"][0:16, 0, 0:128]),
                    after=[tp_a, xaT_user], name=f"evt{t}")
        ptp_user = ev_t

        def mm2(te, t=t, bk=bk):
            for piece in range(2):
                po = psum["out2"][:, bk, 512 * piece:512 * (piece + 1)]
                for dc in range(ND):
                    te.matmul(po, sb["xt"][:, dc, 1 + 128 * t:1 + 128 * (t + 1)],
                              sb["kq"][:, dc, 512 * piece:512 * (piece + 1)],
                              start=(dc == 0), stop=False)
                te.matmul(po, sb["c_ones1bf"][:, :], sb["bias_bf"][0:1, 512 * piece:512 * (piece + 1)],
                          start=False, stop=False)
                mm = te.matmul(po, sb["xaT"][:, :], sb["b_bf"][0:16, 512 * piece:512 * (piece + 1)],
                               start=False, stop=True)
            return mm
        m2 = p.op("tensor", mm2, after=[ev_t, kq_done, cbias, cb_, out2_user[bk]], name=f"mm2_{t}")
        xaT_user = m2
        sl = t % 2
        lx = p.dma(f"xe{sl}", lambda e, t=t, sl=sl: e.dma_start(
            out=sb["xbuf"][:, sl, :], in_=x_ap[128 * t:128 * (t + 1), :]),
            after=[prev_ep[sl]], name=f"xe_ld{t}")
        ep = p.op("vector", lambda v, sl=sl, bk=bk: v.scalar_tensor_tensor(
            sb["ybuf"][:, sl, :], psum["out2"][:, bk, :], 0.5,
            sb["xbuf"][:, sl, :], OP.mult, OP.add),
            after=[m2, lx], name=f"ep{t}")
        out2_user[bk] = ep
        st = p.dma(f"yst{sl}", lambda e, t=t, sl=sl: e.dma_start(
            out=y_ap[128 * t:128 * (t + 1), :], in_=sb["ybuf"][:, sl, :]),
            after=[ep], name=f"y_st{t}")
        prev_ep[sl] = st
        y_stores.append(st)

    if debug:
        y_stores = y_stores + [
            p.dma("dbg", lambda e: e.dma_start(out=dbg["d_kq"], in_=sb["kq"][:]), after=[kq_done], name="d_kq"),
            p.dma("dbg", lambda e: e.dma_start(out=dbg["d_sc1"], in_=sb["sc1"][:]), after=[kt6, kt7], name="d_sc1"),
            p.dma("dbg", lambda e: e.dma_start(out=dbg["d_xt"], in_=sb["xt"][:]), after=[xt_done], name="d_xt"),
            p.dma("dbg", lambda e: e.dma_start(out=dbg["d_g"], in_=sb["st_g"][:]), after=[fac], name="d_g"),
            p.dma("dbg", lambda e: e.dma_start(out=dbg["d_bg"], in_=sb["st_bg"][:]), after=[fac], name="d_bg"),
        ]
    p.barrier("sync", y_stores, name=tag + "final")
    stack.close()
    p.seal(name=tag + "seal")
    return {"kq": kq_dram}

def build_ffn1(debug=False):
    nc = bass.Bass("TRN2", target_bir_lowering=False)
    aps = {}
    aps["x"] = nc.dram_tensor("x", [SEQ, D], F32, kind="ExternalInput").ap()
    aps["y"] = nc.dram_tensor("y", [SEQ, D], F32, kind="ExternalOutput").ap()
    aps["kern"] = nc.dram_tensor("kern", [D, D], F32, kind="ExternalInput").ap()
    aps["bias"] = nc.dram_tensor("bias", [1, D], F32, kind="ExternalInput").ap()
    aps["la"] = nc.dram_tensor("la", [D, 16], F32, kind="ExternalInput").ap()
    aps["lb"] = nc.dram_tensor("lb", [16, D], F32, kind="ExternalInput").ap()
    cs = declare_consts(nc)
    outer = contextlib.ExitStack()
    sbc = alloc_consts(nc, outer)
    p = Prog()
    cevs = load_consts(p, cs, sbc)
    emit_ffn1(nc, p, sbc, cevs, aps, "f1_", debug=debug)
    p.emit(nc, outer)
    return nc



def run_ffn1(xs, inputs, consts):
    nc = build_ffn1()
    w = {"kern": np.ascontiguousarray(np.asarray(inputs["L1_ffn_kernel"], np.float32)),
         "bias": np.ascontiguousarray(np.asarray(inputs["L1_ffn_bias"], np.float32).reshape(1, D)),
         "la": np.ascontiguousarray(np.asarray(inputs["L1_ffn_a"], np.float32)),
         "lb": np.ascontiguousarray(np.asarray(inputs["L1_ffn_b"], np.float32))}
    in_maps = []
    for c in range(NCORES):
        m = {"x": xs[c]}
        m.update(w)
        m.update(consts)
        in_maps.append(m)
    res = run_bass_kernel_spmd(nc, in_maps, core_ids=list(range(NCORES)))
    return [res.results[c]["y"] for c in range(NCORES)]



def emit_attn1(nc, p, sbc, cevs, aps, tag, debug=False):
    x_ap, y_ap = aps["x"], aps["y"]
    wq_ap, wk_ap, wv_ap, wo_ap = aps["Wq"], aps["Wk"], aps["Wv"], aps["Wo"]
    bq_ap, bk_ap, bv_ap, bo_ap = aps["bq"], aps["bk"], aps["bv"], aps["bo"]
    qnw_ap, knw_ap = aps["qnw_t"], aps["knw_t"]

    H = 16
    ISQ = 0.125
    stack = contextlib.ExitStack()
    E = stack.enter_context
    sb = dict(sbc)
    sb["xbuf"] = E(nc.sbuf_tensor(tag + "xbuf", [128, 2, 1024], F32))
    sb["scr_sq"] = E(nc.sbuf_tensor(tag + "scr_sq", [128, 1040], F32))
    sb["scr_s"] = E(nc.sbuf_tensor(tag + "scr_s", [128, 256], F32))
    sb["xq_bf"] = E(nc.sbuf_tensor(tag + "xq_bf", [128, 1024], BF16))
    for n in ("st_sum", "st_ssq", "st_max", "st_min", "st_m", "st_rstd",
              "st_g", "st_bg", "st_iso"):
        sb[n] = E(nc.sbuf_tensor(tag + n, [128, 64], F32))
    sb["xtp"] = E(nc.sbuf_tensor(tag + "xtp", [128, ND, 128], BF16))
    sb["wcur"] = E(nc.sbuf_tensor(tag + "wcur", [128, ND, 1040], BF16))
    sb["wo_bf"] = E(nc.sbuf_tensor(tag + "wo_bf", [128, ND, 1024], BF16))
    sb["wcast"] = E(nc.sbuf_tensor(tag + "wcast", [128, 1040], F32))
    sb["brow"] = E(nc.sbuf_tensor(tag + "brow", [1, 1, 1040], BF16))
    sb["brow_f"] = E(nc.sbuf_tensor(tag + "brow_f", [1, 1040], F32))
    sb["qnrep"] = E(nc.sbuf_tensor(tag + "qnrep", [128, 1024], BF16))
    sb["knrep"] = E(nc.sbuf_tensor(tag + "knrep", [128, 1024], BF16))
    sb["nwrow"] = E(nc.sbuf_tensor(tag + "nwrow", [1, 1024], F32))
    sb["qnT"] = E(nc.sbuf_tensor(tag + "qnT", [128, ND, 2048], BF16))
    sb["knT"] = E(nc.sbuf_tensor(tag + "knT", [128, ND, 2048], BF16))
    sb["vaug"] = E(nc.sbuf_tensor(tag + "vaug", [128, 16, 16, 65], BF16))
    sb["gsig"] = E(nc.sbuf_tensor(tag + "gsig", [128, 16, 16], BF16))
    sb["eT0"] = E(nc.sbuf_tensor(tag + "eT0", [128, 16, 256], BF16))
    sb["eT1"] = E(nc.sbuf_tensor(tag + "eT1", [128, 16, 256], BF16))
    sb["negmT"] = E(nc.sbuf_tensor(tag + "negmT", [128, 2048], BF16))
    sb["mqt"] = E(nc.sbuf_tensor(tag + "mqt", [128, 136], F32))
    sb["otokS"] = E(nc.sbuf_tensor(tag + "otokS", [128, 2, 16, 65], BF16))
    sb["ob65"] = E(nc.sbuf_tensor(tag + "ob65", [65, 256], BF16))
    sb["ofT"] = E(nc.sbuf_tensor(tag + "ofT", [128, ND, 256], BF16))
    sb["qn_s"] = E(nc.sbuf_tensor(tag + "qn_s", [128, 1024], BF16))
    sb["stt"] = E(nc.sbuf_tensor(tag + "stt", [128, 128], F32))
    sb["fcol"] = E(nc.sbuf_tensor(tag + "fcol", [128, 2, 16], F32))

    psum = {
        "psA": E(nc.psum_tensor(tag + "psA", [128, 1024], F32)),
        "psB": E(nc.psum_tensor(tag + "psB", [128, 1024], F32)),
        "psC": E(nc.psum_tensor(tag + "psC", [128, 1024], F32)),
        "ptp": E(nc.psum_tensor(tag + "ptp", [128, ND, 128], BF16)),
        "pbr": E(nc.psum_tensor(tag + "pbr", [128, 512], F32)),
    }

    U = {"pbr": None, "ptp": None, "psA": None, "psB": None, "xtp": None,
         "xn_tp": None, "xslot": [None, None], "eT": None, "qnT": None,
         "knT": None, "vaug": None, "negmT": None, "otokS": None, "ofT": None,
         "mqt": None, "wcur": None, "qn_s": None, "ob65": None,
         "otokS_last": None, "psB0": None, "psB1": None,
         "psA0": None, "psA1": None, "psC0": None, "psC1": None,
         "eT0": None, "eT1": None}

    fac = emit_stats_factors(p, sb, x_ap)
    U["xslot"] = [fac, fac]

    bevs = {}
    prev_b = [None]
    _baps = {0: (bq_ap, 1040), 1: (bk_ap, 1024), 2: (bv_ap, 1024), 3: (bo_ap, 1024)}

    def load_bias(j, extra=()):
        bap, cols = _baps[j]
        ld = p.dma("brl", lambda e, bap=bap, cols=cols: e.dma_start(
            out=sb["brow_f"][:, 0:cols], in_=bap),
            after=[prev_b[0]] + list(extra), name=f"br_ld{j}")
        cvt = p.op("vector", lambda v, cols=cols: v.tensor_copy(
            sb["brow"][:, 0, 0:cols], sb["brow_f"][:, 0:cols]), after=[ld],
            name=f"br_c{j}", small=True)
        bevs[j] = cvt
        prev_b[0] = cvt
        return cvt
    nwevs = {}
    for j, nap in enumerate([qnw_ap, knw_ap]):
        dest = sb["qnrep"] if j == 0 else sb["knrep"]
        ld = p.dma("brl", lambda e, nap=nap: e.dma_start(
            out=sb["nwrow"][:], in_=nap), after=[prev_b, nwevs.get(j - 1)], name=f"nw_ld{j}")
        last = None
        for pc in range(2):
            mm = p.op("tensor", lambda te, pc=pc: te.matmul(
                psum["pbr"][:, 0:512], sb["c_ones1f"][:, :],
                sb["nwrow"][0:1, 512 * pc:512 * (pc + 1)], start=True, stop=True),
                after=[ld, U["pbr"], cevs[3]], name=f"nw_mm{j}_{pc}")
            ev = p.op("vector", lambda v, pc=pc, dest=dest: v.tensor_copy(
                dest[:, 512 * pc:512 * (pc + 1)], psum["pbr"][:, 0:512]),
                after=[mm], name=f"nw_ev{j}_{pc}")
            U["pbr"] = ev
            last = ev
        nwevs[j] = last
    wo_done = None
    for i in range(ND):
        ld = p.dma("wcl", lambda e, i=i: e.dma_start(
            out=sb["wcast"][:, 0:1024], in_=wo_ap[128 * i:128 * (i + 1), :]),
            after=[wo_done], name=f"wo_ld{i}")
        wo_done = p.op("vector", lambda v, i=i: v.tensor_copy(
            sb["wo_bf"][:, i, :], sb["wcast"][:, 0:1024]), after=[ld], name=f"wo_c{i}")


    def build_xn_tile(t):
        sl = t % 2
        ld = p.dma(f"xn{sl}", lambda e, t=t, sl=sl: e.dma_start(
            out=sb["xbuf"][:, sl, :], in_=x_ap[128 * t:128 * (t + 1), :]),
            after=[U["xslot"][sl], fac], name=f"xn_ld{t}")
        a1 = p.op("scalar", lambda s, t=t, sl=sl: s.activation(
            sb["xq_bf"][:], sb["xbuf"][:, sl, :], AF.Identity,
            bias=sb["st_bg"][:, 4 * t:4 * t + 1], scale=sb["st_g"][:, 4 * t:4 * t + 1]),
            after=[ld, U["xn_tp"]], name=f"xn_a{t}")
        U["xslot"][sl] = a1
        tp = p.op("tensor", lambda te: [te.transpose(
            psum["ptp"][:, dc, :], sb["xq_bf"][:, 128 * dc:128 * (dc + 1)],
            sb["c_identbf"][:]) for dc in range(ND)][-1],
            after=[a1, U["ptp"], cevs[0], U["xtp"]], name=f"xn_tpq{t}")
        ev = p.op("vector", lambda v: v.tensor_copy(sb["xtp"][:, :, :], psum["ptp"][:, :, :]),
                  after=[tp], name=f"xn_evq{t}")
        U["xn_tp"] = tp
        U["ptp"] = ev
        return ev

    vz = p.op("vector", lambda v: v.memset(sb["vaug"][:, :, :, 64:65], 1.0), name="v_ones")

    def load_w(name, wap, cols, prev):
        last = None
        for i in range(ND):
            ld = p.dma("wcl", lambda e, i=i, wap=wap, cols=cols: e.dma_start(
                out=sb["wcast"][:, 0:cols], in_=wap[128 * i:128 * (i + 1), :]),
                after=[prev], name=f"wl_{name}{i}")
            cvt = p.op("vector", lambda v, i=i, cols=cols: v.tensor_copy(
                sb["wcur"][:, i, 0:cols], sb["wcast"][:, 0:cols]),
                after=[ld, U["wcur"]], name=f"wc_{name}{i}")
            prev = cvt
            last = cvt
        return last

    qkv_last = None
    wprev = wo_done
    rn_prev = None
    for phase, (wap, cols, bj, nrep, dstT) in enumerate(
            [(wq_ap, 1040, 0, "qnrep", "qnT"), (wk_ap, 1024, 1, "knrep", "knT"),
             (wv_ap, 1024, 2, None, None)]):
        wld = load_w("qkv%d" % phase, wap, cols, wprev)
        wprev = wld
        load_bias(bj, extra=[qkv_last])
        last_mm = None
        for t in range(NT):
            xe = build_xn_tile(t)
            npieces = [(0, 512), (512, 512)] + ([(1024, 16)] if cols == 1040 else [])

            def qmm(te, t=t, npieces=npieces, bj=bj):
                for off, n in npieces:
                    tgt = psum["psA"][:, off:off + n] if off < 1024 else psum["pbr"][:, 0:16]
                    for dc in range(ND):
                        te.matmul(tgt, sb["xtp"][:, dc, :],
                                  sb["wcur"][:, dc, off:off + n], start=(dc == 0), stop=False)
                    mm = te.matmul(tgt, sb["c_ones1bf"][:, :], sb["brow"][0:1, 0, off:off + n],
                                   start=False, stop=True)
                return mm
            mm = p.op("tensor", qmm,
                      after=[xe, wld, bevs[bj], cevs[2], U["psA"]] +
                            ([U["pbr"]] if cols == 1040 else []),
                      name=f"qkv{phase}_mm{t}")
            U["xtp"] = mm
            last_mm = mm
            if phase == 2:
                ev = p.op("vector", lambda v, t=t: v.tensor_copy(
                    sb["vaug"][:, t, :, 0:64],
                    psum["psA"][:, 0:1024].rearrange("p (h i) -> p h i", i=64)),
                    after=[mm, vz, U["vaug"]], name=f"vev{t}")
                U["psA"] = ev
                U["vaug"] = None
                qkv_last = ev
                continue
            if phase == 0:
                gs = p.op("scalar", lambda s, t=t: s.activation(
                    sb["gsig"][:, t, :], psum["pbr"][:, 0:16], AF.Sigmoid),
                    after=[mm], name=f"gsig{t}")
                U["pbr"] = gs
            sq = p.op("scalar", lambda s: s.activation(
                sb["scr_sq"][:, 0:1024], psum["psA"][:, 0:1024], AF.Square),
                after=[mm, rn_prev], name=f"rn_sq{phase}_{t}")
            sg = p.op("vector", lambda v: v.tensor_reduce(
                sb["stt"][:, 0:16], sb["scr_sq"][:, 0:1024].rearrange("p (h i) -> p h i", i=64),
                AX.X, OP.add), after=[sq, rn_prev], name=f"rn_red{phase}_{t}", small=True)
            s1 = p.op("gpsimd", lambda g: g.tensor_scalar(
                sb["stt"][:, 16:32], sb["stt"][:, 0:16], 1.0 / 64.0, 1e-6, OP.mult, OP.add),
                after=[sg], name=f"rn_m{phase}_{t}", small=True)
            s2 = p.op("scalar", lambda sc: sc.activation(
                sb["stt"][:, 32:48], sb["stt"][:, 16:32], AF.Ln),
                after=[s1], name=f"rn_l{phase}_{t}", small=True)
            s3 = p.op("scalar", lambda sc: sc.activation(
                sb["stt"][:, 48:64], sb["stt"][:, 32:48], AF.Exp, scale=-0.5),
                after=[s2], name=f"rn_e{phase}_{t}", small=True)

            def rnmul(v):
                for h in range(H):
                    r = v.tensor_scalar(sb["qn_s"][:, 64 * h:64 * (h + 1)],
                                        psum["psA"][:, 64 * h:64 * (h + 1)],
                                        sb["stt"][:, 48 + h:49 + h], None, OP.mult)
                return r
            s4 = p.op("vector", rnmul, after=[s3, U["qn_s"]], name=f"rn_x{phase}_{t}")
            U["psA"] = s4
            s5 = p.op("vector", lambda v, nrep=nrep: v.tensor_tensor(
                sb["xq_bf"][:, 0:1024], sb["qn_s"][:], sb[nrep][:], OP.mult),
                after=[s4, nwevs[0 if nrep == "qnrep" else 1], U["xn_tp"]],
                name=f"rn_y{phase}_{t}")
            U["qn_s"] = s5
            rn_prev = s4
            tp = p.op("tensor", lambda te: [te.transpose(
                psum["ptp"][:, dc, :], sb["xq_bf"][:, 128 * dc:128 * (dc + 1)],
                sb["c_identbf"][:]) for dc in range(ND)][-1],
                after=[s5, U["ptp"]], name=f"rn_tp{phase}_{t}")
            U["xn_tp"] = tp
            ev = p.op("vector", lambda v, t=t, dstT=dstT: v.tensor_copy(
                sb[dstT][:, :, 128 * t:128 * (t + 1)], psum["ptp"][:, :, :]),
                after=[tp, U[dstT]], name=f"rn_ev{phase}_{t}")
            U["ptp"] = ev
            qkv_last = ev
        U["wcur"] = last_mm
    load_bias(3, extra=[qkv_last])
    qkv_bar = qkv_last

    U["psA0"] = U["psA"]
    U["psA1"] = U["psA"]
    U["psB0"] = U["psB"]
    U["psB1"] = U["psB"]
    stat_state = {"last": None}

    def emit_stats_qt(qt):
        stat_last = stat_state["last"]
        for h in range(H):
            rb = 64 * (h % 2)
            ch = h // 2
            for half, ps in ((0, "psA"), (1, "psB")):
                def smm2(te, rb=rb, ch=ch, qt=qt, half=half, ps=ps):
                    for pc in range(2):
                        mm = te.matmul(psum[ps][:, 512 * pc:512 * (pc + 1)],
                                       sb["qnT"][rb:rb + 64, ch, 128 * qt:128 * (qt + 1)],
                                       sb["knT"][rb:rb + 64, ch,
                                                 1024 * half + 512 * pc:1024 * half + 512 * (pc + 1)],
                                       start=True, stop=True)
                    return mm
                mm = p.op("tensor", smm2, after=[qkv_bar, U[ps + "0"], U[ps + "1"]],
                          name=f"st_mm{qt}_{h}_{half}")
                rd = p.op("vector", lambda v, ps=ps, half=half, h=h: v.tensor_reduce(
                    sb["mqt"][:, 8 * h + 4 * half:8 * h + 4 * half + 1], psum[ps][:, :],
                    AX.X, OP.max, negate=True),
                    after=[mm, U["mqt"]], name=f"st_rd{qt}_{h}_{half}", small=True)
                U[ps + "0"] = rd
                U[ps + "1"] = rd
                stat_last = rd
        mva = sb["mqt"][:, 0:128].rearrange("p (t u) -> p t u", u=8)[:, :, 0:1]
        mvb = sb["mqt"][:, 4:132].rearrange("p (t u) -> p t u", u=8)[:, :, 0:1]
        cmb = p.op("vector", lambda v, mva=mva, mvb=mvb: v.tensor_tensor(
            mva, mva, mvb, OP.min),
            after=[U["psA0"], U["psB0"]], name=f"st_cmb{qt}", small=True)
        tqv = sb["mqt"][:, 0:128].rearrange("p (t u) -> p t u", u=8)[:, :, 0:1].rearrange(
            "p t u -> p (t u)")
        tp = p.op("tensor", lambda te, tqv=tqv: te.transpose(
            psum["pbr"][0:16, 0:128], tqv, sb["c_identf"][:]),
            after=[cmb, U["pbr"], cevs[1]], name=f"st_tp{qt}")
        ev1 = p.op("vector", lambda v, qt=qt: v.tensor_copy(
            sb["negmT"][0:16, 128 * qt:128 * (qt + 1)], psum["pbr"][0:16, 0:128]),
            after=[tp, U["negmT"]], name=f"st_ev{qt}")
        ev2 = p.op("vector", lambda v, qt=qt: v.tensor_copy(
            sb["negmT"][64:80, 128 * qt:128 * (qt + 1)], psum["pbr"][0:16, 0:128]),
            after=[ev1], name=f"st_ev2{qt}")
        U["pbr"] = ev2
        U["mqt"] = tp
        stat_state["last"] = ev2
        return ev2

    U["psC0"] = None
    U["psC1"] = None
    y_stores = []
    prev_ep = [None, None]
    for sp in range(8):
        q0 = 256 * sp
        emit_stats_qt(2 * sp)
        negm_bar = emit_stats_qt(2 * sp + 1)
        for hp in range(H // 2):
            ch = hp
            exp_evs = {0: [], 1: []}
            for kt in range(NT):
                rot = kt % 2
                pst = "psB" if rot == 0 else "psC"
                for half, rb in ((0, 0), (1, 64)):
                    h = 2 * hp + half

                    def stmm(te, rb=rb, ch=ch, kt=kt, q0=q0, pst=pst, h=h, half=half):
                        te.matmul(psum[pst][:, 512 * half:512 * half + 256],
                                  sb["knT"][rb:rb + 64, ch, 128 * kt:128 * (kt + 1)],
                                  sb["qnT"][rb:rb + 64, ch, q0:q0 + 256],
                                  start=True, stop=False)
                        return te.matmul(psum[pst][:, 512 * half:512 * half + 256],
                                         sb["c_sels"][rb:rb + 16, h, :],
                                         sb["negmT"][rb:rb + 16, q0:q0 + 256],
                                         start=False, stop=True)
                    mm = p.op("tensor", stmm, after=[negm_bar, U[pst + str(half)], cevs[4]],
                              name=f"sT{sp}_{h}_{kt}")
                    ex = p.op("scalar", lambda s, kt=kt, pst=pst, half=half, hb=half: s.activation(
                        sb["eT%d" % hb][:, kt, :], psum[pst][:, 512 * half:512 * half + 256],
                        AF.Exp, scale=ISQ), after=[mm, U["eT%d" % half]],
                        name=f"sTe{sp}_{h}_{kt}")
                    U[pst + str(half)] = ex
                    exp_evs[half].append(ex)
            for half in (0, 1):
                h = 2 * hp + half
                pva = half

                def pvmm(te, h=h, pva=pva, half=half):
                    for kt in range(NT):
                        mm = te.matmul(psum["psA"][0:65, 512 * pva:512 * pva + 256],
                                       sb["vaug"][:, kt, h, :],
                                       sb["eT%d" % half][:, kt, :], start=(kt == 0), stop=(kt == 15))
                    return mm
                pv = p.op("tensor", pvmm, after=exp_evs[half] + [U["psA%d" % pva], qkv_bar],
                          name=f"pv{sp}_{h}")
                U["eT%d" % half] = pv
                ob = p.op("vector", lambda v, pva=pva: v.tensor_copy(
                    sb["ob65"][:, :], psum["psA"][0:65, 512 * pva:512 * pva + 256]),
                    after=[pv, U["ob65"]], name=f"obev{sp}_{h}")
                U["psA%d" % pva] = ob
                for tt2 in range(2):
                    tp = p.op("tensor", lambda te, tt2=tt2: te.transpose(
                        psum["ptp"][:, 0, 0:65],
                        sb["ob65"][:, 128 * tt2:128 * (tt2 + 1)], sb["c_identbf"][0:65, 0:65]),
                        after=[ob, U["ptp"]], name=f"otp{sp}_{h}_{tt2}")
                    ev = p.op("vector", lambda v, tt2=tt2, h=h: v.tensor_copy(
                        sb["otokS"][:, tt2, h, :], psum["ptp"][:, 0, 0:65]),
                        after=[tp, U["otokS"]], name=f"otev{sp}_{h}_{tt2}")
                    U["ptp"] = ev
                    U["ob65"] = tp
                    U["otokS_last"] = ev
        ot_bar = U["otokS_last"]
        of_evs = []
        for tt2 in range(2):
            gt = 2 * sp + tt2
            rsv = sb["otokS"][:, tt2, :, 64:65]
            rc = p.op("vector", lambda v, tt2=tt2, rsv=rsv: v.reciprocal(
                sb["fcol"][:, tt2, :], rsv), after=[ot_bar], name=f"frec{sp}_{tt2}", small=True)
            fm = p.op("gpsimd", lambda g, tt2=tt2, gt=gt: g.tensor_tensor(
                sb["fcol"][:, tt2, :], sb["fcol"][:, tt2, :], sb["gsig"][:, gt, :], OP.mult),
                after=[rc, qkv_bar], name=f"fmul{sp}_{tt2}", small=True)

            def fscale(v, tt2=tt2):
                for h in range(H):
                    r = v.tensor_scalar(sb["xq_bf"][:, 64 * h:64 * (h + 1)],
                                        sb["otokS"][:, tt2, h, 0:64],
                                        sb["fcol"][:, tt2, h:h + 1], None, OP.mult)
                return r
            fs = p.op("vector", fscale, after=[fm, ot_bar, U.get("xqo")], name=f"fsc{sp}_{tt2}")
            tpf = p.op("tensor", lambda te: [te.transpose(
                psum["ptp"][:, dc, :],
                sb["xq_bf"][:, 128 * dc:128 * (dc + 1)],
                sb["c_identbf"][:]) for dc in range(ND)][-1],
                after=[fs, U["ptp"]], name=f"ftp{sp}_{tt2}")
            U["xqo"] = tpf
            fe = p.op("vector", lambda v, tt2=tt2: v.tensor_copy(
                sb["ofT"][:, :, 128 * tt2:128 * (tt2 + 1)], psum["ptp"][:, :, :]),
                after=[tpf, U["ofT"]], name=f"fev{sp}_{tt2}")
            U["ptp"] = fe
            U["otokS"] = fs
            of_evs.append(fe)
        for tt2 in range(2):
            gt = 2 * sp + tt2

            def omm(te, tt2=tt2):
                for pc in range(2):
                    for dc in range(ND):
                        te.matmul(psum["psB"][:, 512 * pc:512 * (pc + 1)],
                                  sb["ofT"][:, dc, 128 * tt2:128 * (tt2 + 1)],
                                  sb["wo_bf"][:, dc, 512 * pc:512 * (pc + 1)],
                                  start=(dc == 0), stop=False)
                    mm = te.matmul(psum["psB"][:, 512 * pc:512 * (pc + 1)],
                                   sb["c_ones1bf"][:, :],
                                   sb["brow"][0:1, 0, 512 * pc:512 * (pc + 1)],
                                   start=False, stop=True)
                return mm
            m2 = p.op("tensor", omm, after=of_evs + [wo_done, bevs[3], U["psB0"], U["psB1"], U["psB"]], name=f"omm{gt}")
            U["ofT"] = m2
            sl = gt % 2
            lx = p.dma(f"xe{sl}", lambda e, gt=gt, sl=sl: e.dma_start(
                out=sb["xbuf"][:, sl, :], in_=x_ap[128 * gt:128 * (gt + 1), :]),
                after=[prev_ep[0], prev_ep[1], U["xslot"][sl]], name=f"xe_ld{gt}")
            ep = p.op("vector", lambda v, sl=sl: v.scalar_tensor_tensor(
                sb["xbuf"][:, 1 - sl, :], psum["psB"][:, 0:1024], 1.0,
                sb["xbuf"][:, sl, :], OP.mult, OP.add),
                after=[m2, lx, prev_ep[0], prev_ep[1], U["xslot"][1 - sl]], name=f"aep{gt}")
            U["psB"] = ep
            U["psB0"] = ep
            U["psB1"] = ep
            U["psC0"] = ep
            U["psC1"] = ep
            U["psA"] = ep
            U["xslot"][sl] = ep
            U["xslot"][1 - sl] = ep
            st = p.dma(f"yst{sl}", lambda e, gt=gt, sl=sl: e.dma_start(
                out=y_ap[128 * gt:128 * (gt + 1), :], in_=sb["xbuf"][:, 1 - sl, :]),
                after=[ep], name=f"y_st{gt}")
            prev_ep[sl] = st
            y_stores.append(st)

    p.barrier("sync", y_stores, name=tag + "final")
    stack.close()
    p.seal(name=tag + "seal")

def build_attn1(debug=False):
    nc = bass.Bass("TRN2", target_bir_lowering=False)
    aps = {}
    aps["x"] = nc.dram_tensor("x", [SEQ, D], F32, kind="ExternalInput").ap()
    aps["y"] = nc.dram_tensor("y", [SEQ, D], F32, kind="ExternalOutput").ap()
    aps["Wq"] = nc.dram_tensor("Wq", [D, 1040], F32, kind="ExternalInput").ap()
    aps["Wk"] = nc.dram_tensor("Wk", [D, D], F32, kind="ExternalInput").ap()
    aps["Wv"] = nc.dram_tensor("Wv", [D, D], F32, kind="ExternalInput").ap()
    aps["Wo"] = nc.dram_tensor("Wo", [D, D], F32, kind="ExternalInput").ap()
    aps["bq"] = nc.dram_tensor("bq", [1, 1040], F32, kind="ExternalInput").ap()
    aps["bk"] = nc.dram_tensor("bk", [1, D], F32, kind="ExternalInput").ap()
    aps["bv"] = nc.dram_tensor("bv", [1, D], F32, kind="ExternalInput").ap()
    aps["bo"] = nc.dram_tensor("bo", [1, D], F32, kind="ExternalInput").ap()
    aps["qnw_t"] = nc.dram_tensor("qnw_t", [1, D], F32, kind="ExternalInput").ap()
    aps["knw_t"] = nc.dram_tensor("knw_t", [1, D], F32, kind="ExternalInput").ap()
    cs = declare_consts(nc)
    outer = contextlib.ExitStack()
    sbc = alloc_consts(nc, outer)
    p = Prog()
    cevs = load_consts(p, cs, sbc)
    emit_attn1(nc, p, sbc, cevs, aps, "a1_", debug=debug)
    p.emit(nc, outer)
    return nc



def run_attn1(xs, inputs, consts):
    nc = build_attn1()
    w = {"Wq": np.ascontiguousarray(np.asarray(inputs["L1_attn_Wq"], np.float32)),
         "Wk": np.ascontiguousarray(np.asarray(inputs["L1_attn_Wk"], np.float32)),
         "Wv": np.ascontiguousarray(np.asarray(inputs["L1_attn_Wv"], np.float32)),
         "Wo": np.ascontiguousarray(np.asarray(inputs["L1_attn_Wo"], np.float32)),
         "bq": np.asarray(inputs["L1_attn_bq"], np.float32).reshape(1, 1040),
         "bk": np.asarray(inputs["L1_attn_bk"], np.float32).reshape(1, D),
         "bv": np.asarray(inputs["L1_attn_bv"], np.float32).reshape(1, D),
         "bo": np.asarray(inputs["L1_attn_bo"], np.float32).reshape(1, D),
         "qnw_t": np.ascontiguousarray(np.tile(np.asarray(inputs["L1_attn_qn_w"], np.float32), 16).reshape(1, D)),
         "knw_t": np.ascontiguousarray(np.tile(np.asarray(inputs["L1_attn_kn_w"], np.float32), 16).reshape(1, D))}
    in_maps = []
    for c in range(NCORES):
        m = {"x": xs[c]}
        m.update(w)
        m.update(consts)
        in_maps.append(m)
    res = run_bass_kernel_spmd(nc, in_maps, core_ids=list(range(NCORES)))
    return [res.results[c]["y"] for c in range(NCORES)]



def build_fused():
    nc = bass.Bass("TRN2", target_bir_lowering=False)
    F = F32
    ap = {}
    ap["x"] = nc.dram_tensor("x", [SEQ, D], F, kind="ExternalInput").ap()
    ap["y"] = nc.dram_tensor("y", [SEQ, D], F, kind="ExternalOutput").ap()
    xs0 = nc.dram_tensor("xscr0", [SEQ, D], F).ap()
    xs1 = nc.dram_tensor("xscr1", [SEQ, D], F).ap()
    ap["w1T"] = nc.dram_tensor("w1T", [D, DH], F, kind="ExternalInput").ap()
    ap["w2T"] = nc.dram_tensor("w2T", [DH, D], F, kind="ExternalInput").ap()
    ap["cw"] = nc.dram_tensor("cw", [128, 96], F, kind="ExternalInput").ap()
    ap["cb"] = nc.dram_tensor("cb", [128, 32], F, kind="ExternalInput").ap()
    ap["wq12"] = nc.dram_tensor("wq12", [D, 2 * D], F, kind="ExternalInput").ap()
    ap["wk12"] = nc.dram_tensor("wk12", [D, 2 * D], F, kind="ExternalInput").ap()
    ap["wv"] = nc.dram_tensor("wv", [D, D], F, kind="ExternalInput").ap()
    ap["lam4"] = nc.dram_tensor("lam4", [128, 4, 8], F, kind="ExternalInput").ap()
    ap["kern"] = nc.dram_tensor("kern", [D, D], F, kind="ExternalInput").ap()
    ap["bias"] = nc.dram_tensor("bias", [1, D], F, kind="ExternalInput").ap()
    ap["la"] = nc.dram_tensor("la", [D, 16], F, kind="ExternalInput").ap()
    ap["lb"] = nc.dram_tensor("lb", [16, D], F, kind="ExternalInput").ap()
    ap["Wq"] = nc.dram_tensor("Wq", [D, 1040], F, kind="ExternalInput").ap()
    ap["Wk"] = nc.dram_tensor("Wk", [D, D], F, kind="ExternalInput").ap()
    ap["Wv"] = nc.dram_tensor("Wv", [D, D], F, kind="ExternalInput").ap()
    ap["Wo"] = nc.dram_tensor("Wo", [D, D], F, kind="ExternalInput").ap()
    ap["bq"] = nc.dram_tensor("bq", [1, 1040], F, kind="ExternalInput").ap()
    ap["bk"] = nc.dram_tensor("bk", [1, D], F, kind="ExternalInput").ap()
    ap["bv"] = nc.dram_tensor("bv", [1, D], F, kind="ExternalInput").ap()
    ap["bo"] = nc.dram_tensor("bo", [1, D], F, kind="ExternalInput").ap()
    ap["qnw_t"] = nc.dram_tensor("qnw_t", [1, D], F, kind="ExternalInput").ap()
    ap["knw_t"] = nc.dram_tensor("knw_t", [1, D], F, kind="ExternalInput").ap()
    cs = declare_consts(nc)
    outer = contextlib.ExitStack()
    sbc = alloc_consts(nc, outer)
    p = Prog()
    cevs = load_consts(p, cs, sbc)
    chain = [ap["x"], xs0, xs1, xs0, xs1, xs0, ap["y"]]

    def io(i):
        d = dict(ap)
        d["x"], d["y"] = chain[i], chain[i + 1]
        return d
    sh0 = emit_ffn0(nc, p, sbc, cevs, io(0), "s1_")
    emit_attn0(nc, p, sbc, cevs, io(1), "s2_")
    emit_ffn0(nc, p, sbc, cevs, io(2), "s3_", shared=sh0)
    sh1 = emit_ffn1(nc, p, sbc, cevs, io(3), "s4_")
    emit_attn1(nc, p, sbc, cevs, io(4), "s5_")
    emit_ffn1(nc, p, sbc, cevs, io(5), "s6_", shared=sh1)
    p.emit(nc, outer)
    return nc


def _fused_inputs(inputs):
    w = _layout_ffn0_weights(inputs)
    lam4 = np.stack([np.asarray(inputs[k], np.float32).reshape(128, 8) for k in
                     ("L0_attn_lambda_q1", "L0_attn_lambda_k1",
                      "L0_attn_lambda_q2", "L0_attn_lambda_k2")], axis=1)
    w.update({
        "wq12": np.ascontiguousarray(np.asarray(inputs["L0_attn_w_q12"], np.float32)),
        "wk12": np.ascontiguousarray(np.asarray(inputs["L0_attn_w_k12"], np.float32)),
        "wv": np.ascontiguousarray(np.asarray(inputs["L0_attn_w_v"], np.float32)),
        "lam4": np.ascontiguousarray(lam4),
        "kern": np.ascontiguousarray(np.asarray(inputs["L1_ffn_kernel"], np.float32)),
        "bias": np.ascontiguousarray(np.asarray(inputs["L1_ffn_bias"], np.float32).reshape(1, D)),
        "la": np.ascontiguousarray(np.asarray(inputs["L1_ffn_a"], np.float32)),
        "lb": np.ascontiguousarray(np.asarray(inputs["L1_ffn_b"], np.float32)),
        "Wq": np.ascontiguousarray(np.asarray(inputs["L1_attn_Wq"], np.float32)),
        "Wk": np.ascontiguousarray(np.asarray(inputs["L1_attn_Wk"], np.float32)),
        "Wv": np.ascontiguousarray(np.asarray(inputs["L1_attn_Wv"], np.float32)),
        "Wo": np.ascontiguousarray(np.asarray(inputs["L1_attn_Wo"], np.float32)),
        "bq": np.asarray(inputs["L1_attn_bq"], np.float32).reshape(1, 1040),
        "bk": np.asarray(inputs["L1_attn_bk"], np.float32).reshape(1, D),
        "bv": np.asarray(inputs["L1_attn_bv"], np.float32).reshape(1, D),
        "bo": np.asarray(inputs["L1_attn_bo"], np.float32).reshape(1, D),
        "qnw_t": np.ascontiguousarray(np.tile(np.asarray(inputs["L1_attn_qn_w"], np.float32), 16).reshape(1, D)),
        "knw_t": np.ascontiguousarray(np.tile(np.asarray(inputs["L1_attn_kn_w"], np.float32), 16).reshape(1, D)),
    })
    w.update(host_consts())
    return w



def _layout_ffn0_weights(inputs):
    w1T = np.ascontiguousarray(inputs["L0_ffn_w1"].T).astype(np.float32)
    w2T = np.ascontiguousarray(inputs["L0_ffn_w2"].T).astype(np.float32)
    cw = np.ascontiguousarray(
        np.asarray(inputs["L0_ffn_conv_w"]).reshape(32, 128, 3).transpose(1, 0, 2).reshape(128, 96)
    ).astype(np.float32)
    cb = np.ascontiguousarray(np.asarray(inputs["L0_ffn_conv_b"]).reshape(32, 128).T).astype(np.float32)
    return {"w1T": w1T, "w2T": w2T, "cw": cw, "cb": cb}


def run_ffn0(xs, inputs, consts):
    nc = build_ffn0()
    w = _layout_ffn0_weights(inputs)
    in_maps = []
    for c in range(NCORES):
        m = {"x": xs[c]}
        m.update(w)
        m.update(consts)
        in_maps.append(m)
    res = run_bass_kernel_spmd(nc, in_maps, core_ids=list(range(NCORES)))
    return [res.results[c]["y"] for c in range(NCORES)]


def kernel(**inputs):
    x = np.ascontiguousarray(np.asarray(inputs["L0_attn_x"], dtype=np.float32))
    nc = build_fused()
    w = _fused_inputs(inputs)
    in_maps = []
    for c in range(NCORES):
        m = {"x": np.ascontiguousarray(x[c])}
        m.update(w)
        in_maps.append(m)
    res = run_bass_kernel_spmd(nc, in_maps, core_ids=list(range(NCORES)))
    return np.stack([res.results[c]["y"] for c in range(NCORES)]).reshape(NCORES, SEQ, D)
